# version 24
# baseline (speedup 1.0000x reference)
import math
import numpy as np
import ml_dtypes
import concourse.bass as bass
import concourse.mybir as mybir
from concourse.bass_utils import run_bass_kernel_spmd
from contextlib import ExitStack

F32 = mybir.dt.float32
BF16 = mybir.dt.bfloat16
AF = mybir.ActivationFunctionType
ALU = mybir.AluOpType
f32 = np.float32

AEFF = float(np.float32(0.99))
D = 1024
S = 16384
NCORES = 8
T_CORE = S // NCORES
NR = 4
T_R = T_CORE // NR
W = 512
HALO = 3072
NSMAC = 5
DFF = 4096
RANK = 256


def merge(*tokens):
    w = {}
    for t in tokens:
        if t is None:
            continue
        items = t.items() if isinstance(t, dict) else [t]
        for s, v in items:
            if w.get(s, -1) < v:
                w[s] = v
    return w


class Sched:
    ENGINES = ("pe", "dve", "act", "gp", "sync")
    DMA_SEMS = ("hw", "wtk", "wtv", "wtq", "miscsw", "xh0", "xh1", "xh2", "xh3",
                "xqs", "ffnw", "st2", "st3", "st4", "st5", "st6", "st7", "xh4", "wbq", "wbv", "wbu",
                "st0", "st1", "tabs")

    def __init__(self):
        self.ops = {e: [] for e in self.ENGINES}
        self.cnt = {k: 0 for k in ("pe", "dve", "act", "gp") + self.DMA_SEMS}

    def _emit(self, eng, fn, waits, inc, amt):
        self.ops[eng].append((merge(waits) if waits else {}, fn, inc, amt))
        if inc is None:
            return None
        self.cnt[inc] += amt
        return (inc, self.cnt[inc])

    def pe(self, fn, waits=None, inc=False):
        return self._emit("pe", fn, waits, "pe" if inc else None, 1)

    def dve(self, fn, waits=None):
        return self._emit("dve", fn, waits, "dve", 1)

    def act(self, fn, waits=None):
        return self._emit("act", fn, waits, "act", 1)

    def gp(self, fn, waits=None):
        return self._emit("gp", fn, waits, "gp", 1)

    def gp_dma(self, fn, sem="miscsw", waits=None, gate=False):
        w = merge(waits, {sem: self.cnt[sem]}) if gate else waits
        return self._emit("gp", fn, w, sem, 16)

    def sync_dma(self, fn, waits=None, sem="hw", gate=False):
        w = merge(waits, {sem: self.cnt[sem]}) if gate else waits
        return self._emit("sync", fn, w, sem, 16)

    def act_dma(self, fn, waits=None, sem="hw", gate=False):
        w = merge(waits, {sem: self.cnt[sem]}) if gate else waits
        return self._emit("act", fn, w, sem, 16)

    def wait_only(self, eng, waits):
        return self._emit(eng, None, waits, None, 1)

    def barrier_token(self):
        return dict(self.cnt)


def build_nc():
    nc = bass.Bass("TRN2", target_bir_lowering=False,
                   dynamic_dma_scratch_size=4096)
    sch = Sched()

    def dram(name, shape, dtype=F32, kind="ExternalInput"):
        return nc.dram_tensor(name, shape, dtype, kind=kind).ap()

    xq_h = dram("xq", [128, 16384])
    xh_h = dram("xh", [128, 24576])
    wtkv_h = dram("wtkv", [128, 16384])
    wtq_h = dram("wtq", [128, 8192])
    brow_h = dram("brow", [1, 2048])
    bqcol_h = dram("bqcol", [128, 16])
    u1_h = dram("u1", [128, 2048])
    v1_h = dram("v1", [128, 8192])
    u2_h = dram("u2", [128, 8192])
    v2_h = dram("v2", [128, 2048])
    b1c_h = dram("b1c", [128, 32])
    b2col_h = dram("b2col", [128, 8])
    master_h = dram("master", [128, 1408])
    aQhi_h = dram("aQhi", [128, 512])
    aQlo_h = dram("aQlo", [128, 512])
    achi_h = dram("achi", [128, 4 + 4 * NSMAC])
    aclo_h = dram("aclo", [128, 4 + 4 * NSMAC])
    vmask_h = dram("vmask", [128, 1])
    ident_h = dram("ident", [128, 128])
    y_h = dram("y", [T_CORE, D], kind="ExternalOutput")
    wtqb_d = nc.dram_tensor("wtqb", [128, 8192], BF16, kind="Internal").ap()
    v1b_d = nc.dram_tensor("v1b", [128, 8192], BF16, kind="Internal").ap()
    u2b_d = nc.dram_tensor("u2b", [128, 8192], BF16, kind="Internal").ap()

    ctx = ExitStack()
    E = ctx.enter_context

    def sb(name, shape, dtype, side="left"):
        return E(nc.sbuf_tensor(name, shape, dtype, side=side))

    sem = {k: E(nc.semaphore(k)) for k in ("pe", "dve", "act", "gp") + Sched.DMA_SEMS}

    pb = [E(nc.psum_tensor(f"pb{i}", [128, 512], F32)) for i in range(6)]
    ptr = [E(nc.psum_tensor(f"ptr{i}", [128, 1024], BF16)) for i in range(2)]
    bank_free = [None] * 6
    bank_open = [False] * 6
    ptr_free = [None] * 2
    _rr = [0]

    def acquire_bank():
        for _ in range(6):
            i = _rr[0]
            _rr[0] = (i + 1) % 6
            if not bank_open[i]:
                bank_open[i] = True
                return i
        raise RuntimeError("no free psum bank")

    def mm_group(tiles, waits, nf=512, mrows=128, last_waits=None):
        bi = acquire_bank()
        n = len(tiles)
        tok = None
        for i, (lh, rh) in enumerate(tiles):
            w = merge(waits, bank_free[bi]) if i == 0 else None
            if i == n - 1 and last_waits:
                w = merge(w, last_waits)
            tok = sch.pe(
                lambda t, lh=lh, rh=rh, st=(i == 0), sp=(i == n - 1), bi=bi,
                nf=nf, mr=mrows:
                t.matmul(pb[bi][0:mr, 0:nf], lh, rh, start=st, stop=sp),
                waits=w, inc=(i == n - 1))
        bank_open[bi] = False
        return bi, tok

    master_s = sb("master_s", [128, 1408], F32)
    aQrep_s = sb("aQrep_s", [128, 512], F32)
    acol_s = sb("acol_s", [128, 4 + 4 * NSMAC], F32)
    vmask_s = sb("vmask_s", [128, 1], F32)
    ident_s = sb("ident_s", [128, 128], BF16)
    ones_s = sb("ones_s", [1, 512], BF16)
    onescol_s = sb("onescol_s", [128, 1], BF16)
    brow_s = sb("brow_s", [1, 2048], BF16)
    bqcol_s = sb("bqcol_s", [128, 16], F32)
    b1c_s = sb("b1c_s", [128, 32], F32)
    b2col_s = sb("b2col_s", [128, 8], F32)
    rowA_s = sb("rowA_s", [1, 512], BF16)
    rowB_s = sb("rowB_s", [1, 512], BF16)
    rowD_s = sb("rowD_s", [1, 512], BF16)
    colsq_s = sb("colsq_s", [128, 8], F32)
    mcq_s = sb("mcq_s", [128, 4], F32)
    cscq_s = sb("cscq_s", [128, 4], F32)
    vcq_s = sb("vcq_s", [128, 4], F32)
    colsb_s = sb("colsb_s", [128, 8], BF16)
    colacc_s = sb("colacc_s", [128, 16], F32)
    cw_s = sb("cw_s", [128, 8], F32)
    cs_s = sb("cs_s", [128, 2], F32)
    mrep_s = sb("mrep_s", [128, 512], BF16)
    rrep_s = sb("rrep_s", [128, 512], BF16)
    ysq_s = [sb(f"ysq{i}", [128, 512], BF16) for i in range(3)]
    xsc_s = [sb(f"xsc{i}", [128, 512], BF16) for i in range(2)]
    mcol_s = sb("mcol_s", [128, 4], F32)
    rcol_s = sb("rcol_s", [128, 4], F32)
    nu1s_s = sb("nu1s_s", [1, 256], BF16)
    p_s = sb("p_s", [128, 512], BF16)
    q2_s = sb("q2_s", [128, 512], BF16)
    stage_s = [sb(f"stage{i}", [128, 1024], F32) for i in range(4)]
    u1_s = sb("u1_s", [128, 2048], BF16)
    v2_s = sb("v2_s", [128, 2048], BF16)

    wtkv_s = sb("wtkv_s", [128, 16384], BF16, side="right")
    xqr2_s = sb("xqr2_s", [128, 8192], BF16, side="right")
    uacc_s = sb("uacc_s", [128, 8192], BF16, side="right")
    kfm2_s = [sb(f"kfm2_{i}", [128, 4096], BF16, side="right") for i in range(2)]
    vtm2_s = [sb(f"vtm2_{i}", [128, 4096], BF16, side="right") for i in range(2)]

    t_ones = sch.gp(lambda g: g.memset(ones_s[:], 1.0))
    t_onescol = sch.gp(lambda g: g.memset(onescol_s[:], 1.0))

    def ld_sw(dst, src, waits=None, dsem="miscsw", gate=False):
        return sch.gp_dma(lambda g, d=dst, s=src: g.dma_start(out=d, in_=s),
                          sem=dsem, waits=waits, gate=gate)

    def ld_hw(dst, src, waits=None, sem="hw", gate=False):
        return sch.sync_dma(lambda s_, d=dst, s=src: s_.dma_start(out=d, in_=s),
                            waits=waits, sem=sem, gate=gate)

    def ld_hw_act(dst, src, waits=None, sem="hw", gate=False):
        return sch.act_dma(lambda a_, d=dst, s=src: a_.dma_start(out=d, in_=s),
                           waits=waits, sem=sem, gate=gate)

    NA = 4 + 4 * NSMAC
    with nc.sbuf_tensor("tabhi", [128, 536], F32) as tabhi_s, \
         nc.sbuf_tensor("tablo", [128, 536], F32) as tablo_s:
        th2 = ld_hw(tabhi_s[:, 0:NA], achi_h[:], sem="tabs")
        tl2 = ld_hw(tablo_s[:, 0:NA], aclo_h[:], sem="tabs")
        t_acol = sch.dve(lambda v: v.tensor_tensor(
            out=acol_s[:], in0=tabhi_s[:, 0:NA], in1=tablo_s[:, 0:NA],
            op=ALU.mult), waits=merge(th2, tl2))
        th = ld_hw(tabhi_s[:, NA:NA + 512], aQhi_h[:], sem="tabs", gate=True)
        tl = ld_hw(tablo_s[:, NA:NA + 512], aQlo_h[:], sem="tabs")
        t_aQ = sch.dve(lambda v: v.tensor_tensor(
            out=aQrep_s[:], in0=tabhi_s[:, NA:NA + 512],
            in1=tablo_s[:, NA:NA + 512], op=ALU.mult), waits=merge(th, tl))
    t_tab = merge(t_acol, t_aQ)
    t_hw_misc = merge(ld_hw(bqcol_s[:], bqcol_h[:]),
                      ld_hw(b1c_s[:], b1c_h[:]),
                      ld_hw(b2col_s[:], b2col_h[:]),
                      ld_hw(vmask_s[:], vmask_h[:]),
                      ld_hw(master_s[:, 0:704], master_h[:, 0:704]),
                      ld_hw(master_s[:, 704:1408], master_h[:, 704:1408]))

    acolb_s = sb("acolb_s", [128, 4 + 4 * NSMAC], BF16)
    s0_s = sb("s0_s", [1, 1], F32)
    t_acolb = sch.dve(lambda v: v.tensor_copy(acolb_s[:], acol_s[:]),
                      waits=t_acol)
    with nc.sbuf_tensor("xhtm", [128, 20480], BF16, side="right") as xhtm_s, \
         nc.sbuf_tensor("sxx", [128, 8192], BF16, side="right") as sxx_s, \
         nc.sbuf_tensor("hx5", [128, 4096], BF16, side="right") as hx5_s, \
         nc.sbuf_tensor("hrow", [1, 1024], BF16, side="right") as hrow_s, \
         nc.sbuf_tensor("hcol", [128, 8], BF16, side="right") as hcol_s, \
         nc.sbuf_tensor("wkr", [1, 1024], BF16, side="right") as wkr_s:
        xh_cuts = [0, 2048, 7168, 12288, 17408, 20480]
        t_xh = [ld_sw(xhtm_s[:, xh_cuts[i]:xh_cuts[i + 1]],
                      xh_h[:, xh_cuts[i]:xh_cuts[i + 1]], dsem=f"xh{i}")
                for i in range(5)]
        xh_piece = [0, 0] + [1] * 5 + [2] * 5 + [3] * 5 + [4] * 3
        for i in range(2):
            ld_sw(wtkv_s[:, 4096 * i:4096 * (i + 1)],
                  wtkv_h[:, 4096 * i:4096 * (i + 1)], dsem="wtk")
        t_wtk = ("wtk", sch.cnt["wtk"])
        ld_sw(brow_s[:], brow_h[:])
        _t_ident = ld_sw(ident_s[:], ident_h[:], gate=True)
        t_sw_misc1 = ("miscsw", sch.cnt["miscsw"])
        t_consts = merge(t_sw_misc1, t_hw_misc, t_ones, t_onescol, t_tab)
        for i in range(2, 4):
            ld_sw(wtkv_s[:, 4096 * i:4096 * (i + 1)],
                  wtkv_h[:, 4096 * i:4096 * (i + 1)], dsem="wtv")
        t_wtkv = ("wtv", sch.cnt["wtv"])
        t_hx5 = ld_sw(hx5_s[:], xh_h[:, 20480:24576], dsem="xh1", gate=True)
        xq_tok = {}
        for rr in range(2):
            xq_tok[rr] = ld_sw(xqr2_s[:, 4096 * rr:4096 * (rr + 1)],
                               xq_h[:, 4096 * rr:4096 * (rr + 1)],
                               dsem="xqs", gate=True)
        sc_tok = []
        for c in range(20):
            sc_tok.append(sch.act(lambda a, c=c: a.activation(
                out=xhtm_s[:, 1024 * c:1024 * (c + 1)],
                in_=xhtm_s[:, 1024 * c:1024 * (c + 1)], func=AF.Identity,
                scale=acol_s[:, 4 + c:4 + c + 1]),
                waits=merge(t_xh[xh_piece[c]], t_acol)))
        t_xs = sc_tok[-1]
        sxx_tok = {}
        all_gk = [(a, db) for a in range(8) for db in range(2)
                  if not (db == 0 and a >= 4)]
        for p in range(2):
            gks = all_gk[6 * p:6 * (p + 1)]
            bks = {g: acquire_bank() for g in gks}
            mmt = {}
            for c in range(20):
                for a, db in gks:
                    bi = bks[(a, db)]
                    w = merge(sc_tok[c],
                              bank_free[bi] if c == 0 else None)
                    mmt[(a, db)] = sch.pe(
                        lambda t, a=a, db=db, c=c, bi=bi: t.matmul(
                            pb[bi][:],
                            xhtm_s[:, 1024 * c + 128 * a:1024 * c + 128 * (a + 1)],
                            xhtm_s[:, 1024 * c + 512 * db:1024 * c + 512 * (db + 1)],
                            start=(c == 0), stop=(c == 19)),
                        waits=w, inc=(c == 19))
            for a, db in gks:
                bi = bks[(a, db)]
                bank_open[bi] = False
                dst = sxx_s[:, 1024 * a + 512 * db:1024 * a + 512 * (db + 1)]
                ev = sch.act(lambda ac, bi=bi, dst=dst: ac.activation(
                    out=dst, in_=pb[bi][:], func=AF.Identity),
                    waits=mmt[(a, db)])
                sxx_tok[(a, db)] = ev
                bank_free[bi] = ev
        for a in range(4, 8):
            pt = a % 2
            ptok = None
            for j in range(4):
                src = sxx_s[:, 1024 * j + 512 + 128 * (a - 4):
                            1024 * j + 512 + 128 * (a - 3)]
                ptok = sch.pe(lambda t, src=src, j=j, pt=pt: t.transpose(
                    ptr[pt][:, 128 * j:128 * (j + 1)], src, ident_s[:]),
                    waits=merge(sxx_tok[(j, 1)], t_consts, _t_ident,
                                ptr_free[pt] if j == 0 else None),
                    inc=(j == 3))
            ev = sch.act(lambda ac, a=a, pt=pt: ac.activation(
                out=sxx_s[:, 1024 * a:1024 * a + 512], in_=ptr[pt][:, 0:512],
                func=AF.Identity), waits=ptok)
            ptr_free[pt] = ev
            sxx_tok[(a, 0)] = ev
        sxx_ready = merge(*sxx_tok.values())
        hr_tok = []
        for db in range(2):
            tiles = [(acolb_s[:, 4 + c:4 + c + 1],
                      xhtm_s[:, 1024 * c + 512 * db:1024 * c + 512 * (db + 1)])
                     for c in range(20)]
            bi, tok = mm_group(tiles, merge(t_xs, t_acolb), mrows=1)
            ev = sch.dve(lambda v, bi=bi, db=db: v.tensor_copy(
                hrow_s[0:1, 512 * db:512 * (db + 1)], pb[bi][0:1, :]), waits=tok)
            hr_tok.append(ev)
            bank_free[bi] = ev
        tiles = [(acolb_s[:, 4 + c:4 + c + 1], acolb_s[:, 4 + c:4 + c + 1])
                 for c in range(20)]
        bi, tok = mm_group(tiles, t_acolb, nf=1, mrows=1)
        t_s0 = sch.dve(lambda v, bi=bi: v.tensor_copy(
            s0_s[0:1, 0:1], pb[bi][0:1, 0:1]), waits=tok)
        bank_free[bi] = t_s0
        t_hrow = merge(*hr_tok)
        halo_gate = merge({"pe": sch.cnt["pe"]},
                          *[(f"xh{i}", sch.cnt[f"xh{i}"]) for i in range(5)])
        hc_tok = []
        for c in range(8):
            bi, tok = mm_group([(hrow_s[0:1, 128 * c:128 * (c + 1)],
                                 ones_s[0:1, 0:1])], merge(t_hrow, t_ones), nf=1)
            ev = sch.dve(lambda v, bi=bi, c=c: v.tensor_copy(
                hcol_s[:, c:c + 1], pb[bi][0:128, 0:1]), waits=tok)
            hc_tok.append(ev)
            bank_free[bi] = ev
        t_hcol = merge(*hc_tok)
        wk_tok = []
        for db in range(2):
            tiles = [(hcol_s[:, c:c + 1],
                      wtkv_s[:, 1024 * c + 512 * db:1024 * c + 512 * (db + 1)])
                     for c in range(8)]
            bi, tok = mm_group(tiles, merge(t_hcol, t_wtk), mrows=1)
            ev = sch.dve(lambda v, bi=bi, db=db: v.tensor_copy(
                wkr_s[0:1, 512 * db:512 * (db + 1)], pb[bi][0:1, :]), waits=tok)
            wk_tok.append(ev)
            bank_free[bi] = ev
        t_wkr = sch.dve(lambda v: v.scalar_tensor_tensor(
            out=wkr_s[0:1, :], in0=brow_s[0:1, 0:1024], scalar=s0_s[0:1, 0:1],
            in1=wkr_s[0:1, :], op0=ALU.mult, op1=ALU.add),
            waits=merge(t_s0, t_sw_misc1, *wk_tok))

        init_tok = {}
        last5 = None
        for ot in range(8):
            tiles = [(wtkv_s[:, 1024 * c + 128 * ot:1024 * c + 128 * (ot + 1)],
                      hx5_s[:, 512 * c:512 * (c + 1)]) for c in range(8)]
            bi, tok = mm_group(tiles, merge(t_hx5, t_wtk))
            last5 = tok
            dst = kfm2_s[1][:, 512 * ot:512 * (ot + 1)]
            ev = sch.act(lambda a, bi=bi, dst=dst, ot=ot: a.activation(
                out=dst, in_=pb[bi][:], func=AF.Identity,
                bias=bqcol_s[:, 8 + ot:8 + ot + 1]), waits=merge(tok, t_hw_misc))
            init_tok[("k", ot)] = ev
            bank_free[bi] = ev
        for tt in range(4):
            for db in range(2):
                tiles = [(hx5_s[:, 512 * c + 128 * tt:512 * c + 128 * (tt + 1)],
                          wtkv_s[:, 8192 + 1024 * c + 512 * db:8192 + 1024 * c + 512 * (db + 1)])
                         for c in range(8)]
                tiles.append((ones_s[0:1, 0:128],
                              brow_s[0:1, 1024 + 512 * db:1024 + 512 * (db + 1)]))
                bi, tok = mm_group(tiles, merge(t_hx5, t_wtkv, t_ones, t_sw_misc1))
                last5 = tok
                dst = vtm2_s[1][:, 1024 * tt + 512 * db:1024 * tt + 512 * (db + 1)]
                ev = sch.act(lambda a, bi=bi, dst=dst: a.activation(
                    out=dst, in_=pb[bi][:], func=AF.Identity,
                    scale=vmask_s[:, 0:1]), waits=merge(tok, t_hw_misc))
                init_tok[("v", tt, db)] = ev
                bank_free[bi] = ev
        init_ready = merge(*init_tok.values())

        m1_half = [kfm2_s[0], vtm2_s[0]]
        m1_tok = {}
        for ap2 in range(8):
            for db in range(2):
                tiles = [(sxx_s[:, 1024 * c + 128 * ap2:1024 * c + 128 * (ap2 + 1)],
                          wtkv_s[:, 1024 * c + 512 * db:1024 * c + 512 * (db + 1)])
                         for c in range(8)]
                tiles.append((hrow_s[0:1, 128 * ap2:128 * (ap2 + 1)],
                              brow_s[0:1, 512 * db:512 * (db + 1)]))
                bi, tok = mm_group(tiles, merge(sxx_ready, t_hrow, t_wtk,
                                                t_sw_misc1))
                dst = m1_half[db][:, 512 * ap2:512 * (ap2 + 1)]
                ev = sch.act(lambda a, bi=bi, dst=dst: a.activation(
                    out=dst, in_=pb[bi][:], func=AF.Identity), waits=tok)
                m1_tok[(ap2, db)] = ev
                bank_free[bi] = ev
        m1_ready = merge(*m1_tok.values())
        uacc_tok = {}
        m1_cons = None
        for a in range(8):
            dbk, ak = a // 4, a % 4
            for db in range(2):
                tiles = [(m1_half[dbk][:, 512 * c + 128 * ak:512 * c + 128 * (ak + 1)],
                          wtkv_s[:, 8192 + 1024 * c + 512 * db:8192 + 1024 * c + 512 * (db + 1)])
                         for c in range(8)]
                tiles.append((wkr_s[0:1, 128 * a:128 * (a + 1)],
                              brow_s[0:1, 1024 + 512 * db:1024 + 512 * (db + 1)]))
                bi, tok = mm_group(tiles, merge(m1_ready, t_wkr, t_wtkv))
                m1_cons = merge(m1_cons, tok)
                dst = uacc_s[:, 1024 * a + 512 * db:1024 * a + 512 * (db + 1)]
                ev = sch.act(lambda ac, bi=bi, dst=dst: ac.activation(
                    out=dst, in_=pb[bi][:], func=AF.Identity), waits=tok)
                uacc_tok[(a, db)] = ev
                bank_free[bi] = ev
        uacc_ready = merge(*uacc_tok.values())
        kfm_cons0_init = m1_cons
    t_consts = merge(t_consts, _t_ident, ("miscsw", sch.cnt["miscsw"]))
    t_u1 = ld_sw(u1_s[:], u1_h[:], gate=True)
    t_v2 = ld_sw(v2_s[:], v2_h[:], gate=True)
    t_consts = merge(t_consts, t_u1, t_v2)
    bi_us, tok_us = mm_group([(onescol_s[:], u1_s[:, 256 * c:256 * (c + 1)])
                              for c in range(8)],
                             merge(t_u1, t_ones, t_onescol), nf=256, mrows=1)
    t_nu1 = sch.act(lambda a, bi=bi_us: a.activation(
        out=nu1s_s[0:1, :], in_=pb[bi][0:1, 0:256], func=AF.Identity,
        scale=f32(-1.0 / 1024.0)), waits=tok_us)
    bank_free[bi_us] = t_nu1

    ln_cons = [None]
    xq_cons = [None, None]
    wb_tok = {}
    p_cons = [None]
    q2_cons = [None]
    h_cons = [None]
    stage_cons = [None] * 4
    xsc_cons = [None, None]
    kfm_tokens = [None, init_ready]
    vtm_tokens = [None, init_ready]
    kfm_cons = [kfm_cons0_init, None]
    vtm_cons = [kfm_cons0_init, None]
    u_ready = [uacc_ready]

    def ln_stats(src_s, src_ready, bw, reps=True, cols=False, mid=None,
                 chunk_tok=None):
        guard = ln_cons[0]
        ctok = chunk_tok or {}
        banks = {}
        bsum = acquire_bank()
        tok = None
        for c in range(8):
            w = merge(ctok.get(c, src_ready), t_consts,
                      bank_free[bsum] if c == 0 else None)
            tok = sch.pe(lambda t, c=c, bi=bsum, ss=src_s: t.matmul(
                pb[bi][0:1, 0:512], onescol_s[:], ss[:, 512 * c:512 * (c + 1)],
                start=(c == 0), stop=(c == 7)), waits=w, inc=(c == 7))
        bank_open[bsum] = False
        banks[0] = (bsum, tok)
        sq_tok = {}
        mmsq_tok = {}
        bsq = acquire_bank()
        for c in range(8):
            dst = ysq_s[c % 3]
            sq_tok[c] = sch.act(
                lambda a, c=c, dst=dst, ss=src_s: a.activation(
                    out=dst[:], in_=ss[:, 512 * c:512 * (c + 1)], func=AF.Square),
                waits=merge(ctok.get(c, src_ready), guard, mmsq_tok.get(c - 3)))
            w = merge(sq_tok[c], bank_free[bsq] if c == 0 else None, t_consts)
            mmsq_tok[c] = sch.pe(
                lambda t, c=c, bi=bsq: t.matmul(
                    pb[bi][0:1, 0:512], onescol_s[:], ysq_s[c % 3][:],
                    start=(c == 0), stop=(c == 7)),
                waits=w, inc=True)
        bank_open[bsq] = False
        banks[1] = (bsq, mmsq_tok[7])
        (bsumi, toks), (bsqi, tokq) = banks[0], banks[1]
        evA = sch.dve(lambda v, bi=bsumi: v.tensor_copy(
            rowA_s[0:1, :], pb[bi][0:1, :]), waits=merge(toks, guard))
        bank_free[bsumi] = evA
        evB = sch.dve(lambda v, bi=bsqi: v.tensor_copy(
            rowB_s[0:1, :], pb[bi][0:1, :]), waits=merge(tokq, guard))
        bank_free[bsqi] = evB
        bcol = acquire_bank()
        ctok = None
        for j in range(8):
            row, rtok = (rowA_s, evA) if j < 4 else (rowB_s, evB)
            jj = j % 4
            w = merge(rtok, t_ones, bank_free[bcol] if j == 0 else None)
            ctok = sch.pe(lambda t, j=j, row=row, jj=jj, bi=bcol: t.matmul(
                pb[bi][0:128, j:j + 1], row[0:1, 128 * jj:128 * (jj + 1)],
                ones_s[0:1, 0:1], start=True, stop=True), waits=w,
                inc=(j == 7))
        bank_open[bcol] = False
        evC = sch.dve(lambda v, bi=bcol: v.tensor_copy(
            colsq_s[:], pb[bi][0:128, 0:8]), waits=merge(ctok, guard))
        bank_free[bcol] = evC
        t1c = sch.dve(lambda v: v.tensor_scalar(
            out=mcq_s[:], in0=colsq_s[:, 0:4], scalar1=float(1.0 / 1024),
            scalar2=None, op0=ALU.mult), waits=evC)
        t2c = sch.dve(lambda v: v.tensor_tensor(
            out=cscq_s[:], in0=mcq_s[:], in1=mcq_s[:], op=ALU.mult), waits=t1c)
        t3c = sch.dve(lambda v: v.scalar_tensor_tensor(
            out=vcq_s[:], in0=colsq_s[:, 4:8], scalar=float(1.0 / 1024),
            in1=cscq_s[:], op0=ALU.mult, op1=ALU.subtract),
            waits=merge(t2c, evC))
        t4c = sch.act(lambda a: a.activation(
            out=cscq_s[:], in_=vcq_s[:], func=AF.Sqrt, bias=f32(1e-5)),
            waits=t3c)
        t5c = sch.dve(lambda v: v.reciprocal(out=vcq_s[:], in_=cscq_s[:]),
                      waits=t4c)
        cb1 = sch.dve(lambda v: v.tensor_copy(colsb_s[:, 0:4], mcq_s[:]),
                      waits=merge(t1c, guard))
        cb2 = sch.dve(lambda v: v.tensor_copy(colsb_s[:, 4:8], vcq_s[:]),
                      waits=merge(t5c, guard))
        out = {"t1": evA, "sq": merge(*sq_tok.values())}
        if reps:
            brow1 = acquire_bank()
            brow2 = acquire_bank()
            rtok1 = rtok2 = None
            for j in range(8):
                bi = brow1 if j < 4 else brow2
                w = merge(cb1 if j < 4 else cb2, t_consts,
                          bank_free[bi] if j % 4 == 0 else None)
                tok = sch.pe(lambda t, j=j, bi=bi: t.matmul(
                    pb[bi][0:1, 128 * (j % 4):128 * (j % 4 + 1)],
                    colsb_s[:, j:j + 1], ident_s[:, 0:128],
                    start=True, stop=True), waits=w, inc=(j % 4 == 3))
                if j == 3:
                    rtok1 = tok
                elif j == 7:
                    rtok2 = tok
            bank_open[brow1] = False
            bank_open[brow2] = False
            evD = sch.dve(lambda v, bi=brow1: v.tensor_copy(
                rowD_s[0:1, :], pb[bi][0:1, 0:512]), waits=merge(rtok1, guard))
            evE = sch.dve(lambda v, bi=brow2: v.tensor_copy(
                rowB_s[0:1, :], pb[bi][0:1, 0:512]), waits=merge(rtok2, guard))
            bank_free[brow1] = evD
            bank_free[brow2] = evE
            bi6, tok6 = mm_group([(ones_s[0:1, 0:128], rowD_s[0:1, :])],
                                 merge(evD, guard, t_consts), nf=512)
            t6 = sch.act(lambda a, bi=bi6: a.activation(
                out=mrep_s[:], in_=pb[bi][:], func=AF.Identity),
                waits=merge(tok6, guard))
            bank_free[bi6] = t6
            bi7, tok7 = mm_group([(ones_s[0:1, 0:128], rowB_s[0:1, :])],
                                 merge(evE, guard), nf=512)
            t7 = sch.act(lambda a, bi=bi7: a.activation(
                out=rrep_s[:], in_=pb[bi][:], func=AF.Identity),
                waits=merge(tok7, guard))
            bank_free[bi7] = t7
            out["reps"] = merge(t6, t7)
        if mid is not None:
            mid()
        return out

    for r in range(NR):
        par = r % 2
        prv = 1 - par
        def bw(eng):
            return None

        with nc.sbuf_tensor(f"y1_{r}", [128, 4096], BF16) as y1_s:
            with nc.sbuf_tensor(f"yraw{r}", [128, 4096], BF16, side="right") as yraw_s:
              with nc.sbuf_tensor(f"wtq{r}", [128, 8192], BF16, side="right") as wtq_s, \
                   nc.sbuf_tensor(f"ktms{r}", [128, 4096], BF16, side="right") as ktms_s:
                gw = merge(p_cons[0], q2_cons[0], h_cons[0],
                           halo_gate if r == 0 else None)
                xqr_s = xqr2_s[:, 4096 * par:4096 * (par + 1)]
                t_xq = xq_tok[r]
                if r < 3:
                    xq_tok[r + 1] = ld_sw(
                        xqr2_s[:, 4096 * ((r + 1) % 2):4096 * ((r + 1) % 2 + 1)],
                        xq_h[:, 4096 * (r + 1):4096 * (r + 2)],
                        waits=xq_cons[(r + 1) % 2], dsem="xqs", gate=True)
                if r == 0:
                    for i in range(2):
                        ld_sw(wtq_s[:, 4096 * i:4096 * (i + 1)],
                              wtq_h[:, 4096 * i:4096 * (i + 1)],
                              waits=gw if i == 0 else None, dsem="wtq",
                              gate=(i == 0))
                    t_wtq = ("wtq", sch.cnt["wtq"])
                    wb_tok["wtq"] = ld_hw(wtqb_d, wtq_s[:], waits=t_wtq,
                                          sem="wbq", gate=True)
                else:
                    t_wtq = ld_hw_act(wtq_s[:], wtqb_d,
                                      waits=merge(p_cons[0],
                                                  wb_tok["v1"] if r == 1 else None),
                                      sem="wbq", gate=True)

                with nc.sbuf_tensor(f"abuf{r}", [128, 4096], BF16) as abuf_s, \
                     nc.sbuf_tensor(f"qfm{r}", [128, 4096], BF16) as qfm_s:
                    qfm_tok, kfm_tok, vtm_tok = {}, {}, {}
                    for ot in range(8):
                        tiles = [(wtkv_s[:, 1024 * c + 128 * ot:1024 * c + 128 * (ot + 1)],
                                  xqr_s[:, 512 * c:512 * (c + 1)]) for c in range(8)]
                        bi, tok = mm_group(tiles, merge(t_xq, t_wtk, kfm_cons[par]))
                        dst = kfm2_s[par][:, 512 * ot:512 * (ot + 1)]
                        ev = sch.act(lambda a, bi=bi, dst=dst, ot=ot, par=par:
                                     a.activation(
                                         out=dst, in_=pb[bi][:], func=AF.Identity,
                                         bias=bqcol_s[:, 8 + ot:8 + ot + 1]),
                                     waits=tok)
                        kfm_tok[ot] = ev
                        bank_free[bi] = ev
                    for tt in range(4):
                        for db in range(2):
                            tiles = [(xqr_s[:, 512 * c + 128 * tt:512 * c + 128 * (tt + 1)],
                                      wtkv_s[:, 8192 + 1024 * c + 512 * db:8192 + 1024 * c + 512 * (db + 1)])
                                     for c in range(8)]
                            tiles.append((ones_s[0:1, 0:128],
                                          brow_s[0:1, 1024 + 512 * db:1024 + 512 * (db + 1)]))
                            bi, tok = mm_group(tiles, merge(t_xq, t_wtkv, vtm_cons[par]))
                            dst = vtm2_s[par][:, 1024 * tt + 512 * db:1024 * tt + 512 * (db + 1)]
                            ev = sch.act(lambda a, bi=bi, dst=dst: a.activation(
                                out=dst, in_=pb[bi][:], func=AF.Identity),
                                waits=tok)
                            vtm_tok[(tt, db)] = ev
                            bank_free[bi] = ev
                    for ot in range(8):
                        tiles = [(wtq_s[:, 1024 * c + 128 * ot:1024 * c + 128 * (ot + 1)],
                                  xqr_s[:, 512 * c:512 * (c + 1)]) for c in range(8)]
                        bi, tok = mm_group(tiles, merge(t_xq, t_wtq, t_consts))
                        dst = qfm_s[:, 512 * ot:512 * (ot + 1)]
                        ev = sch.act(lambda a, bi=bi, dst=dst, ot=ot: a.activation(
                            out=dst, in_=pb[bi][:], func=AF.Identity,
                            bias=bqcol_s[:, ot:ot + 1]),
                            waits=merge(tok, bw("act")))
                        qfm_tok[ot] = ev
                        bank_free[bi] = ev
                    qfm_ready = merge(*qfm_tok.values())
                    kfm_ready = merge(*kfm_tok.values())
                    vtm_ready = merge(*vtm_tok.values())
                    kfm_tokens[par] = kfm_ready
                    vtm_tokens[par] = vtm_ready
                    t_s1_pe = {"pe": sch.cnt["pe"]}

                    a_tok = {}
                    for jt in range(8):
                        src = kfm2_s[prv] if jt < 4 else kfm2_s[par]
                        src_tok = kfm_tokens[prv] if jt < 4 else kfm_ready
                        jj = jt % 4
                        tiles = [(src[:, 512 * c + 128 * jj:512 * c + 128 * (jj + 1)],
                                  qfm_s[:, 512 * c:512 * (c + 1)]) for c in range(8)]
                        bi, tok = mm_group(tiles, merge(src_tok, qfm_ready))
                        if jt < 4:
                            kfm_cons[prv] = merge(kfm_cons[prv], tok)
                        dst = abuf_s[:, 512 * jt:512 * (jt + 1)]
                        c0 = 896 - 128 * jt
                        ev = sch.dve(lambda v, bi=bi, dst=dst, c0=c0:
                                     v.tensor_tensor(out=dst, in0=pb[bi][:],
                                                     in1=master_s[:, c0:c0 + 512],
                                                     op=ALU.mult), waits=tok)
                        a_tok[jt] = ev
                        bank_free[bi] = ev
                    a_ready = merge(*a_tok.values())

                    yraw_tok = {}
                    last_cross_pe = None
                    for dv in range(8):
                        tiles = []
                        for jt in range(8):
                            vsrc = vtm2_s[prv] if jt < 4 else vtm2_s[par]
                            jj = jt % 4
                            tiles.append((vsrc[:, 1024 * jj + 128 * dv:1024 * jj + 128 * (dv + 1)],
                                          abuf_s[:, 512 * jt:512 * (jt + 1)]))
                        biA, tokA = mm_group(tiles, merge(a_ready, vtm_tokens[prv],
                                                          vtm_ready))
                        vtm_cons[prv] = merge(vtm_cons[prv], tokA)
                        tiles = [(uacc_s[:, 1024 * a + 128 * dv:1024 * a + 128 * (dv + 1)],
                                  qfm_s[:, 512 * a:512 * (a + 1)]) for a in range(8)]
                        biB, tokB = mm_group(tiles, merge(u_ready[0], qfm_ready))
                        last_cross_pe = merge(last_cross_pe, tokB)
                        xb = dv % 2
                        s1 = sch.dve(lambda v, biB=biB, xb=xb: v.tensor_tensor(
                            out=xsc_s[xb][:], in0=pb[biB][:], in1=aQrep_s[:],
                            op=ALU.mult), waits=merge(tokB, xsc_cons[xb]))
                        bank_free[biB] = s1
                        s2 = sch.dve(lambda v, xq=xqr_s, dv=dv, xb=xb: v.tensor_tensor(
                            out=xsc_s[xb][:], in0=xsc_s[xb][:],
                            in1=xq[:, 512 * dv:512 * (dv + 1)], op=ALU.add),
                            waits=merge(s1, t_xq))
                        dst = yraw_s[:, 512 * dv:512 * (dv + 1)]
                        s3 = sch.dve(lambda v, biA=biA, dst=dst, xb=xb: v.tensor_tensor(
                            out=dst, in0=pb[biA][:], in1=xsc_s[xb][:], op=ALU.add),
                            waits=merge(tokA, s2))
                        yraw_tok[dv] = s3
                        xsc_cons[xb] = s3
                        bank_free[biA] = s3
                    yraw_ready = merge(*yraw_tok.values())

                    def s2_emit(prv=prv, lcp=None):
                        ktm_tok = {}
                        for tt in range(4):
                            for g in range(2):
                                pt = (2 * tt + g) % 2
                                tok = None
                                for j in range(4):
                                    dc = 4 * g + j
                                    tok = sch.pe(lambda t, pt=pt, dc=dc, tt=tt, j=j, prv=prv:
                                                 t.transpose(
                                                     ptr[pt][:, 128 * j:128 * (j + 1)],
                                                     kfm2_s[prv][:, 512 * dc + 128 * tt:512 * dc + 128 * (tt + 1)],
                                                     ident_s[:]),
                                                 waits=merge(kfm_tokens[prv],
                                                             ptr_free[pt] if j == 0 else None,
                                                             t_consts), inc=(j == 3))
                                kfm_cons[prv] = merge(kfm_cons[prv], tok)
                                dst = ktms_s[:, 1024 * tt + 512 * g:1024 * tt + 512 * (g + 1)]
                                if g == 0:
                                    ev = sch.act(lambda a, pt=pt, dst=dst, tt=tt:
                                                 a.activation(
                                                     out=dst, in_=ptr[pt][:, 0:512],
                                                     func=AF.Identity,
                                                     scale=acol_s[:, tt:tt + 1]),
                                                 waits=merge(tok, wb_tok.get("u2") if r == 1 else None))
                                else:
                                    ev = sch.dve(lambda v, pt=pt, dst=dst, tt=tt:
                                                 v.tensor_scalar(
                                                     out=dst, in0=ptr[pt][:, 0:512],
                                                     scalar1=acol_s[:, tt:tt + 1],
                                                     scalar2=None, op0=ALU.mult),
                                                 waits=merge(tok, wb_tok.get("u2") if r == 1 else None))
                                ptr_free[pt] = ev
                                ktm_tok[(tt, g)] = ev
                        ktm_ready = merge(*ktm_tok.values())
                        u2_tok = {}
                        for a in range(8):
                            for db in range(2):
                                tiles = [(ktms_s[:, 1024 * tt + 128 * a:1024 * tt + 128 * (a + 1)],
                                          vtm2_s[prv][:, 1024 * tt + 512 * db:1024 * tt + 512 * (db + 1)])
                                         for tt in range(4)]
                                bi, tok = mm_group(tiles, merge(ktm_ready, vtm_tokens[prv],
                                                                lcp))
                                vtm_cons[prv] = merge(vtm_cons[prv], tok)
                                dst = uacc_s[:, 1024 * a + 512 * db:1024 * a + 512 * (db + 1)]
                                ev = sch.dve(lambda v, bi=bi, dst=dst: v.tensor_tensor(
                                    out=dst, in0=pb[bi][:], in1=dst, op=ALU.add),
                                    waits=merge(tok, u_ready[0], lcp))
                                u2_tok[(a, db)] = ev
                                bank_free[bi] = ev
                        _chi = float(np.float32(AEFF ** W))
                        _clo = float(np.float32(AEFF ** W / np.float64(np.float32(AEFF ** W))))
                        t_us = sch.dve(lambda v: v.tensor_scalar(
                            out=uacc_s[:], in0=uacc_s[:], scalar1=_chi,
                            scalar2=_clo, op0=ALU.mult, op1=ALU.mult),
                            waits=merge(*u2_tok.values()))
                        u_ready[0] = t_us

                    pbis = {}

                    def emit_px():
                        for tb in range(2):
                            for rt in range(2):
                                bi = acquire_bank()
                                pbis[(tb, rt)] = bi
                                for i in range(8):
                                    w = merge(yraw_ready, t_consts,
                                              bank_free[bi]) if i == 0 else None
                                    sch.pe(lambda t, bi=bi, rt=rt, tb=tb, i=i, ys=yraw_s:
                                           t.matmul(
                                               pb[bi][:, 0:256],
                                               u1_s[:, 256 * i + 128 * rt:256 * i + 128 * (rt + 1)],
                                               ys[:, 512 * i + 256 * tb:512 * i + 256 * (tb + 1)],
                                               start=(i == 0), stop=False), waits=w)

                    def mid_all(prv=prv, lcp=last_cross_pe, do_s2=(r < NR - 1)):
                        if do_s2:
                            s2_emit(prv, lcp)
                        emit_px()

                    ln1 = ln_stats(yraw_s, yraw_ready, bw, reps=True, mid=mid_all,
                                   chunk_tok=yraw_tok)
                    reps_tok, sq_done = ln1["reps"], ln1["sq"]
                    w_s3done = {"pe": sch.cnt["pe"], "dve": sch.cnt["dve"]}
                    xq_cons[par] = w_s3done

              with nc.sbuf_tensor(f"v1s{r}", [128, 8192], BF16, side="right") as v1_s, \
                   nc.sbuf_tensor(f"u2s{r}", [128, 8192], BF16, side="right") as u2_s, \
                   nc.sbuf_tensor(f"h{r}", [128, 8192], BF16, side="right") as h_s, \
                   nc.sbuf_tensor(f"out2_{r}", [128, 4096], BF16, side="right") as out2_s:
                  ww2 = {"pe": sch.cnt["pe"], "dve": sch.cnt["dve"],
                         "act": sch.cnt["act"]}
                  if r == 0:
                      ld_sw(v1_s[:, 0:4096], v1_h[:, 0:4096],
                            waits=merge(t_s1_pe, wb_tok["wtq"]),
                            dsem="ffnw", gate=True)
                      ld_sw(v1_s[:, 4096:8192], v1_h[:, 4096:8192], dsem="ffnw")
                      t_v1 = ("ffnw", sch.cnt["ffnw"])
                      wb_tok["v1"] = ld_hw(v1b_d, v1_s[:], waits=t_v1,
                                           sem="wbv", gate=True)
                      ld_sw(u2_s[:, 0:4096], u2_h[:, 0:4096], waits=ww2,
                            dsem="ffnw", gate=True)
                      ld_sw(u2_s[:, 4096:8192], u2_h[:, 4096:8192], dsem="ffnw")
                      t_ffnw = ("ffnw", sch.cnt["ffnw"])
                      wb_tok["u2"] = ld_hw(u2b_d, u2_s[:], waits=t_ffnw,
                                           sem="wbu", gate=True)
                  else:
                      t_v1 = ld_hw_act(v1_s[:], v1b_d,
                                       waits=merge(t_s1_pe, wb_tok["v1"]),
                                       sem="wbv", gate=True)
                      t_ffnw = ld_hw_act(u2_s[:], u2b_d,
                                         waits=merge(ww2, wb_tok["u2"]),
                                         sem="wbu", gate=True)
                  out2_tok = {}
                  for tb in range(2):
                      p_tok = {}
                      for rt in range(2):
                          bi = pbis[(tb, rt)]
                          tok = sch.pe(lambda t, bi=bi, rt=rt, tb=tb: t.matmul(
                              pb[bi][:, 0:256],
                              nu1s_s[0:1, 128 * rt:128 * (rt + 1)],
                              rowA_s[0:1, 256 * tb:256 * (tb + 1)],
                              start=False, stop=True),
                              waits=merge(ln1["t1"], t_nu1) if (tb, rt) == (0, 0) else None,
                              inc=True)
                          bank_open[bi] = False
                          dst = p_s[:, 256 * rt:256 * (rt + 1)]
                          ev = sch.dve(lambda v, bi=bi, dst=dst, tb=tb: v.tensor_tensor(
                              out=dst, in0=pb[bi][:, 0:256],
                              in1=rrep_s[:, 256 * tb:256 * (tb + 1)], op=ALU.mult),
                              waits=merge(tok, p_cons[0], reps_tok))
                          p_tok[rt] = ev
                          bank_free[bi] = ev
                      p_ready = merge(*p_tok.values())
                      if tb == 0:
                          y1_tok = {}
                          for c in range(8):
                              dtmp = ysq_s[c % 3]
                              ta = sch.dve(lambda v, c=c, dtmp=dtmp, ys=yraw_s:
                                           v.tensor_tensor(
                                               out=dtmp[:],
                                               in0=ys[:, 512 * c:512 * (c + 1)],
                                               in1=mrep_s[:], op=ALU.subtract),
                                           waits=merge(reps_tok, sq_done,
                                                       y1_tok.get(c - 4)))
                              y1_tok[c] = sch.dve(lambda v, c=c, dtmp=dtmp, y1s=y1_s:
                                                  v.tensor_tensor(
                                                      out=y1s[:, 512 * c:512 * (c + 1)],
                                                      in0=dtmp[:], in1=rrep_s[:],
                                                      op=ALU.mult), waits=ta)
                          y1_ready = merge(*y1_tok.values())
                          ln_cons[0] = y1_ready
                      h_ev = None
                      for ft in range(32):
                          tiles = [(v1_s[:, 4096 * rc + 128 * ft:4096 * rc + 128 * (ft + 1)],
                                    p_s[:, 256 * rc:256 * (rc + 1)]) for rc in range(2)]
                          bi, tok = mm_group(tiles, merge(p_ready, h_cons[0], t_v1), nf=256)
                          p_cons[0] = merge(p_cons[0], tok)
                          dst = h_s[:, 256 * ft:256 * (ft + 1)]
                          h_ev = sch.act(lambda a, bi=bi, dst=dst, ft=ft: a.activation(
                              out=dst, in_=pb[bi][:, 0:256], func=AF.Gelu_apprx_tanh,
                              bias=b1c_s[:, ft:ft + 1]), waits=tok)
                          bank_free[bi] = h_ev
                      q2_tok = {}
                      for rt in range(2):
                          tiles = [(u2_s[:, 256 * fc + 128 * rt:256 * fc + 128 * (rt + 1)],
                                    h_s[:, 256 * fc:256 * (fc + 1)]) for fc in range(32)]
                          bi, tok = mm_group(tiles, merge(h_ev, q2_cons[0], t_ffnw), nf=256)
                          h_cons[0] = merge(h_cons[0], tok)
                          dst = q2_s[:, 256 * rt:256 * (rt + 1)]
                          ev = sch.act(lambda a, bi=bi, dst=dst: a.activation(
                              out=dst, in_=pb[bi][:, 0:256], func=AF.Identity),
                              waits=tok)
                          q2_tok[rt] = ev
                          bank_free[bi] = ev
                      q2_ready = merge(*q2_tok.values())
                      for dt in range(8):
                          tiles = [(v2_s[:, 1024 * rc + 128 * dt:1024 * rc + 128 * (dt + 1)],
                                    q2_s[:, 256 * rc:256 * (rc + 1)]) for rc in range(2)]
                          bi, tok = mm_group(tiles, q2_ready, nf=256)
                          q2_cons[0] = merge(q2_cons[0], tok)
                          dst = out2_s[:, 512 * dt + 256 * tb:512 * dt + 256 * (tb + 1)]
                          yin = y1_s[:, 512 * dt + 256 * tb:512 * dt + 256 * (tb + 1)]
                          ev = sch.dve(lambda v, bi=bi, dst=dst, yin=yin, dt=dt:
                                       v.scalar_tensor_tensor(
                                           out=dst, in0=pb[bi][:, 0:256],
                                           scalar=b2col_s[:, dt:dt + 1],
                                           in1=yin, op0=ALU.add, op1=ALU.add),
                                       waits=tok)
                          out2_tok[(tb, dt)] = ev
                          bank_free[bi] = ev
                  out2_ready = merge(*out2_tok.values())

                  s5_ev = {}
                  s5_sq_all = []
                  s5_nrm_all = []

                  def s5_tile(tt):
                      sg = tt
                      o = 4 * (tt % 2)
                      ev_all = []
                      sq_all = []
                      for g in range(2):
                          pt = g % 2
                          tok = None
                          for j in range(4):
                              dc = 4 * g + j
                              tok = sch.pe(lambda t, pt=pt, dc=dc, tt=tt, j=j, o2=out2_s:
                                           t.transpose(
                                               ptr[pt][:, 128 * j:128 * (j + 1)],
                                               o2[:, 512 * dc + 128 * tt:512 * dc + 128 * (tt + 1)],
                                               ident_s[:]),
                                           waits=merge(out2_tok[(tt // 2, dc)],
                                                       ptr_free[pt] if j == 0 else None,
                                                       t_consts),
                                           inc=(j == 3))
                          ev = sch.act(lambda a, pt=pt, g=g, sg=sg, tt=tt: a.activation(
                              out=stage_s[sg][:, 512 * g:512 * (g + 1)],
                              in_=ptr[pt][:, 0:512], func=AF.Identity,
                              accum_out=colacc_s[:, 4 * tt + g:4 * tt + g + 1]),
                              waits=merge(tok, stage_cons[sg]))
                          sq = sch.act(lambda a, pt=pt, g=g, tt=tt: a.activation(
                              out=ysq_s[g][:], in_=ptr[pt][:, 0:512], func=AF.Square,
                              accum_out=colacc_s[:, 4 * tt + 2 + g:4 * tt + 3 + g]),
                              waits=merge(tok, ln1["t1"]))
                          ptr_free[pt] = merge(ev, sq)
                          ev_all.append(ev)
                          sq_all.append(sq)
                      s5_ev[tt] = merge(*ev_all)
                      s5_sq_all.extend(sq_all)
                      ta = sch.dve(lambda v, tt=tt, o=o: v.tensor_tensor(
                          out=cw_s[:, o:o + 1], in0=colacc_s[:, 4 * tt:4 * tt + 1],
                          in1=colacc_s[:, 4 * tt + 1:4 * tt + 2], op=ALU.add),
                          waits=s5_ev[tt])
                      tb2 = sch.dve(lambda v, tt=tt, o=o: v.tensor_scalar(
                          out=mcol_s[:, tt:tt + 1], in0=cw_s[:, o:o + 1],
                          scalar1=float(-1.0 / 1024), scalar2=None, op0=ALU.mult),
                          waits=ta)
                      tc = sch.dve(lambda v, tt=tt, o=o: v.tensor_tensor(
                          out=cw_s[:, o + 1:o + 2], in0=colacc_s[:, 4 * tt + 2:4 * tt + 3],
                          in1=colacc_s[:, 4 * tt + 3:4 * tt + 4], op=ALU.add),
                          waits=merge(*sq_all))
                      td = sch.dve(lambda v, tt=tt, o=o: v.tensor_tensor(
                          out=cw_s[:, o + 2:o + 3], in0=mcol_s[:, tt:tt + 1],
                          in1=mcol_s[:, tt:tt + 1], op=ALU.mult), waits=tb2)
                      te = sch.dve(lambda v, tt=tt, o=o: v.scalar_tensor_tensor(
                          out=cw_s[:, o + 3:o + 4], in0=cw_s[:, o + 1:o + 2],
                          scalar=float(1.0 / 1024), in1=cw_s[:, o + 2:o + 3],
                          op0=ALU.mult, op1=ALU.subtract), waits=merge(tc, td))
                      tf = sch.act(lambda a, tt=tt, o=o: a.activation(
                          out=cs_s[:, tt % 2:tt % 2 + 1], in_=cw_s[:, o + 3:o + 4],
                          func=AF.Sqrt, bias=f32(1e-5)), waits=te)
                      tg = sch.dve(lambda v, tt=tt: v.reciprocal(
                          out=rcol_s[:, tt:tt + 1], in_=cs_s[:, tt % 2:tt % 2 + 1]),
                          waits=tf)
                      return merge(tb2, tg)

                  for tt in range(4):
                      sg = tt
                      cols_tok = s5_tile(tt)
                      nrm = sch.dve(lambda v, sg=sg, tt=tt: v.tensor_scalar(
                          out=stage_s[sg][:], in0=stage_s[sg][:],
                          scalar1=mcol_s[:, tt:tt + 1], scalar2=rcol_s[:, tt:tt + 1],
                          op0=ALU.add, op1=ALU.mult),
                          waits=merge(s5_ev[tt], cols_tok))
                      s5_nrm_all.append(nrm)
                      st_half = []
                      for hh in range(2):
                          st_half.append(sch.sync_dma(
                              lambda s_, sg=sg, tt=tt, r=r, hh=hh: s_.dma_start(
                                  out=y_h[512 * r + 128 * tt + 64 * hh:
                                          512 * r + 128 * tt + 64 * (hh + 1), :],
                                  in_=stage_s[sg][64 * hh:64 * (hh + 1), :]),
                              sem=f"st{sg + 4 * hh}", waits=nrm, gate=True))
                      stage_cons[sg] = merge(*st_half)
                  ln_cons[0] = merge(y1_ready, p_ready, *s5_sq_all, *s5_nrm_all)

    sch.wait_only("sync", {k: sch.cnt[k] for k in Sched.DMA_SEMS if sch.cnt[k]})

    with nc.Block() as block:
        def replay(eng_name):
            def go(engine):
                waited = {}
                for waits, fn, inc, amt in sch.ops[eng_name]:
                    for s, v in waits.items():
                        if waited.get(s, 0) < v:
                            engine.wait_ge(sem[s], v)
                            waited[s] = v
                    if fn is None:
                        continue
                    inst = fn(engine)
                    if inc is not None:
                        inst.then_inc(sem[inc], amt)
            return go
        block.tensor(replay("pe"))
        block.vector(replay("dve"))
        block.scalar(replay("act"))
        block.gpsimd(replay("gp"))
        block.sync(replay("sync"))

    ctx.close()
    return nc


_M32_CACHE = [None]


def _m32_table():
    if _M32_CACHE[0] is not None:
        return _M32_CACHE[0]
    try:
        import jax
        import jax.numpy as jnp
        with jax.default_device(jax.devices("cpu")[0]):
            m = np.asarray(0.99 ** jnp.arange(S).astype(jnp.float32),
                           dtype=np.float64)
    except Exception:
        m = (AEFF ** np.arange(S, dtype=np.float64)).astype(np.float32).astype(np.float64)
    _M32_CACHE[0] = m
    return m


def _hilo(t64):
    hi = t64.astype(np.float32)
    with np.errstate(divide="ignore", invalid="ignore"):
        lo = np.where(hi != 0, t64 / hi.astype(np.float64), 0.0).astype(np.float32)
    return np.ascontiguousarray(hi), np.ascontiguousarray(lo)


def _fm(a):
    T = a.shape[0]
    return np.ascontiguousarray(
        a.T.reshape(D // 128, 128, T).transpose(1, 0, 2).reshape(128, -1))


def _ck(w, rows):
    C = w.shape[1]
    return np.ascontiguousarray(
        w.reshape(rows, 128, C).transpose(1, 0, 2).reshape(128, rows * C))


def make_in_maps(L0_attn_x, L0_attn_W_qkv, L0_attn_b_qkv, L0_ffn_U1, L0_ffn_V1,
                 L0_ffn_b1, L0_ffn_U2, L0_ffn_V2, L0_ffn_b2):
    x = np.asarray(L0_attn_x, dtype=np.float32).reshape(S, D)
    Wq = np.asarray(L0_attn_W_qkv, dtype=np.float32)
    bq = np.asarray(L0_attn_b_qkv, dtype=np.float32)
    U1 = np.asarray(L0_ffn_U1, dtype=np.float32)
    V1 = np.asarray(L0_ffn_V1, dtype=np.float32)
    b1 = np.asarray(L0_ffn_b1, dtype=np.float32)
    U2 = np.asarray(L0_ffn_U2, dtype=np.float32)
    V2 = np.asarray(L0_ffn_V2, dtype=np.float32)
    b2 = np.asarray(L0_ffn_b2, dtype=np.float32)

    WT = np.ascontiguousarray(Wq.T)
    wtq = _ck(WT[:, 0:1024], 8)
    wtkv = np.concatenate([_ck(WT[:, 1024:2048], 8),
                           _ck(WT[:, 2048:3072], 8)], axis=1)
    brow = np.ascontiguousarray(bq[None, 1024:])
    bqcol = np.stack([bq[128 * o:128 * (o + 1)] for o in range(8)]
                     + [bq[1024 + 128 * o:1024 + 128 * (o + 1)] for o in range(8)],
                     axis=1).astype(np.float32)
    u1 = _ck(U1, 8)
    v1 = _ck(V1, 2)
    u2 = _ck(U2, 32)
    v2 = _ck(V2, 2)
    b1c = np.ascontiguousarray(b1.reshape(32, 128).T)
    b2col = np.ascontiguousarray(b2.reshape(8, 128).T)

    m32 = _m32_table()
    p = np.arange(128)[:, None]
    cc = np.arange(1408)[None, :]
    Dm = cc - 384 - p
    master = np.where(Dm >= 0, m32[np.clip(Dm, 0, None)] / 32.0, 0.0).astype(np.float32)

    t = np.arange(T_R, dtype=np.float64)
    aQ64 = np.broadcast_to(AEFF ** (W + t) / 32.0, (128, T_R))
    aQhi, aQlo = _hilo(aQ64)
    c4 = np.arange(4, dtype=np.float64)[None, :]
    acol4_64 = AEFF ** (-(p + 128 * c4))
    cH = np.arange(4 * NSMAC, dtype=np.float64)[None, :]
    acolH_base64 = AEFF ** (NSMAC * 512 - (128 * cH + p))
    ident = np.eye(128, dtype=np.float32)

    shared = dict(wtkv=wtkv, wtq=wtq, brow=brow, bqcol=bqcol, u1=u1, v1=v1,
                  u2=u2, v2=v2, b1c=b1c, b2col=b2col, master=master,
                  aQhi=aQhi, aQlo=aQlo, ident=ident)

    in_maps = []
    for core in range(NCORES):
        own = x[T_CORE * core:T_CORE * (core + 1)]
        xq = np.concatenate([_fm(own[T_R * rr:T_R * (rr + 1)]) for rr in range(NR)],
                            axis=1)
        lo = T_CORE * core - HALO
        halo = np.zeros((HALO, D), dtype=np.float32)
        src_lo = max(lo, 0)
        if src_lo < T_CORE * core:
            halo[src_lo - lo:] = x[src_lo:T_CORE * core]
        xh_tm = halo[0:2560].reshape(20, 128, D).transpose(1, 0, 2).reshape(128, 20480)
        xh = np.concatenate([xh_tm, _fm(halo[2560:3072])], axis=1)
        hidx = 128 * cH + p
        valid = (lo + hidx) >= 0
        acolH64 = np.where(valid, np.sqrt(acolH_base64), 0.0)
        ac64 = np.concatenate([acol4_64, acolH64], axis=1)
        achi, aclo = _hilo(ac64)
        vmask = np.full((128, 1), 1.0 if core > 0 else 0.0, dtype=np.float32)
        m = dict(shared)
        m.update(xq=np.ascontiguousarray(xq), xh=np.ascontiguousarray(xh),
                 achi=achi, aclo=aclo, vmask=vmask)
        in_maps.append(m)
    return in_maps


def kernel(L0_attn_x, L0_attn_W_qkv, L0_attn_b_qkv, L0_ffn_U1, L0_ffn_V1,
           L0_ffn_b1, L0_ffn_U2, L0_ffn_V2, L0_ffn_b2):
    in_maps = make_in_maps(L0_attn_x, L0_attn_W_qkv, L0_attn_b_qkv, L0_ffn_U1,
                           L0_ffn_V1, L0_ffn_b1, L0_ffn_U2, L0_ffn_V2, L0_ffn_b2)
    nc = build_nc()
    res = run_bass_kernel_spmd(nc, in_maps, core_ids=list(range(NCORES)))
    out = np.concatenate([r["y"] for r in res.results], axis=0)
    return out.reshape(1, S, D).astype(np.float32)


# version 25
# speedup vs baseline: 1.0034x; 1.0034x over previous
import math
import numpy as np
import ml_dtypes
import concourse.bass as bass
import concourse.mybir as mybir
from concourse.bass_utils import run_bass_kernel_spmd
from contextlib import ExitStack

F32 = mybir.dt.float32
BF16 = mybir.dt.bfloat16
AF = mybir.ActivationFunctionType
ALU = mybir.AluOpType
f32 = np.float32

AEFF = float(np.float32(0.99))
D = 1024
S = 16384
NCORES = 8
T_CORE = S // NCORES
NR = 4
T_R = T_CORE // NR
W = 512
HALO = 3072
NSMAC = 5
DFF = 4096
RANK = 256


def merge(*tokens):
    w = {}
    for t in tokens:
        if t is None:
            continue
        items = t.items() if isinstance(t, dict) else [t]
        for s, v in items:
            if w.get(s, -1) < v:
                w[s] = v
    return w


class Sched:
    ENGINES = ("pe", "dve", "act", "gp", "sync")
    DMA_SEMS = ("hw", "wtk", "wtv", "wtq", "miscsw", "xh0", "xh1", "xh2", "xh3",
                "xqs", "ffnw", "st2", "st3", "st4", "st5", "st6", "st7", "xh4", "wbq", "wbv", "wbu",
                "st0", "st1", "tabs")

    def __init__(self):
        self.ops = {e: [] for e in self.ENGINES}
        self.cnt = {k: 0 for k in ("pe", "dve", "act", "gp") + self.DMA_SEMS}

    def _emit(self, eng, fn, waits, inc, amt):
        self.ops[eng].append((merge(waits) if waits else {}, fn, inc, amt))
        if inc is None:
            return None
        self.cnt[inc] += amt
        return (inc, self.cnt[inc])

    def pe(self, fn, waits=None, inc=False):
        return self._emit("pe", fn, waits, "pe" if inc else None, 1)

    def dve(self, fn, waits=None):
        return self._emit("dve", fn, waits, "dve", 1)

    def act(self, fn, waits=None):
        return self._emit("act", fn, waits, "act", 1)

    def gp(self, fn, waits=None):
        return self._emit("gp", fn, waits, "gp", 1)

    def gp_dma(self, fn, sem="miscsw", waits=None, gate=False):
        w = merge(waits, {sem: self.cnt[sem]}) if gate else waits
        return self._emit("gp", fn, w, sem, 16)

    def sync_dma(self, fn, waits=None, sem="hw", gate=False):
        w = merge(waits, {sem: self.cnt[sem]}) if gate else waits
        return self._emit("sync", fn, w, sem, 16)

    def act_dma(self, fn, waits=None, sem="hw", gate=False):
        w = merge(waits, {sem: self.cnt[sem]}) if gate else waits
        return self._emit("act", fn, w, sem, 16)

    def wait_only(self, eng, waits):
        return self._emit(eng, None, waits, None, 1)

    def barrier_token(self):
        return dict(self.cnt)


def build_nc():
    nc = bass.Bass("TRN2", target_bir_lowering=False,
                   dynamic_dma_scratch_size=4096)
    sch = Sched()

    def dram(name, shape, dtype=F32, kind="ExternalInput"):
        return nc.dram_tensor(name, shape, dtype, kind=kind).ap()

    xq_h = dram("xq", [128, 16384])
    xh_h = dram("xh", [128, 24576])
    wtkv_h = dram("wtkv", [128, 16384])
    wtq_h = dram("wtq", [128, 8192])
    brow_h = dram("brow", [1, 2048])
    bqcol_h = dram("bqcol", [128, 16])
    u1_h = dram("u1", [128, 2048])
    v1_h = dram("v1", [128, 8192])
    u2_h = dram("u2", [128, 8192])
    v2_h = dram("v2", [128, 2048])
    b1c_h = dram("b1c", [128, 32])
    b2col_h = dram("b2col", [128, 8])
    master_h = dram("master", [128, 1408])
    aQhi_h = dram("aQhi", [128, 512])
    aQlo_h = dram("aQlo", [128, 512])
    achi_h = dram("achi", [128, 4 + 4 * NSMAC])
    aclo_h = dram("aclo", [128, 4 + 4 * NSMAC])
    vmask_h = dram("vmask", [128, 1])
    ident_h = dram("ident", [128, 128])
    y_h = dram("y", [T_CORE, D], kind="ExternalOutput")
    wtqb_d = nc.dram_tensor("wtqb", [128, 8192], BF16, kind="Internal").ap()
    v1b_d = nc.dram_tensor("v1b", [128, 8192], BF16, kind="Internal").ap()
    u2b_d = nc.dram_tensor("u2b", [128, 8192], BF16, kind="Internal").ap()

    ctx = ExitStack()
    E = ctx.enter_context

    def sb(name, shape, dtype, side="left"):
        return E(nc.sbuf_tensor(name, shape, dtype, side=side))

    sem = {k: E(nc.semaphore(k)) for k in ("pe", "dve", "act", "gp") + Sched.DMA_SEMS}

    pb = [E(nc.psum_tensor(f"pb{i}", [128, 512], F32)) for i in range(6)]
    ptr = [E(nc.psum_tensor(f"ptr{i}", [128, 1024], BF16)) for i in range(2)]
    bank_free = [None] * 6
    bank_open = [False] * 6
    ptr_free = [None] * 2
    _rr = [0]

    def acquire_bank():
        for _ in range(6):
            i = _rr[0]
            _rr[0] = (i + 1) % 6
            if not bank_open[i]:
                bank_open[i] = True
                return i
        raise RuntimeError("no free psum bank")

    def mm_group(tiles, waits, nf=512, mrows=128, last_waits=None):
        bi = acquire_bank()
        n = len(tiles)
        tok = None
        for i, (lh, rh) in enumerate(tiles):
            w = merge(waits, bank_free[bi]) if i == 0 else None
            if i == n - 1 and last_waits:
                w = merge(w, last_waits)
            tok = sch.pe(
                lambda t, lh=lh, rh=rh, st=(i == 0), sp=(i == n - 1), bi=bi,
                nf=nf, mr=mrows:
                t.matmul(pb[bi][0:mr, 0:nf], lh, rh, start=st, stop=sp),
                waits=w, inc=(i == n - 1))
        bank_open[bi] = False
        return bi, tok

    master_s = sb("master_s", [128, 1408], F32)
    aQrep_s = sb("aQrep_s", [128, 512], F32)
    acol_s = sb("acol_s", [128, 4 + 4 * NSMAC], F32)
    vmask_s = sb("vmask_s", [128, 1], F32)
    ident_s = sb("ident_s", [128, 128], BF16)
    ones_s = sb("ones_s", [1, 512], BF16)
    onescol_s = sb("onescol_s", [128, 1], BF16)
    brow_s = sb("brow_s", [1, 2048], BF16)
    bqcol_s = sb("bqcol_s", [128, 16], F32)
    b1c_s = sb("b1c_s", [128, 32], F32)
    b2col_s = sb("b2col_s", [128, 8], F32)
    rowA_s = sb("rowA_s", [1, 512], BF16)
    rowB_s = sb("rowB_s", [1, 512], BF16)
    rowD_s = sb("rowD_s", [1, 512], BF16)
    colsq_s = sb("colsq_s", [128, 8], F32)
    mcq_s = sb("mcq_s", [128, 4], F32)
    cscq_s = sb("cscq_s", [128, 4], F32)
    vcq_s = sb("vcq_s", [128, 4], F32)
    colsb_s = sb("colsb_s", [128, 8], BF16)
    colacc_s = sb("colacc_s", [128, 16], F32)
    cw_s = sb("cw_s", [128, 8], F32)
    cs_s = sb("cs_s", [128, 2], F32)
    mrep_s = sb("mrep_s", [128, 512], BF16)
    rrep_s = sb("rrep_s", [128, 512], BF16)
    ysq_s = [sb(f"ysq{i}", [128, 512], BF16) for i in range(3)]
    xsc_s = [sb(f"xsc{i}", [128, 512], BF16) for i in range(2)]
    mcol_s = sb("mcol_s", [128, 4], F32)
    rcol_s = sb("rcol_s", [128, 4], F32)
    nu1s_s = sb("nu1s_s", [1, 256], BF16)
    p_s = sb("p_s", [128, 512], BF16)
    q2_s = sb("q2_s", [128, 512], BF16)
    stage_s = [sb(f"stage{i}", [128, 1024], F32) for i in range(4)]
    u1_s = sb("u1_s", [128, 2048], BF16)
    v2_s = sb("v2_s", [128, 2048], BF16)

    wtkv_s = sb("wtkv_s", [128, 16384], BF16, side="right")
    xqr2_s = sb("xqr2_s", [128, 8192], BF16, side="right")
    uacc_s = sb("uacc_s", [128, 8192], BF16, side="right")
    kfm2_s = [sb(f"kfm2_{i}", [128, 4096], BF16, side="right") for i in range(2)]
    vtm2_s = [sb(f"vtm2_{i}", [128, 4096], BF16, side="right") for i in range(2)]


    def ld_sw(dst, src, waits=None, dsem="miscsw", gate=False):
        return sch.gp_dma(lambda g, d=dst, s=src: g.dma_start(out=d, in_=s),
                          sem=dsem, waits=waits, gate=gate)

    def ld_hw(dst, src, waits=None, sem="hw", gate=False):
        return sch.sync_dma(lambda s_, d=dst, s=src: s_.dma_start(out=d, in_=s),
                            waits=waits, sem=sem, gate=gate)

    def ld_hw_act(dst, src, waits=None, sem="hw", gate=False):
        return sch.act_dma(lambda a_, d=dst, s=src: a_.dma_start(out=d, in_=s),
                           waits=waits, sem=sem, gate=gate)

    NA = 4 + 4 * NSMAC
    with nc.sbuf_tensor("tabhi", [128, 536], F32) as tabhi_s, \
         nc.sbuf_tensor("tablo", [128, 536], F32) as tablo_s:
        th2 = ld_hw(tabhi_s[:, 0:NA], achi_h[:], sem="tabs")
        tl2 = ld_hw(tablo_s[:, 0:NA], aclo_h[:], sem="tabs")
        t_acol = sch.dve(lambda v: v.tensor_tensor(
            out=acol_s[:], in0=tabhi_s[:, 0:NA], in1=tablo_s[:, 0:NA],
            op=ALU.mult), waits=merge(th2, tl2))
        th = ld_hw(tabhi_s[:, NA:NA + 512], aQhi_h[:], sem="tabs", gate=True)
        tl = ld_hw(tablo_s[:, NA:NA + 512], aQlo_h[:], sem="tabs")
        t_aQ = sch.dve(lambda v: v.tensor_tensor(
            out=aQrep_s[:], in0=tabhi_s[:, NA:NA + 512],
            in1=tablo_s[:, NA:NA + 512], op=ALU.mult), waits=merge(th, tl))
    t_tab = merge(t_acol, t_aQ)
    t_hw_misc = merge(ld_hw(bqcol_s[:], bqcol_h[:]),
                      ld_hw(b1c_s[:], b1c_h[:]),
                      ld_hw(b2col_s[:], b2col_h[:]),
                      ld_hw(vmask_s[:], vmask_h[:]),
                      ld_hw(master_s[:, 0:704], master_h[:, 0:704]),
                      ld_hw(master_s[:, 704:1408], master_h[:, 704:1408]))

    acolb_s = sb("acolb_s", [128, 4 + 4 * NSMAC], BF16)
    s0_s = sb("s0_s", [1, 1], F32)
    t_acolb = sch.dve(lambda v: v.tensor_copy(acolb_s[:], acol_s[:]),
                      waits=t_acol)
    with nc.sbuf_tensor("xhtm", [128, 20480], BF16, side="right") as xhtm_s, \
         nc.sbuf_tensor("sxx", [128, 8192], BF16, side="right") as sxx_s, \
         nc.sbuf_tensor("hx5", [128, 4096], BF16, side="right") as hx5_s, \
         nc.sbuf_tensor("hrow", [1, 1024], BF16, side="right") as hrow_s, \
         nc.sbuf_tensor("hcol", [128, 8], BF16, side="right") as hcol_s, \
         nc.sbuf_tensor("wkr", [1, 1024], BF16, side="right") as wkr_s:
        xh_cuts = [0, 1024, 6144, 11264, 16384, 20480]
        t_xh = [ld_sw(xhtm_s[:, xh_cuts[i]:xh_cuts[i + 1]],
                      xh_h[:, xh_cuts[i]:xh_cuts[i + 1]], dsem=f"xh{i}")
                for i in range(5)]
        xh_piece = [0] + [1] * 5 + [2] * 5 + [3] * 5 + [4] * 4
        t_ones = sch.gp(lambda g: g.memset(ones_s[:], 1.0))
        t_onescol = sch.gp(lambda g: g.memset(onescol_s[:], 1.0))
        for i in range(2):
            ld_sw(wtkv_s[:, 4096 * i:4096 * (i + 1)],
                  wtkv_h[:, 4096 * i:4096 * (i + 1)], dsem="wtk")
        t_wtk = ("wtk", sch.cnt["wtk"])
        ld_sw(brow_s[:], brow_h[:])
        _t_ident = ld_sw(ident_s[:], ident_h[:], gate=True)
        t_sw_misc1 = ("miscsw", sch.cnt["miscsw"])
        t_consts = merge(t_sw_misc1, t_hw_misc, t_ones, t_onescol, t_tab)
        for i in range(2, 4):
            ld_sw(wtkv_s[:, 4096 * i:4096 * (i + 1)],
                  wtkv_h[:, 4096 * i:4096 * (i + 1)], dsem="wtv")
        t_wtkv = ("wtv", sch.cnt["wtv"])
        t_hx5 = ld_sw(hx5_s[:], xh_h[:, 20480:24576], dsem="xh1", gate=True)
        xq_tok = {}
        for rr in range(2):
            xq_tok[rr] = ld_sw(xqr2_s[:, 4096 * rr:4096 * (rr + 1)],
                               xq_h[:, 4096 * rr:4096 * (rr + 1)],
                               dsem="xqs", gate=True)
        sc_tok = []
        for c in range(20):
            sc_tok.append(sch.act(lambda a, c=c: a.activation(
                out=xhtm_s[:, 1024 * c:1024 * (c + 1)],
                in_=xhtm_s[:, 1024 * c:1024 * (c + 1)], func=AF.Identity,
                scale=acol_s[:, 4 + c:4 + c + 1]),
                waits=merge(t_xh[xh_piece[c]], t_acol)))
        t_xs = sc_tok[-1]
        sxx_tok = {}
        all_gk = [(a, db) for a in range(8) for db in range(2)
                  if not (db == 0 and a >= 4)]
        for p in range(2):
            gks = all_gk[6 * p:6 * (p + 1)]
            bks = {g: acquire_bank() for g in gks}
            mmt = {}
            for c in range(20):
                for a, db in gks:
                    bi = bks[(a, db)]
                    w = merge(sc_tok[c],
                              bank_free[bi] if c == 0 else None)
                    mmt[(a, db)] = sch.pe(
                        lambda t, a=a, db=db, c=c, bi=bi: t.matmul(
                            pb[bi][:],
                            xhtm_s[:, 1024 * c + 128 * a:1024 * c + 128 * (a + 1)],
                            xhtm_s[:, 1024 * c + 512 * db:1024 * c + 512 * (db + 1)],
                            start=(c == 0), stop=(c == 19)),
                        waits=w, inc=(c == 19))
            for a, db in gks:
                bi = bks[(a, db)]
                bank_open[bi] = False
                dst = sxx_s[:, 1024 * a + 512 * db:1024 * a + 512 * (db + 1)]
                ev = sch.act(lambda ac, bi=bi, dst=dst: ac.activation(
                    out=dst, in_=pb[bi][:], func=AF.Identity),
                    waits=mmt[(a, db)])
                sxx_tok[(a, db)] = ev
                bank_free[bi] = ev
        for a in range(4, 8):
            pt = a % 2
            ptok = None
            for j in range(4):
                src = sxx_s[:, 1024 * j + 512 + 128 * (a - 4):
                            1024 * j + 512 + 128 * (a - 3)]
                ptok = sch.pe(lambda t, src=src, j=j, pt=pt: t.transpose(
                    ptr[pt][:, 128 * j:128 * (j + 1)], src, ident_s[:]),
                    waits=merge(sxx_tok[(j, 1)], t_consts, _t_ident,
                                ptr_free[pt] if j == 0 else None),
                    inc=(j == 3))
            ev = sch.act(lambda ac, a=a, pt=pt: ac.activation(
                out=sxx_s[:, 1024 * a:1024 * a + 512], in_=ptr[pt][:, 0:512],
                func=AF.Identity), waits=ptok)
            ptr_free[pt] = ev
            sxx_tok[(a, 0)] = ev
        sxx_ready = merge(*sxx_tok.values())
        hr_tok = []
        for db in range(2):
            tiles = [(acolb_s[:, 4 + c:4 + c + 1],
                      xhtm_s[:, 1024 * c + 512 * db:1024 * c + 512 * (db + 1)])
                     for c in range(20)]
            bi, tok = mm_group(tiles, merge(t_xs, t_acolb), mrows=1)
            ev = sch.dve(lambda v, bi=bi, db=db: v.tensor_copy(
                hrow_s[0:1, 512 * db:512 * (db + 1)], pb[bi][0:1, :]), waits=tok)
            hr_tok.append(ev)
            bank_free[bi] = ev
        tiles = [(acolb_s[:, 4 + c:4 + c + 1], acolb_s[:, 4 + c:4 + c + 1])
                 for c in range(20)]
        bi, tok = mm_group(tiles, t_acolb, nf=1, mrows=1)
        t_s0 = sch.dve(lambda v, bi=bi: v.tensor_copy(
            s0_s[0:1, 0:1], pb[bi][0:1, 0:1]), waits=tok)
        bank_free[bi] = t_s0
        t_hrow = merge(*hr_tok)
        halo_gate = merge({"pe": sch.cnt["pe"]},
                          *[(f"xh{i}", sch.cnt[f"xh{i}"]) for i in range(5)])
        hc_tok = []
        for c in range(8):
            bi, tok = mm_group([(hrow_s[0:1, 128 * c:128 * (c + 1)],
                                 ones_s[0:1, 0:1])], merge(t_hrow, t_ones), nf=1)
            ev = sch.dve(lambda v, bi=bi, c=c: v.tensor_copy(
                hcol_s[:, c:c + 1], pb[bi][0:128, 0:1]), waits=tok)
            hc_tok.append(ev)
            bank_free[bi] = ev
        t_hcol = merge(*hc_tok)
        wk_tok = []
        for db in range(2):
            tiles = [(hcol_s[:, c:c + 1],
                      wtkv_s[:, 1024 * c + 512 * db:1024 * c + 512 * (db + 1)])
                     for c in range(8)]
            bi, tok = mm_group(tiles, merge(t_hcol, t_wtk), mrows=1)
            ev = sch.dve(lambda v, bi=bi, db=db: v.tensor_copy(
                wkr_s[0:1, 512 * db:512 * (db + 1)], pb[bi][0:1, :]), waits=tok)
            wk_tok.append(ev)
            bank_free[bi] = ev
        t_wkr = sch.dve(lambda v: v.scalar_tensor_tensor(
            out=wkr_s[0:1, :], in0=brow_s[0:1, 0:1024], scalar=s0_s[0:1, 0:1],
            in1=wkr_s[0:1, :], op0=ALU.mult, op1=ALU.add),
            waits=merge(t_s0, t_sw_misc1, *wk_tok))

        init_tok = {}
        last5 = None
        for ot in range(8):
            tiles = [(wtkv_s[:, 1024 * c + 128 * ot:1024 * c + 128 * (ot + 1)],
                      hx5_s[:, 512 * c:512 * (c + 1)]) for c in range(8)]
            bi, tok = mm_group(tiles, merge(t_hx5, t_wtk))
            last5 = tok
            dst = kfm2_s[1][:, 512 * ot:512 * (ot + 1)]
            ev = sch.act(lambda a, bi=bi, dst=dst, ot=ot: a.activation(
                out=dst, in_=pb[bi][:], func=AF.Identity,
                bias=bqcol_s[:, 8 + ot:8 + ot + 1]), waits=merge(tok, t_hw_misc))
            init_tok[("k", ot)] = ev
            bank_free[bi] = ev
        for tt in range(4):
            for db in range(2):
                tiles = [(hx5_s[:, 512 * c + 128 * tt:512 * c + 128 * (tt + 1)],
                          wtkv_s[:, 8192 + 1024 * c + 512 * db:8192 + 1024 * c + 512 * (db + 1)])
                         for c in range(8)]
                tiles.append((ones_s[0:1, 0:128],
                              brow_s[0:1, 1024 + 512 * db:1024 + 512 * (db + 1)]))
                bi, tok = mm_group(tiles, merge(t_hx5, t_wtkv, t_ones, t_sw_misc1))
                last5 = tok
                dst = vtm2_s[1][:, 1024 * tt + 512 * db:1024 * tt + 512 * (db + 1)]
                ev = sch.act(lambda a, bi=bi, dst=dst: a.activation(
                    out=dst, in_=pb[bi][:], func=AF.Identity,
                    scale=vmask_s[:, 0:1]), waits=merge(tok, t_hw_misc))
                init_tok[("v", tt, db)] = ev
                bank_free[bi] = ev
        init_ready = merge(*init_tok.values())

        m1_half = [kfm2_s[0], vtm2_s[0]]
        m1_tok = {}
        for ap2 in range(8):
            for db in range(2):
                tiles = [(sxx_s[:, 1024 * c + 128 * ap2:1024 * c + 128 * (ap2 + 1)],
                          wtkv_s[:, 1024 * c + 512 * db:1024 * c + 512 * (db + 1)])
                         for c in range(8)]
                tiles.append((hrow_s[0:1, 128 * ap2:128 * (ap2 + 1)],
                              brow_s[0:1, 512 * db:512 * (db + 1)]))
                bi, tok = mm_group(tiles, merge(sxx_ready, t_hrow, t_wtk,
                                                t_sw_misc1))
                dst = m1_half[db][:, 512 * ap2:512 * (ap2 + 1)]
                ev = sch.act(lambda a, bi=bi, dst=dst: a.activation(
                    out=dst, in_=pb[bi][:], func=AF.Identity), waits=tok)
                m1_tok[(ap2, db)] = ev
                bank_free[bi] = ev
        m1_ready = merge(*m1_tok.values())
        uacc_tok = {}
        m1_cons = None
        for a in range(8):
            dbk, ak = a // 4, a % 4
            for db in range(2):
                tiles = [(m1_half[dbk][:, 512 * c + 128 * ak:512 * c + 128 * (ak + 1)],
                          wtkv_s[:, 8192 + 1024 * c + 512 * db:8192 + 1024 * c + 512 * (db + 1)])
                         for c in range(8)]
                tiles.append((wkr_s[0:1, 128 * a:128 * (a + 1)],
                              brow_s[0:1, 1024 + 512 * db:1024 + 512 * (db + 1)]))
                bi, tok = mm_group(tiles, merge(m1_ready, t_wkr, t_wtkv))
                m1_cons = merge(m1_cons, tok)
                dst = uacc_s[:, 1024 * a + 512 * db:1024 * a + 512 * (db + 1)]
                ev = sch.act(lambda ac, bi=bi, dst=dst: ac.activation(
                    out=dst, in_=pb[bi][:], func=AF.Identity), waits=tok)
                uacc_tok[(a, db)] = ev
                bank_free[bi] = ev
        uacc_ready = merge(*uacc_tok.values())
        kfm_cons0_init = m1_cons
    t_consts = merge(t_consts, _t_ident, ("miscsw", sch.cnt["miscsw"]))
    t_u1 = ld_sw(u1_s[:], u1_h[:], gate=True)
    t_v2 = ld_sw(v2_s[:], v2_h[:], gate=True)
    t_consts = merge(t_consts, t_u1, t_v2)
    bi_us, tok_us = mm_group([(onescol_s[:], u1_s[:, 256 * c:256 * (c + 1)])
                              for c in range(8)],
                             merge(t_u1, t_ones, t_onescol), nf=256, mrows=1)
    t_nu1 = sch.act(lambda a, bi=bi_us: a.activation(
        out=nu1s_s[0:1, :], in_=pb[bi][0:1, 0:256], func=AF.Identity,
        scale=f32(-1.0 / 1024.0)), waits=tok_us)
    bank_free[bi_us] = t_nu1

    ln_cons = [None]
    xq_cons = [None, None]
    wb_tok = {}
    p_cons = [None]
    q2_cons = [None]
    h_cons = [None]
    stage_cons = [None] * 4
    xsc_cons = [None, None]
    kfm_tokens = [None, init_ready]
    vtm_tokens = [None, init_ready]
    kfm_cons = [kfm_cons0_init, None]
    vtm_cons = [kfm_cons0_init, None]
    u_ready = [uacc_ready]

    def ln_stats(src_s, src_ready, bw, reps=True, cols=False, mid=None,
                 chunk_tok=None):
        guard = ln_cons[0]
        ctok = chunk_tok or {}
        banks = {}
        bsum = acquire_bank()
        tok = None
        for c in range(8):
            w = merge(ctok.get(c, src_ready), t_consts,
                      bank_free[bsum] if c == 0 else None)
            tok = sch.pe(lambda t, c=c, bi=bsum, ss=src_s: t.matmul(
                pb[bi][0:1, 0:512], onescol_s[:], ss[:, 512 * c:512 * (c + 1)],
                start=(c == 0), stop=(c == 7)), waits=w, inc=(c == 7))
        bank_open[bsum] = False
        banks[0] = (bsum, tok)
        sq_tok = {}
        mmsq_tok = {}
        bsq = acquire_bank()
        for c in range(8):
            dst = ysq_s[c % 3]
            sq_tok[c] = sch.act(
                lambda a, c=c, dst=dst, ss=src_s: a.activation(
                    out=dst[:], in_=ss[:, 512 * c:512 * (c + 1)], func=AF.Square),
                waits=merge(ctok.get(c, src_ready), guard, mmsq_tok.get(c - 3)))
            w = merge(sq_tok[c], bank_free[bsq] if c == 0 else None, t_consts)
            mmsq_tok[c] = sch.pe(
                lambda t, c=c, bi=bsq: t.matmul(
                    pb[bi][0:1, 0:512], onescol_s[:], ysq_s[c % 3][:],
                    start=(c == 0), stop=(c == 7)),
                waits=w, inc=True)
        bank_open[bsq] = False
        banks[1] = (bsq, mmsq_tok[7])
        (bsumi, toks), (bsqi, tokq) = banks[0], banks[1]
        evA = sch.dve(lambda v, bi=bsumi: v.tensor_copy(
            rowA_s[0:1, :], pb[bi][0:1, :]), waits=merge(toks, guard))
        bank_free[bsumi] = evA
        evB = sch.dve(lambda v, bi=bsqi: v.tensor_copy(
            rowB_s[0:1, :], pb[bi][0:1, :]), waits=merge(tokq, guard))
        bank_free[bsqi] = evB
        bcol = acquire_bank()
        ctok = None
        for j in range(8):
            row, rtok = (rowA_s, evA) if j < 4 else (rowB_s, evB)
            jj = j % 4
            w = merge(rtok, t_ones, bank_free[bcol] if j == 0 else None)
            ctok = sch.pe(lambda t, j=j, row=row, jj=jj, bi=bcol: t.matmul(
                pb[bi][0:128, j:j + 1], row[0:1, 128 * jj:128 * (jj + 1)],
                ones_s[0:1, 0:1], start=True, stop=True), waits=w,
                inc=(j == 7))
        bank_open[bcol] = False
        evC = sch.dve(lambda v, bi=bcol: v.tensor_copy(
            colsq_s[:], pb[bi][0:128, 0:8]), waits=merge(ctok, guard))
        bank_free[bcol] = evC
        t1c = sch.dve(lambda v: v.tensor_scalar(
            out=mcq_s[:], in0=colsq_s[:, 0:4], scalar1=float(1.0 / 1024),
            scalar2=None, op0=ALU.mult), waits=evC)
        t2c = sch.dve(lambda v: v.tensor_tensor(
            out=cscq_s[:], in0=mcq_s[:], in1=mcq_s[:], op=ALU.mult), waits=t1c)
        t3c = sch.dve(lambda v: v.scalar_tensor_tensor(
            out=vcq_s[:], in0=colsq_s[:, 4:8], scalar=float(1.0 / 1024),
            in1=cscq_s[:], op0=ALU.mult, op1=ALU.subtract),
            waits=merge(t2c, evC))
        t4c = sch.act(lambda a: a.activation(
            out=cscq_s[:], in_=vcq_s[:], func=AF.Sqrt, bias=f32(1e-5)),
            waits=t3c)
        t5c = sch.dve(lambda v: v.reciprocal(out=vcq_s[:], in_=cscq_s[:]),
                      waits=t4c)
        cb1 = sch.dve(lambda v: v.tensor_copy(colsb_s[:, 0:4], mcq_s[:]),
                      waits=merge(t1c, guard))
        cb2 = sch.dve(lambda v: v.tensor_copy(colsb_s[:, 4:8], vcq_s[:]),
                      waits=merge(t5c, guard))
        out = {"t1": evA, "sq": merge(*sq_tok.values())}
        if reps:
            brow1 = acquire_bank()
            brow2 = acquire_bank()
            rtok1 = rtok2 = None
            for j in range(8):
                bi = brow1 if j < 4 else brow2
                w = merge(cb1 if j < 4 else cb2, t_consts,
                          bank_free[bi] if j % 4 == 0 else None)
                tok = sch.pe(lambda t, j=j, bi=bi: t.matmul(
                    pb[bi][0:1, 128 * (j % 4):128 * (j % 4 + 1)],
                    colsb_s[:, j:j + 1], ident_s[:, 0:128],
                    start=True, stop=True), waits=w, inc=(j % 4 == 3))
                if j == 3:
                    rtok1 = tok
                elif j == 7:
                    rtok2 = tok
            bank_open[brow1] = False
            bank_open[brow2] = False
            evD = sch.dve(lambda v, bi=brow1: v.tensor_copy(
                rowD_s[0:1, :], pb[bi][0:1, 0:512]), waits=merge(rtok1, guard))
            evE = sch.dve(lambda v, bi=brow2: v.tensor_copy(
                rowB_s[0:1, :], pb[bi][0:1, 0:512]), waits=merge(rtok2, guard))
            bank_free[brow1] = evD
            bank_free[brow2] = evE
            bi6, tok6 = mm_group([(ones_s[0:1, 0:128], rowD_s[0:1, :])],
                                 merge(evD, guard, t_consts), nf=512)
            t6 = sch.act(lambda a, bi=bi6: a.activation(
                out=mrep_s[:], in_=pb[bi][:], func=AF.Identity),
                waits=merge(tok6, guard))
            bank_free[bi6] = t6
            bi7, tok7 = mm_group([(ones_s[0:1, 0:128], rowB_s[0:1, :])],
                                 merge(evE, guard), nf=512)
            t7 = sch.act(lambda a, bi=bi7: a.activation(
                out=rrep_s[:], in_=pb[bi][:], func=AF.Identity),
                waits=merge(tok7, guard))
            bank_free[bi7] = t7
            out["reps"] = merge(t6, t7)
        if mid is not None:
            mid()
        return out

    for r in range(NR):
        par = r % 2
        prv = 1 - par
        def bw(eng):
            return None

        with nc.sbuf_tensor(f"y1_{r}", [128, 4096], BF16) as y1_s:
            with nc.sbuf_tensor(f"yraw{r}", [128, 4096], BF16, side="right") as yraw_s:
              with nc.sbuf_tensor(f"wtq{r}", [128, 8192], BF16, side="right") as wtq_s, \
                   nc.sbuf_tensor(f"ktms{r}", [128, 4096], BF16, side="right") as ktms_s:
                gw = merge(p_cons[0], q2_cons[0], h_cons[0],
                           halo_gate if r == 0 else None)
                xqr_s = xqr2_s[:, 4096 * par:4096 * (par + 1)]
                t_xq = xq_tok[r]
                if r < 3:
                    xq_tok[r + 1] = ld_sw(
                        xqr2_s[:, 4096 * ((r + 1) % 2):4096 * ((r + 1) % 2 + 1)],
                        xq_h[:, 4096 * (r + 1):4096 * (r + 2)],
                        waits=xq_cons[(r + 1) % 2], dsem="xqs", gate=True)
                if r == 0:
                    for i in range(2):
                        ld_sw(wtq_s[:, 4096 * i:4096 * (i + 1)],
                              wtq_h[:, 4096 * i:4096 * (i + 1)],
                              waits=gw if i == 0 else None, dsem="wtq",
                              gate=(i == 0))
                    t_wtq = ("wtq", sch.cnt["wtq"])
                    wb_tok["wtq"] = ld_hw(wtqb_d, wtq_s[:], waits=t_wtq,
                                          sem="wbq", gate=True)
                else:
                    t_wtq = ld_hw_act(wtq_s[:], wtqb_d,
                                      waits=merge(p_cons[0],
                                                  wb_tok["v1"] if r == 1 else None),
                                      sem="wbq", gate=True)

                with nc.sbuf_tensor(f"abuf{r}", [128, 4096], BF16) as abuf_s, \
                     nc.sbuf_tensor(f"qfm{r}", [128, 4096], BF16) as qfm_s:
                    qfm_tok, kfm_tok, vtm_tok = {}, {}, {}
                    for ot in range(8):
                        tiles = [(wtkv_s[:, 1024 * c + 128 * ot:1024 * c + 128 * (ot + 1)],
                                  xqr_s[:, 512 * c:512 * (c + 1)]) for c in range(8)]
                        bi, tok = mm_group(tiles, merge(t_xq, t_wtk, kfm_cons[par]))
                        dst = kfm2_s[par][:, 512 * ot:512 * (ot + 1)]
                        ev = sch.act(lambda a, bi=bi, dst=dst, ot=ot, par=par:
                                     a.activation(
                                         out=dst, in_=pb[bi][:], func=AF.Identity,
                                         bias=bqcol_s[:, 8 + ot:8 + ot + 1]),
                                     waits=tok)
                        kfm_tok[ot] = ev
                        bank_free[bi] = ev
                    for tt in range(4):
                        for db in range(2):
                            tiles = [(xqr_s[:, 512 * c + 128 * tt:512 * c + 128 * (tt + 1)],
                                      wtkv_s[:, 8192 + 1024 * c + 512 * db:8192 + 1024 * c + 512 * (db + 1)])
                                     for c in range(8)]
                            tiles.append((ones_s[0:1, 0:128],
                                          brow_s[0:1, 1024 + 512 * db:1024 + 512 * (db + 1)]))
                            bi, tok = mm_group(tiles, merge(t_xq, t_wtkv, vtm_cons[par]))
                            dst = vtm2_s[par][:, 1024 * tt + 512 * db:1024 * tt + 512 * (db + 1)]
                            ev = sch.act(lambda a, bi=bi, dst=dst: a.activation(
                                out=dst, in_=pb[bi][:], func=AF.Identity),
                                waits=tok)
                            vtm_tok[(tt, db)] = ev
                            bank_free[bi] = ev
                    for ot in range(8):
                        tiles = [(wtq_s[:, 1024 * c + 128 * ot:1024 * c + 128 * (ot + 1)],
                                  xqr_s[:, 512 * c:512 * (c + 1)]) for c in range(8)]
                        bi, tok = mm_group(tiles, merge(t_xq, t_wtq, t_consts))
                        dst = qfm_s[:, 512 * ot:512 * (ot + 1)]
                        ev = sch.act(lambda a, bi=bi, dst=dst, ot=ot: a.activation(
                            out=dst, in_=pb[bi][:], func=AF.Identity,
                            bias=bqcol_s[:, ot:ot + 1]),
                            waits=merge(tok, bw("act")))
                        qfm_tok[ot] = ev
                        bank_free[bi] = ev
                    qfm_ready = merge(*qfm_tok.values())
                    kfm_ready = merge(*kfm_tok.values())
                    vtm_ready = merge(*vtm_tok.values())
                    kfm_tokens[par] = kfm_ready
                    vtm_tokens[par] = vtm_ready
                    t_s1_pe = {"pe": sch.cnt["pe"]}

                    a_tok = {}
                    for jt in range(8):
                        src = kfm2_s[prv] if jt < 4 else kfm2_s[par]
                        src_tok = kfm_tokens[prv] if jt < 4 else kfm_ready
                        jj = jt % 4
                        tiles = [(src[:, 512 * c + 128 * jj:512 * c + 128 * (jj + 1)],
                                  qfm_s[:, 512 * c:512 * (c + 1)]) for c in range(8)]
                        bi, tok = mm_group(tiles, merge(src_tok, qfm_ready))
                        if jt < 4:
                            kfm_cons[prv] = merge(kfm_cons[prv], tok)
                        dst = abuf_s[:, 512 * jt:512 * (jt + 1)]
                        c0 = 896 - 128 * jt
                        ev = sch.dve(lambda v, bi=bi, dst=dst, c0=c0:
                                     v.tensor_tensor(out=dst, in0=pb[bi][:],
                                                     in1=master_s[:, c0:c0 + 512],
                                                     op=ALU.mult), waits=tok)
                        a_tok[jt] = ev
                        bank_free[bi] = ev
                    a_ready = merge(*a_tok.values())

                    yraw_tok = {}
                    last_cross_pe = None
                    for dv in range(8):
                        tiles = []
                        for jt in range(8):
                            vsrc = vtm2_s[prv] if jt < 4 else vtm2_s[par]
                            jj = jt % 4
                            tiles.append((vsrc[:, 1024 * jj + 128 * dv:1024 * jj + 128 * (dv + 1)],
                                          abuf_s[:, 512 * jt:512 * (jt + 1)]))
                        biA, tokA = mm_group(tiles, merge(a_ready, vtm_tokens[prv],
                                                          vtm_ready))
                        vtm_cons[prv] = merge(vtm_cons[prv], tokA)
                        tiles = [(uacc_s[:, 1024 * a + 128 * dv:1024 * a + 128 * (dv + 1)],
                                  qfm_s[:, 512 * a:512 * (a + 1)]) for a in range(8)]
                        biB, tokB = mm_group(tiles, merge(u_ready[0], qfm_ready))
                        last_cross_pe = merge(last_cross_pe, tokB)
                        xb = dv % 2
                        s1 = sch.dve(lambda v, biB=biB, xb=xb: v.tensor_tensor(
                            out=xsc_s[xb][:], in0=pb[biB][:], in1=aQrep_s[:],
                            op=ALU.mult), waits=merge(tokB, xsc_cons[xb]))
                        bank_free[biB] = s1
                        s2 = sch.dve(lambda v, xq=xqr_s, dv=dv, xb=xb: v.tensor_tensor(
                            out=xsc_s[xb][:], in0=xsc_s[xb][:],
                            in1=xq[:, 512 * dv:512 * (dv + 1)], op=ALU.add),
                            waits=merge(s1, t_xq))
                        dst = yraw_s[:, 512 * dv:512 * (dv + 1)]
                        s3 = sch.dve(lambda v, biA=biA, dst=dst, xb=xb: v.tensor_tensor(
                            out=dst, in0=pb[biA][:], in1=xsc_s[xb][:], op=ALU.add),
                            waits=merge(tokA, s2))
                        yraw_tok[dv] = s3
                        xsc_cons[xb] = s3
                        bank_free[biA] = s3
                    yraw_ready = merge(*yraw_tok.values())

                    def s2_emit(prv=prv, lcp=None):
                        ktm_tok = {}
                        for tt in range(4):
                            for g in range(2):
                                pt = (2 * tt + g) % 2
                                tok = None
                                for j in range(4):
                                    dc = 4 * g + j
                                    tok = sch.pe(lambda t, pt=pt, dc=dc, tt=tt, j=j, prv=prv:
                                                 t.transpose(
                                                     ptr[pt][:, 128 * j:128 * (j + 1)],
                                                     kfm2_s[prv][:, 512 * dc + 128 * tt:512 * dc + 128 * (tt + 1)],
                                                     ident_s[:]),
                                                 waits=merge(kfm_tokens[prv],
                                                             ptr_free[pt] if j == 0 else None,
                                                             t_consts), inc=(j == 3))
                                kfm_cons[prv] = merge(kfm_cons[prv], tok)
                                dst = ktms_s[:, 1024 * tt + 512 * g:1024 * tt + 512 * (g + 1)]
                                if g == 0:
                                    ev = sch.act(lambda a, pt=pt, dst=dst, tt=tt:
                                                 a.activation(
                                                     out=dst, in_=ptr[pt][:, 0:512],
                                                     func=AF.Identity,
                                                     scale=acol_s[:, tt:tt + 1]),
                                                 waits=merge(tok, wb_tok.get("u2") if r == 1 else None))
                                else:
                                    ev = sch.dve(lambda v, pt=pt, dst=dst, tt=tt:
                                                 v.tensor_scalar(
                                                     out=dst, in0=ptr[pt][:, 0:512],
                                                     scalar1=acol_s[:, tt:tt + 1],
                                                     scalar2=None, op0=ALU.mult),
                                                 waits=merge(tok, wb_tok.get("u2") if r == 1 else None))
                                ptr_free[pt] = ev
                                ktm_tok[(tt, g)] = ev
                        ktm_ready = merge(*ktm_tok.values())
                        u2_tok = {}
                        for a in range(8):
                            for db in range(2):
                                tiles = [(ktms_s[:, 1024 * tt + 128 * a:1024 * tt + 128 * (a + 1)],
                                          vtm2_s[prv][:, 1024 * tt + 512 * db:1024 * tt + 512 * (db + 1)])
                                         for tt in range(4)]
                                bi, tok = mm_group(tiles, merge(ktm_ready, vtm_tokens[prv],
                                                                lcp))
                                vtm_cons[prv] = merge(vtm_cons[prv], tok)
                                dst = uacc_s[:, 1024 * a + 512 * db:1024 * a + 512 * (db + 1)]
                                ev = sch.dve(lambda v, bi=bi, dst=dst: v.tensor_tensor(
                                    out=dst, in0=pb[bi][:], in1=dst, op=ALU.add),
                                    waits=merge(tok, u_ready[0], lcp))
                                u2_tok[(a, db)] = ev
                                bank_free[bi] = ev
                        _chi = float(np.float32(AEFF ** W))
                        _clo = float(np.float32(AEFF ** W / np.float64(np.float32(AEFF ** W))))
                        t_us = sch.dve(lambda v: v.tensor_scalar(
                            out=uacc_s[:], in0=uacc_s[:], scalar1=_chi,
                            scalar2=_clo, op0=ALU.mult, op1=ALU.mult),
                            waits=merge(*u2_tok.values()))
                        u_ready[0] = t_us

                    pbis = {}

                    def emit_px():
                        for tb in range(2):
                            for rt in range(2):
                                bi = acquire_bank()
                                pbis[(tb, rt)] = bi
                                for i in range(8):
                                    w = merge(yraw_ready, t_consts,
                                              bank_free[bi]) if i == 0 else None
                                    sch.pe(lambda t, bi=bi, rt=rt, tb=tb, i=i, ys=yraw_s:
                                           t.matmul(
                                               pb[bi][:, 0:256],
                                               u1_s[:, 256 * i + 128 * rt:256 * i + 128 * (rt + 1)],
                                               ys[:, 512 * i + 256 * tb:512 * i + 256 * (tb + 1)],
                                               start=(i == 0), stop=False), waits=w)

                    def mid_all(prv=prv, lcp=last_cross_pe, do_s2=(r < NR - 1)):
                        if do_s2:
                            s2_emit(prv, lcp)
                        emit_px()

                    ln1 = ln_stats(yraw_s, yraw_ready, bw, reps=True, mid=mid_all,
                                   chunk_tok=yraw_tok)
                    reps_tok, sq_done = ln1["reps"], ln1["sq"]
                    w_s3done = {"pe": sch.cnt["pe"], "dve": sch.cnt["dve"]}
                    xq_cons[par] = w_s3done

              with nc.sbuf_tensor(f"v1s{r}", [128, 8192], BF16, side="right") as v1_s, \
                   nc.sbuf_tensor(f"u2s{r}", [128, 8192], BF16, side="right") as u2_s, \
                   nc.sbuf_tensor(f"h{r}", [128, 8192], BF16, side="right") as h_s, \
                   nc.sbuf_tensor(f"out2_{r}", [128, 4096], BF16, side="right") as out2_s:
                  ww2 = {"pe": sch.cnt["pe"], "dve": sch.cnt["dve"],
                         "act": sch.cnt["act"]}
                  if r == 0:
                      ld_sw(v1_s[:, 0:4096], v1_h[:, 0:4096],
                            waits=merge(t_s1_pe, wb_tok["wtq"]),
                            dsem="ffnw", gate=True)
                      ld_sw(v1_s[:, 4096:8192], v1_h[:, 4096:8192], dsem="ffnw")
                      t_v1 = ("ffnw", sch.cnt["ffnw"])
                      wb_tok["v1"] = ld_hw(v1b_d, v1_s[:], waits=t_v1,
                                           sem="wbv", gate=True)
                      ld_sw(u2_s[:, 0:4096], u2_h[:, 0:4096], waits=ww2,
                            dsem="ffnw", gate=True)
                      ld_sw(u2_s[:, 4096:8192], u2_h[:, 4096:8192], dsem="ffnw")
                      t_ffnw = ("ffnw", sch.cnt["ffnw"])
                      wb_tok["u2"] = ld_hw(u2b_d, u2_s[:], waits=t_ffnw,
                                           sem="wbu", gate=True)
                  else:
                      t_v1 = ld_hw_act(v1_s[:], v1b_d,
                                       waits=merge(t_s1_pe, wb_tok["v1"]),
                                       sem="wbv", gate=True)
                      t_ffnw = ld_hw_act(u2_s[:], u2b_d,
                                         waits=merge(ww2, wb_tok["u2"]),
                                         sem="wbu", gate=True)
                  out2_tok = {}
                  for tb in range(2):
                      p_tok = {}
                      for rt in range(2):
                          bi = pbis[(tb, rt)]
                          tok = sch.pe(lambda t, bi=bi, rt=rt, tb=tb: t.matmul(
                              pb[bi][:, 0:256],
                              nu1s_s[0:1, 128 * rt:128 * (rt + 1)],
                              rowA_s[0:1, 256 * tb:256 * (tb + 1)],
                              start=False, stop=True),
                              waits=merge(ln1["t1"], t_nu1) if (tb, rt) == (0, 0) else None,
                              inc=True)
                          bank_open[bi] = False
                          dst = p_s[:, 256 * rt:256 * (rt + 1)]
                          ev = sch.dve(lambda v, bi=bi, dst=dst, tb=tb: v.tensor_tensor(
                              out=dst, in0=pb[bi][:, 0:256],
                              in1=rrep_s[:, 256 * tb:256 * (tb + 1)], op=ALU.mult),
                              waits=merge(tok, p_cons[0], reps_tok))
                          p_tok[rt] = ev
                          bank_free[bi] = ev
                      p_ready = merge(*p_tok.values())
                      if tb == 0:
                          y1_tok = {}
                          for c in range(8):
                              dtmp = ysq_s[c % 3]
                              ta = sch.dve(lambda v, c=c, dtmp=dtmp, ys=yraw_s:
                                           v.tensor_tensor(
                                               out=dtmp[:],
                                               in0=ys[:, 512 * c:512 * (c + 1)],
                                               in1=mrep_s[:], op=ALU.subtract),
                                           waits=merge(reps_tok, sq_done,
                                                       y1_tok.get(c - 4)))
                              y1_tok[c] = sch.dve(lambda v, c=c, dtmp=dtmp, y1s=y1_s:
                                                  v.tensor_tensor(
                                                      out=y1s[:, 512 * c:512 * (c + 1)],
                                                      in0=dtmp[:], in1=rrep_s[:],
                                                      op=ALU.mult), waits=ta)
                          y1_ready = merge(*y1_tok.values())
                          ln_cons[0] = y1_ready
                      h_ev = None
                      for ft in range(32):
                          tiles = [(v1_s[:, 4096 * rc + 128 * ft:4096 * rc + 128 * (ft + 1)],
                                    p_s[:, 256 * rc:256 * (rc + 1)]) for rc in range(2)]
                          bi, tok = mm_group(tiles, merge(p_ready, h_cons[0], t_v1), nf=256)
                          p_cons[0] = merge(p_cons[0], tok)
                          dst = h_s[:, 256 * ft:256 * (ft + 1)]
                          h_ev = sch.act(lambda a, bi=bi, dst=dst, ft=ft: a.activation(
                              out=dst, in_=pb[bi][:, 0:256], func=AF.Gelu_apprx_tanh,
                              bias=b1c_s[:, ft:ft + 1]), waits=tok)
                          bank_free[bi] = h_ev
                      q2_tok = {}
                      for rt in range(2):
                          tiles = [(u2_s[:, 256 * fc + 128 * rt:256 * fc + 128 * (rt + 1)],
                                    h_s[:, 256 * fc:256 * (fc + 1)]) for fc in range(32)]
                          bi, tok = mm_group(tiles, merge(h_ev, q2_cons[0], t_ffnw), nf=256)
                          h_cons[0] = merge(h_cons[0], tok)
                          dst = q2_s[:, 256 * rt:256 * (rt + 1)]
                          ev = sch.act(lambda a, bi=bi, dst=dst: a.activation(
                              out=dst, in_=pb[bi][:, 0:256], func=AF.Identity),
                              waits=tok)
                          q2_tok[rt] = ev
                          bank_free[bi] = ev
                      q2_ready = merge(*q2_tok.values())
                      for dt in range(8):
                          tiles = [(v2_s[:, 1024 * rc + 128 * dt:1024 * rc + 128 * (dt + 1)],
                                    q2_s[:, 256 * rc:256 * (rc + 1)]) for rc in range(2)]
                          bi, tok = mm_group(tiles, q2_ready, nf=256)
                          q2_cons[0] = merge(q2_cons[0], tok)
                          dst = out2_s[:, 512 * dt + 256 * tb:512 * dt + 256 * (tb + 1)]
                          yin = y1_s[:, 512 * dt + 256 * tb:512 * dt + 256 * (tb + 1)]
                          ev = sch.dve(lambda v, bi=bi, dst=dst, yin=yin, dt=dt:
                                       v.scalar_tensor_tensor(
                                           out=dst, in0=pb[bi][:, 0:256],
                                           scalar=b2col_s[:, dt:dt + 1],
                                           in1=yin, op0=ALU.add, op1=ALU.add),
                                       waits=tok)
                          out2_tok[(tb, dt)] = ev
                          bank_free[bi] = ev
                  out2_ready = merge(*out2_tok.values())

                  s5_ev = {}
                  s5_sq_all = []
                  s5_nrm_all = []

                  def s5_tile(tt):
                      sg = tt
                      o = 4 * (tt % 2)
                      ev_all = []
                      sq_all = []
                      for g in range(2):
                          pt = g % 2
                          tok = None
                          for j in range(4):
                              dc = 4 * g + j
                              tok = sch.pe(lambda t, pt=pt, dc=dc, tt=tt, j=j, o2=out2_s:
                                           t.transpose(
                                               ptr[pt][:, 128 * j:128 * (j + 1)],
                                               o2[:, 512 * dc + 128 * tt:512 * dc + 128 * (tt + 1)],
                                               ident_s[:]),
                                           waits=merge(out2_tok[(tt // 2, dc)],
                                                       ptr_free[pt] if j == 0 else None,
                                                       t_consts),
                                           inc=(j == 3))
                          ev = sch.act(lambda a, pt=pt, g=g, sg=sg, tt=tt: a.activation(
                              out=stage_s[sg][:, 512 * g:512 * (g + 1)],
                              in_=ptr[pt][:, 0:512], func=AF.Identity,
                              accum_out=colacc_s[:, 4 * tt + g:4 * tt + g + 1]),
                              waits=merge(tok, stage_cons[sg]))
                          sq = sch.act(lambda a, pt=pt, g=g, tt=tt: a.activation(
                              out=ysq_s[g][:], in_=ptr[pt][:, 0:512], func=AF.Square,
                              accum_out=colacc_s[:, 4 * tt + 2 + g:4 * tt + 3 + g]),
                              waits=merge(tok, ln1["t1"]))
                          ptr_free[pt] = merge(ev, sq)
                          ev_all.append(ev)
                          sq_all.append(sq)
                      s5_ev[tt] = merge(*ev_all)
                      s5_sq_all.extend(sq_all)
                      ta = sch.dve(lambda v, tt=tt, o=o: v.tensor_tensor(
                          out=cw_s[:, o:o + 1], in0=colacc_s[:, 4 * tt:4 * tt + 1],
                          in1=colacc_s[:, 4 * tt + 1:4 * tt + 2], op=ALU.add),
                          waits=s5_ev[tt])
                      tb2 = sch.dve(lambda v, tt=tt, o=o: v.tensor_scalar(
                          out=mcol_s[:, tt:tt + 1], in0=cw_s[:, o:o + 1],
                          scalar1=float(-1.0 / 1024), scalar2=None, op0=ALU.mult),
                          waits=ta)
                      tc = sch.dve(lambda v, tt=tt, o=o: v.tensor_tensor(
                          out=cw_s[:, o + 1:o + 2], in0=colacc_s[:, 4 * tt + 2:4 * tt + 3],
                          in1=colacc_s[:, 4 * tt + 3:4 * tt + 4], op=ALU.add),
                          waits=merge(*sq_all))
                      td = sch.dve(lambda v, tt=tt, o=o: v.tensor_tensor(
                          out=cw_s[:, o + 2:o + 3], in0=mcol_s[:, tt:tt + 1],
                          in1=mcol_s[:, tt:tt + 1], op=ALU.mult), waits=tb2)
                      te = sch.dve(lambda v, tt=tt, o=o: v.scalar_tensor_tensor(
                          out=cw_s[:, o + 3:o + 4], in0=cw_s[:, o + 1:o + 2],
                          scalar=float(1.0 / 1024), in1=cw_s[:, o + 2:o + 3],
                          op0=ALU.mult, op1=ALU.subtract), waits=merge(tc, td))
                      tf = sch.act(lambda a, tt=tt, o=o: a.activation(
                          out=cs_s[:, tt % 2:tt % 2 + 1], in_=cw_s[:, o + 3:o + 4],
                          func=AF.Sqrt, bias=f32(1e-5)), waits=te)
                      tg = sch.dve(lambda v, tt=tt: v.reciprocal(
                          out=rcol_s[:, tt:tt + 1], in_=cs_s[:, tt % 2:tt % 2 + 1]),
                          waits=tf)
                      return merge(tb2, tg)

                  for tt in range(4):
                      sg = tt
                      cols_tok = s5_tile(tt)
                      nrm = sch.dve(lambda v, sg=sg, tt=tt: v.tensor_scalar(
                          out=stage_s[sg][:], in0=stage_s[sg][:],
                          scalar1=mcol_s[:, tt:tt + 1], scalar2=rcol_s[:, tt:tt + 1],
                          op0=ALU.add, op1=ALU.mult),
                          waits=merge(s5_ev[tt], cols_tok))
                      s5_nrm_all.append(nrm)
                      st_half = []
                      for hh in range(2):
                          st_half.append(sch.sync_dma(
                              lambda s_, sg=sg, tt=tt, r=r, hh=hh: s_.dma_start(
                                  out=y_h[512 * r + 128 * tt + 64 * hh:
                                          512 * r + 128 * tt + 64 * (hh + 1), :],
                                  in_=stage_s[sg][64 * hh:64 * (hh + 1), :]),
                              sem=f"st{sg + 4 * hh}", waits=nrm, gate=True))
                      stage_cons[sg] = merge(*st_half)
                  ln_cons[0] = merge(y1_ready, p_ready, *s5_sq_all, *s5_nrm_all)

    sch.wait_only("sync", {k: sch.cnt[k] for k in Sched.DMA_SEMS if sch.cnt[k]})

    with nc.Block() as block:
        def replay(eng_name):
            def go(engine):
                waited = {}
                for waits, fn, inc, amt in sch.ops[eng_name]:
                    for s, v in waits.items():
                        if waited.get(s, 0) < v:
                            engine.wait_ge(sem[s], v)
                            waited[s] = v
                    if fn is None:
                        continue
                    inst = fn(engine)
                    if inc is not None:
                        inst.then_inc(sem[inc], amt)
            return go
        block.tensor(replay("pe"))
        block.vector(replay("dve"))
        block.scalar(replay("act"))
        block.gpsimd(replay("gp"))
        block.sync(replay("sync"))

    ctx.close()
    return nc


_M32_CACHE = [None]


def _m32_table():
    if _M32_CACHE[0] is not None:
        return _M32_CACHE[0]
    try:
        import jax
        import jax.numpy as jnp
        with jax.default_device(jax.devices("cpu")[0]):
            m = np.asarray(0.99 ** jnp.arange(S).astype(jnp.float32),
                           dtype=np.float64)
    except Exception:
        m = (AEFF ** np.arange(S, dtype=np.float64)).astype(np.float32).astype(np.float64)
    _M32_CACHE[0] = m
    return m


def _hilo(t64):
    hi = t64.astype(np.float32)
    with np.errstate(divide="ignore", invalid="ignore"):
        lo = np.where(hi != 0, t64 / hi.astype(np.float64), 0.0).astype(np.float32)
    return np.ascontiguousarray(hi), np.ascontiguousarray(lo)


def _fm(a):
    T = a.shape[0]
    return np.ascontiguousarray(
        a.T.reshape(D // 128, 128, T).transpose(1, 0, 2).reshape(128, -1))


def _ck(w, rows):
    C = w.shape[1]
    return np.ascontiguousarray(
        w.reshape(rows, 128, C).transpose(1, 0, 2).reshape(128, rows * C))


def make_in_maps(L0_attn_x, L0_attn_W_qkv, L0_attn_b_qkv, L0_ffn_U1, L0_ffn_V1,
                 L0_ffn_b1, L0_ffn_U2, L0_ffn_V2, L0_ffn_b2):
    x = np.asarray(L0_attn_x, dtype=np.float32).reshape(S, D)
    Wq = np.asarray(L0_attn_W_qkv, dtype=np.float32)
    bq = np.asarray(L0_attn_b_qkv, dtype=np.float32)
    U1 = np.asarray(L0_ffn_U1, dtype=np.float32)
    V1 = np.asarray(L0_ffn_V1, dtype=np.float32)
    b1 = np.asarray(L0_ffn_b1, dtype=np.float32)
    U2 = np.asarray(L0_ffn_U2, dtype=np.float32)
    V2 = np.asarray(L0_ffn_V2, dtype=np.float32)
    b2 = np.asarray(L0_ffn_b2, dtype=np.float32)

    WT = np.ascontiguousarray(Wq.T)
    wtq = _ck(WT[:, 0:1024], 8)
    wtkv = np.concatenate([_ck(WT[:, 1024:2048], 8),
                           _ck(WT[:, 2048:3072], 8)], axis=1)
    brow = np.ascontiguousarray(bq[None, 1024:])
    bqcol = np.stack([bq[128 * o:128 * (o + 1)] for o in range(8)]
                     + [bq[1024 + 128 * o:1024 + 128 * (o + 1)] for o in range(8)],
                     axis=1).astype(np.float32)
    u1 = _ck(U1, 8)
    v1 = _ck(V1, 2)
    u2 = _ck(U2, 32)
    v2 = _ck(V2, 2)
    b1c = np.ascontiguousarray(b1.reshape(32, 128).T)
    b2col = np.ascontiguousarray(b2.reshape(8, 128).T)

    m32 = _m32_table()
    p = np.arange(128)[:, None]
    cc = np.arange(1408)[None, :]
    Dm = cc - 384 - p
    master = np.where(Dm >= 0, m32[np.clip(Dm, 0, None)] / 32.0, 0.0).astype(np.float32)

    t = np.arange(T_R, dtype=np.float64)
    aQ64 = np.broadcast_to(AEFF ** (W + t) / 32.0, (128, T_R))
    aQhi, aQlo = _hilo(aQ64)
    c4 = np.arange(4, dtype=np.float64)[None, :]
    acol4_64 = AEFF ** (-(p + 128 * c4))
    cH = np.arange(4 * NSMAC, dtype=np.float64)[None, :]
    acolH_base64 = AEFF ** (NSMAC * 512 - (128 * cH + p))
    ident = np.eye(128, dtype=np.float32)

    shared = dict(wtkv=wtkv, wtq=wtq, brow=brow, bqcol=bqcol, u1=u1, v1=v1,
                  u2=u2, v2=v2, b1c=b1c, b2col=b2col, master=master,
                  aQhi=aQhi, aQlo=aQlo, ident=ident)

    in_maps = []
    for core in range(NCORES):
        own = x[T_CORE * core:T_CORE * (core + 1)]
        xq = np.concatenate([_fm(own[T_R * rr:T_R * (rr + 1)]) for rr in range(NR)],
                            axis=1)
        lo = T_CORE * core - HALO
        halo = np.zeros((HALO, D), dtype=np.float32)
        src_lo = max(lo, 0)
        if src_lo < T_CORE * core:
            halo[src_lo - lo:] = x[src_lo:T_CORE * core]
        xh_tm = halo[0:2560].reshape(20, 128, D).transpose(1, 0, 2).reshape(128, 20480)
        xh = np.concatenate([xh_tm, _fm(halo[2560:3072])], axis=1)
        hidx = 128 * cH + p
        valid = (lo + hidx) >= 0
        acolH64 = np.where(valid, np.sqrt(acolH_base64), 0.0)
        ac64 = np.concatenate([acol4_64, acolH64], axis=1)
        achi, aclo = _hilo(ac64)
        vmask = np.full((128, 1), 1.0 if core > 0 else 0.0, dtype=np.float32)
        m = dict(shared)
        m.update(xq=np.ascontiguousarray(xq), xh=np.ascontiguousarray(xh),
                 achi=achi, aclo=aclo, vmask=vmask)
        in_maps.append(m)
    return in_maps


def kernel(L0_attn_x, L0_attn_W_qkv, L0_attn_b_qkv, L0_ffn_U1, L0_ffn_V1,
           L0_ffn_b1, L0_ffn_U2, L0_ffn_V2, L0_ffn_b2):
    in_maps = make_in_maps(L0_attn_x, L0_attn_W_qkv, L0_attn_b_qkv, L0_ffn_U1,
                           L0_ffn_V1, L0_ffn_b1, L0_ffn_U2, L0_ffn_V2, L0_ffn_b2)
    nc = build_nc()
    res = run_bass_kernel_spmd(nc, in_maps, core_ids=list(range(NCORES)))
    out = np.concatenate([r["y"] for r in res.results], axis=0)
    return out.reshape(1, S, D).astype(np.float32)


# version 26 (final)
# speedup vs baseline: 1.0034x; 1.0001x over previous
import math
import numpy as np
import ml_dtypes
import concourse.bass as bass
import concourse.mybir as mybir
from concourse.bass_utils import run_bass_kernel_spmd
from contextlib import ExitStack

F32 = mybir.dt.float32
BF16 = mybir.dt.bfloat16
AF = mybir.ActivationFunctionType
ALU = mybir.AluOpType
f32 = np.float32

AEFF = float(np.float32(0.99))
D = 1024
S = 16384
NCORES = 8
T_CORE = S // NCORES
NR = 4
T_R = T_CORE // NR
W = 512
HALO = 3072
NSMAC = 5
DFF = 4096
RANK = 256


def merge(*tokens):
    w = {}
    for t in tokens:
        if t is None:
            continue
        items = t.items() if isinstance(t, dict) else [t]
        for s, v in items:
            if w.get(s, -1) < v:
                w[s] = v
    return w


class Sched:
    ENGINES = ("pe", "dve", "act", "gp", "sync")
    DMA_SEMS = ("hw", "wtk", "wtv", "wtq", "miscsw", "xh0", "xh1", "xh2", "xh3",
                "xqs", "ffnw", "st2", "st3", "st4", "st5", "st6", "st7", "xh4", "wbq", "wbv", "wbu",
                "st0", "st1", "tabs")

    def __init__(self):
        self.ops = {e: [] for e in self.ENGINES}
        self.cnt = {k: 0 for k in ("pe", "dve", "act", "gp") + self.DMA_SEMS}

    def _emit(self, eng, fn, waits, inc, amt):
        self.ops[eng].append((merge(waits) if waits else {}, fn, inc, amt))
        if inc is None:
            return None
        self.cnt[inc] += amt
        return (inc, self.cnt[inc])

    def pe(self, fn, waits=None, inc=False):
        return self._emit("pe", fn, waits, "pe" if inc else None, 1)

    def dve(self, fn, waits=None):
        return self._emit("dve", fn, waits, "dve", 1)

    def act(self, fn, waits=None):
        return self._emit("act", fn, waits, "act", 1)

    def gp(self, fn, waits=None):
        return self._emit("gp", fn, waits, "gp", 1)

    def gp_dma(self, fn, sem="miscsw", waits=None, gate=False):
        w = merge(waits, {sem: self.cnt[sem]}) if gate else waits
        return self._emit("gp", fn, w, sem, 16)

    def sync_dma(self, fn, waits=None, sem="hw", gate=False):
        w = merge(waits, {sem: self.cnt[sem]}) if gate else waits
        return self._emit("sync", fn, w, sem, 16)

    def act_dma(self, fn, waits=None, sem="hw", gate=False):
        w = merge(waits, {sem: self.cnt[sem]}) if gate else waits
        return self._emit("act", fn, w, sem, 16)

    def wait_only(self, eng, waits):
        return self._emit(eng, None, waits, None, 1)

    def barrier_token(self):
        return dict(self.cnt)


def build_nc():
    nc = bass.Bass("TRN2", target_bir_lowering=False,
                   dynamic_dma_scratch_size=4096, num_swdge_queues=2)
    sch = Sched()

    def dram(name, shape, dtype=F32, kind="ExternalInput"):
        return nc.dram_tensor(name, shape, dtype, kind=kind).ap()

    xq_h = dram("xq", [128, 16384])
    xh_h = dram("xh", [128, 24576])
    wtkv_h = dram("wtkv", [128, 16384])
    wtq_h = dram("wtq", [128, 8192])
    brow_h = dram("brow", [1, 2048])
    bqcol_h = dram("bqcol", [128, 16])
    u1_h = dram("u1", [128, 2048])
    v1_h = dram("v1", [128, 8192])
    u2_h = dram("u2", [128, 8192])
    v2_h = dram("v2", [128, 2048])
    b1c_h = dram("b1c", [128, 32])
    b2col_h = dram("b2col", [128, 8])
    master_h = dram("master", [128, 1408])
    aQhi_h = dram("aQhi", [128, 512])
    aQlo_h = dram("aQlo", [128, 512])
    achi_h = dram("achi", [128, 4 + 4 * NSMAC])
    aclo_h = dram("aclo", [128, 4 + 4 * NSMAC])
    vmask_h = dram("vmask", [128, 1])
    ident_h = dram("ident", [128, 128])
    y_h = dram("y", [T_CORE, D], kind="ExternalOutput")
    wtqb_d = nc.dram_tensor("wtqb", [128, 8192], BF16, kind="Internal").ap()
    v1b_d = nc.dram_tensor("v1b", [128, 8192], BF16, kind="Internal").ap()
    u2b_d = nc.dram_tensor("u2b", [128, 8192], BF16, kind="Internal").ap()

    ctx = ExitStack()
    E = ctx.enter_context

    def sb(name, shape, dtype, side="left"):
        return E(nc.sbuf_tensor(name, shape, dtype, side=side))

    sem = {k: E(nc.semaphore(k)) for k in ("pe", "dve", "act", "gp") + Sched.DMA_SEMS}

    pb = [E(nc.psum_tensor(f"pb{i}", [128, 512], F32)) for i in range(6)]
    ptr = [E(nc.psum_tensor(f"ptr{i}", [128, 1024], BF16)) for i in range(2)]
    bank_free = [None] * 6
    bank_open = [False] * 6
    ptr_free = [None] * 2
    _rr = [0]

    def acquire_bank():
        for _ in range(6):
            i = _rr[0]
            _rr[0] = (i + 1) % 6
            if not bank_open[i]:
                bank_open[i] = True
                return i
        raise RuntimeError("no free psum bank")

    def mm_group(tiles, waits, nf=512, mrows=128, last_waits=None):
        bi = acquire_bank()
        n = len(tiles)
        tok = None
        for i, (lh, rh) in enumerate(tiles):
            w = merge(waits, bank_free[bi]) if i == 0 else None
            if i == n - 1 and last_waits:
                w = merge(w, last_waits)
            tok = sch.pe(
                lambda t, lh=lh, rh=rh, st=(i == 0), sp=(i == n - 1), bi=bi,
                nf=nf, mr=mrows:
                t.matmul(pb[bi][0:mr, 0:nf], lh, rh, start=st, stop=sp),
                waits=w, inc=(i == n - 1))
        bank_open[bi] = False
        return bi, tok

    master_s = sb("master_s", [128, 1408], F32)
    aQrep_s = sb("aQrep_s", [128, 512], F32)
    acol_s = sb("acol_s", [128, 4 + 4 * NSMAC], F32)
    vmask_s = sb("vmask_s", [128, 1], F32)
    ident_s = sb("ident_s", [128, 128], BF16)
    ones_s = sb("ones_s", [1, 512], BF16)
    onescol_s = sb("onescol_s", [128, 1], BF16)
    brow_s = sb("brow_s", [1, 2048], BF16)
    bqcol_s = sb("bqcol_s", [128, 16], F32)
    b1c_s = sb("b1c_s", [128, 32], F32)
    b2col_s = sb("b2col_s", [128, 8], F32)
    rowA_s = sb("rowA_s", [1, 512], BF16)
    rowB_s = sb("rowB_s", [1, 512], BF16)
    rowD_s = sb("rowD_s", [1, 512], BF16)
    colsq_s = sb("colsq_s", [128, 8], F32)
    mcq_s = sb("mcq_s", [128, 4], F32)
    cscq_s = sb("cscq_s", [128, 4], F32)
    vcq_s = sb("vcq_s", [128, 4], F32)
    colsb_s = sb("colsb_s", [128, 8], BF16)
    colacc_s = sb("colacc_s", [128, 16], F32)
    cw_s = sb("cw_s", [128, 8], F32)
    cs_s = sb("cs_s", [128, 2], F32)
    mrep_s = sb("mrep_s", [128, 512], BF16)
    rrep_s = sb("rrep_s", [128, 512], BF16)
    ysq_s = [sb(f"ysq{i}", [128, 512], BF16) for i in range(3)]
    xsc_s = [sb(f"xsc{i}", [128, 512], BF16) for i in range(2)]
    mcol_s = sb("mcol_s", [128, 4], F32)
    rcol_s = sb("rcol_s", [128, 4], F32)
    nu1s_s = sb("nu1s_s", [1, 256], BF16)
    p_s = sb("p_s", [128, 512], BF16)
    q2_s = sb("q2_s", [128, 512], BF16)
    stage_s = [sb(f"stage{i}", [128, 1024], F32) for i in range(4)]
    u1_s = sb("u1_s", [128, 2048], BF16)
    v2_s = sb("v2_s", [128, 2048], BF16)

    wtkv_s = sb("wtkv_s", [128, 16384], BF16, side="right")
    xqr2_s = sb("xqr2_s", [128, 8192], BF16, side="right")
    uacc_s = sb("uacc_s", [128, 8192], BF16, side="right")
    kfm2_s = [sb(f"kfm2_{i}", [128, 4096], BF16, side="right") for i in range(2)]
    vtm2_s = [sb(f"vtm2_{i}", [128, 4096], BF16, side="right") for i in range(2)]

    t_ones = sch.gp(lambda g: g.memset(ones_s[:], 1.0))
    t_onescol = sch.gp(lambda g: g.memset(onescol_s[:], 1.0))

    def ld_sw(dst, src, waits=None, dsem="miscsw", gate=False):
        return sch.gp_dma(lambda g, d=dst, s=src: g.dma_start(out=d, in_=s),
                          sem=dsem, waits=waits, gate=gate)

    def ld_hw(dst, src, waits=None, sem="hw", gate=False):
        return sch.sync_dma(lambda s_, d=dst, s=src: s_.dma_start(out=d, in_=s),
                            waits=waits, sem=sem, gate=gate)

    def ld_hw_act(dst, src, waits=None, sem="hw", gate=False):
        return sch.act_dma(lambda a_, d=dst, s=src: a_.dma_start(out=d, in_=s),
                           waits=waits, sem=sem, gate=gate)

    NA = 4 + 4 * NSMAC
    with nc.sbuf_tensor("tabhi", [128, 536], F32) as tabhi_s, \
         nc.sbuf_tensor("tablo", [128, 536], F32) as tablo_s:
        th2 = ld_hw(tabhi_s[:, 0:NA], achi_h[:], sem="tabs")
        tl2 = ld_hw(tablo_s[:, 0:NA], aclo_h[:], sem="tabs")
        t_acol = sch.dve(lambda v: v.tensor_tensor(
            out=acol_s[:], in0=tabhi_s[:, 0:NA], in1=tablo_s[:, 0:NA],
            op=ALU.mult), waits=merge(th2, tl2))
        th = ld_hw(tabhi_s[:, NA:NA + 512], aQhi_h[:], sem="tabs", gate=True)
        tl = ld_hw(tablo_s[:, NA:NA + 512], aQlo_h[:], sem="tabs")
        t_aQ = sch.dve(lambda v: v.tensor_tensor(
            out=aQrep_s[:], in0=tabhi_s[:, NA:NA + 512],
            in1=tablo_s[:, NA:NA + 512], op=ALU.mult), waits=merge(th, tl))
    t_tab = merge(t_acol, t_aQ)
    t_hw_misc = merge(ld_hw(bqcol_s[:], bqcol_h[:]),
                      ld_hw(b1c_s[:], b1c_h[:]),
                      ld_hw(b2col_s[:], b2col_h[:]),
                      ld_hw(vmask_s[:], vmask_h[:]),
                      ld_hw(master_s[:, 0:704], master_h[:, 0:704]),
                      ld_hw(master_s[:, 704:1408], master_h[:, 704:1408]))

    acolb_s = sb("acolb_s", [128, 4 + 4 * NSMAC], BF16)
    s0_s = sb("s0_s", [1, 1], F32)
    t_acolb = sch.dve(lambda v: v.tensor_copy(acolb_s[:], acol_s[:]),
                      waits=t_acol)
    with nc.sbuf_tensor("xhtm", [128, 20480], BF16, side="right") as xhtm_s, \
         nc.sbuf_tensor("sxx", [128, 8192], BF16, side="right") as sxx_s, \
         nc.sbuf_tensor("hx5", [128, 4096], BF16, side="right") as hx5_s, \
         nc.sbuf_tensor("hrow", [1, 1024], BF16, side="right") as hrow_s, \
         nc.sbuf_tensor("hcol", [128, 8], BF16, side="right") as hcol_s, \
         nc.sbuf_tensor("wkr", [1, 1024], BF16, side="right") as wkr_s:
        xh_cuts = [0, 2048, 7168, 12288, 17408, 20480]
        t_xh = [ld_sw(xhtm_s[:, xh_cuts[i]:xh_cuts[i + 1]],
                      xh_h[:, xh_cuts[i]:xh_cuts[i + 1]], dsem=f"xh{i}")
                for i in range(5)]
        xh_piece = [0, 0] + [1] * 5 + [2] * 5 + [3] * 5 + [4] * 3
        for i in range(2):
            ld_sw(wtkv_s[:, 4096 * i:4096 * (i + 1)],
                  wtkv_h[:, 4096 * i:4096 * (i + 1)], dsem="wtk")
        t_wtk = ("wtk", sch.cnt["wtk"])
        ld_sw(brow_s[:], brow_h[:])
        _t_ident = ld_sw(ident_s[:], ident_h[:], gate=True)
        t_sw_misc1 = ("miscsw", sch.cnt["miscsw"])
        t_consts = merge(t_sw_misc1, t_hw_misc, t_ones, t_onescol, t_tab)
        for i in range(2, 4):
            ld_sw(wtkv_s[:, 4096 * i:4096 * (i + 1)],
                  wtkv_h[:, 4096 * i:4096 * (i + 1)], dsem="wtv")
        t_wtkv = ("wtv", sch.cnt["wtv"])
        t_hx5 = ld_sw(hx5_s[:], xh_h[:, 20480:24576], dsem="xh1", gate=True)
        xq_tok = {}
        for rr in range(2):
            xq_tok[rr] = ld_sw(xqr2_s[:, 4096 * rr:4096 * (rr + 1)],
                               xq_h[:, 4096 * rr:4096 * (rr + 1)],
                               dsem="xqs", gate=True)
        sc_tok = []
        for c in range(20):
            sc_tok.append(sch.act(lambda a, c=c: a.activation(
                out=xhtm_s[:, 1024 * c:1024 * (c + 1)],
                in_=xhtm_s[:, 1024 * c:1024 * (c + 1)], func=AF.Identity,
                scale=acol_s[:, 4 + c:4 + c + 1]),
                waits=merge(t_xh[xh_piece[c]], t_acol)))
        t_xs = sc_tok[-1]
        sxx_tok = {}
        all_gk = [(a, db) for a in range(8) for db in range(2)
                  if not (db == 0 and a >= 4)]
        for p in range(2):
            gks = all_gk[6 * p:6 * (p + 1)]
            bks = {g: acquire_bank() for g in gks}
            mmt = {}
            for c in range(20):
                for a, db in gks:
                    bi = bks[(a, db)]
                    w = merge(sc_tok[c],
                              bank_free[bi] if c == 0 else None)
                    mmt[(a, db)] = sch.pe(
                        lambda t, a=a, db=db, c=c, bi=bi: t.matmul(
                            pb[bi][:],
                            xhtm_s[:, 1024 * c + 128 * a:1024 * c + 128 * (a + 1)],
                            xhtm_s[:, 1024 * c + 512 * db:1024 * c + 512 * (db + 1)],
                            start=(c == 0), stop=(c == 19)),
                        waits=w, inc=(c == 19))
            for a, db in gks:
                bi = bks[(a, db)]
                bank_open[bi] = False
                dst = sxx_s[:, 1024 * a + 512 * db:1024 * a + 512 * (db + 1)]
                ev = sch.act(lambda ac, bi=bi, dst=dst: ac.activation(
                    out=dst, in_=pb[bi][:], func=AF.Identity),
                    waits=mmt[(a, db)])
                sxx_tok[(a, db)] = ev
                bank_free[bi] = ev
        for a in range(4, 8):
            pt = a % 2
            ptok = None
            for j in range(4):
                src = sxx_s[:, 1024 * j + 512 + 128 * (a - 4):
                            1024 * j + 512 + 128 * (a - 3)]
                ptok = sch.pe(lambda t, src=src, j=j, pt=pt: t.transpose(
                    ptr[pt][:, 128 * j:128 * (j + 1)], src, ident_s[:]),
                    waits=merge(sxx_tok[(j, 1)], t_consts, _t_ident,
                                ptr_free[pt] if j == 0 else None),
                    inc=(j == 3))
            ev = sch.act(lambda ac, a=a, pt=pt: ac.activation(
                out=sxx_s[:, 1024 * a:1024 * a + 512], in_=ptr[pt][:, 0:512],
                func=AF.Identity), waits=ptok)
            ptr_free[pt] = ev
            sxx_tok[(a, 0)] = ev
        sxx_ready = merge(*sxx_tok.values())
        hr_tok = []
        for db in range(2):
            tiles = [(acolb_s[:, 4 + c:4 + c + 1],
                      xhtm_s[:, 1024 * c + 512 * db:1024 * c + 512 * (db + 1)])
                     for c in range(20)]
            bi, tok = mm_group(tiles, merge(t_xs, t_acolb), mrows=1)
            ev = sch.dve(lambda v, bi=bi, db=db: v.tensor_copy(
                hrow_s[0:1, 512 * db:512 * (db + 1)], pb[bi][0:1, :]), waits=tok)
            hr_tok.append(ev)
            bank_free[bi] = ev
        tiles = [(acolb_s[:, 4 + c:4 + c + 1], acolb_s[:, 4 + c:4 + c + 1])
                 for c in range(20)]
        bi, tok = mm_group(tiles, t_acolb, nf=1, mrows=1)
        t_s0 = sch.dve(lambda v, bi=bi: v.tensor_copy(
            s0_s[0:1, 0:1], pb[bi][0:1, 0:1]), waits=tok)
        bank_free[bi] = t_s0
        t_hrow = merge(*hr_tok)
        halo_gate = merge({"pe": sch.cnt["pe"]},
                          *[(f"xh{i}", sch.cnt[f"xh{i}"]) for i in range(5)])
        hc_tok = []
        for c in range(8):
            bi, tok = mm_group([(hrow_s[0:1, 128 * c:128 * (c + 1)],
                                 ones_s[0:1, 0:1])], merge(t_hrow, t_ones), nf=1)
            ev = sch.dve(lambda v, bi=bi, c=c: v.tensor_copy(
                hcol_s[:, c:c + 1], pb[bi][0:128, 0:1]), waits=tok)
            hc_tok.append(ev)
            bank_free[bi] = ev
        t_hcol = merge(*hc_tok)
        wk_tok = []
        for db in range(2):
            tiles = [(hcol_s[:, c:c + 1],
                      wtkv_s[:, 1024 * c + 512 * db:1024 * c + 512 * (db + 1)])
                     for c in range(8)]
            bi, tok = mm_group(tiles, merge(t_hcol, t_wtk), mrows=1)
            ev = sch.dve(lambda v, bi=bi, db=db: v.tensor_copy(
                wkr_s[0:1, 512 * db:512 * (db + 1)], pb[bi][0:1, :]), waits=tok)
            wk_tok.append(ev)
            bank_free[bi] = ev
        t_wkr = sch.dve(lambda v: v.scalar_tensor_tensor(
            out=wkr_s[0:1, :], in0=brow_s[0:1, 0:1024], scalar=s0_s[0:1, 0:1],
            in1=wkr_s[0:1, :], op0=ALU.mult, op1=ALU.add),
            waits=merge(t_s0, t_sw_misc1, *wk_tok))

        init_tok = {}
        last5 = None
        for ot in range(8):
            tiles = [(wtkv_s[:, 1024 * c + 128 * ot:1024 * c + 128 * (ot + 1)],
                      hx5_s[:, 512 * c:512 * (c + 1)]) for c in range(8)]
            bi, tok = mm_group(tiles, merge(t_hx5, t_wtk))
            last5 = tok
            dst = kfm2_s[1][:, 512 * ot:512 * (ot + 1)]
            ev = sch.act(lambda a, bi=bi, dst=dst, ot=ot: a.activation(
                out=dst, in_=pb[bi][:], func=AF.Identity,
                bias=bqcol_s[:, 8 + ot:8 + ot + 1]), waits=merge(tok, t_hw_misc))
            init_tok[("k", ot)] = ev
            bank_free[bi] = ev
        for tt in range(4):
            for db in range(2):
                tiles = [(hx5_s[:, 512 * c + 128 * tt:512 * c + 128 * (tt + 1)],
                          wtkv_s[:, 8192 + 1024 * c + 512 * db:8192 + 1024 * c + 512 * (db + 1)])
                         for c in range(8)]
                tiles.append((ones_s[0:1, 0:128],
                              brow_s[0:1, 1024 + 512 * db:1024 + 512 * (db + 1)]))
                bi, tok = mm_group(tiles, merge(t_hx5, t_wtkv, t_ones, t_sw_misc1))
                last5 = tok
                dst = vtm2_s[1][:, 1024 * tt + 512 * db:1024 * tt + 512 * (db + 1)]
                ev = sch.act(lambda a, bi=bi, dst=dst: a.activation(
                    out=dst, in_=pb[bi][:], func=AF.Identity,
                    scale=vmask_s[:, 0:1]), waits=merge(tok, t_hw_misc))
                init_tok[("v", tt, db)] = ev
                bank_free[bi] = ev
        init_ready = merge(*init_tok.values())

        m1_half = [kfm2_s[0], vtm2_s[0]]
        m1_tok = {}
        for ap2 in range(8):
            for db in range(2):
                tiles = [(sxx_s[:, 1024 * c + 128 * ap2:1024 * c + 128 * (ap2 + 1)],
                          wtkv_s[:, 1024 * c + 512 * db:1024 * c + 512 * (db + 1)])
                         for c in range(8)]
                tiles.append((hrow_s[0:1, 128 * ap2:128 * (ap2 + 1)],
                              brow_s[0:1, 512 * db:512 * (db + 1)]))
                bi, tok = mm_group(tiles, merge(sxx_ready, t_hrow, t_wtk,
                                                t_sw_misc1))
                dst = m1_half[db][:, 512 * ap2:512 * (ap2 + 1)]
                ev = sch.act(lambda a, bi=bi, dst=dst: a.activation(
                    out=dst, in_=pb[bi][:], func=AF.Identity), waits=tok)
                m1_tok[(ap2, db)] = ev
                bank_free[bi] = ev
        m1_ready = merge(*m1_tok.values())
        uacc_tok = {}
        m1_cons = None
        for a in range(8):
            dbk, ak = a // 4, a % 4
            for db in range(2):
                tiles = [(m1_half[dbk][:, 512 * c + 128 * ak:512 * c + 128 * (ak + 1)],
                          wtkv_s[:, 8192 + 1024 * c + 512 * db:8192 + 1024 * c + 512 * (db + 1)])
                         for c in range(8)]
                tiles.append((wkr_s[0:1, 128 * a:128 * (a + 1)],
                              brow_s[0:1, 1024 + 512 * db:1024 + 512 * (db + 1)]))
                bi, tok = mm_group(tiles, merge(m1_ready, t_wkr, t_wtkv))
                m1_cons = merge(m1_cons, tok)
                dst = uacc_s[:, 1024 * a + 512 * db:1024 * a + 512 * (db + 1)]
                ev = sch.act(lambda ac, bi=bi, dst=dst: ac.activation(
                    out=dst, in_=pb[bi][:], func=AF.Identity), waits=tok)
                uacc_tok[(a, db)] = ev
                bank_free[bi] = ev
        uacc_ready = merge(*uacc_tok.values())
        kfm_cons0_init = m1_cons
    t_consts = merge(t_consts, _t_ident, ("miscsw", sch.cnt["miscsw"]))
    t_u1 = ld_sw(u1_s[:], u1_h[:], gate=True)
    t_v2 = ld_sw(v2_s[:], v2_h[:], gate=True)
    t_consts = merge(t_consts, t_u1, t_v2)
    bi_us, tok_us = mm_group([(onescol_s[:], u1_s[:, 256 * c:256 * (c + 1)])
                              for c in range(8)],
                             merge(t_u1, t_ones, t_onescol), nf=256, mrows=1)
    t_nu1 = sch.act(lambda a, bi=bi_us: a.activation(
        out=nu1s_s[0:1, :], in_=pb[bi][0:1, 0:256], func=AF.Identity,
        scale=f32(-1.0 / 1024.0)), waits=tok_us)
    bank_free[bi_us] = t_nu1

    ln_cons = [None]
    xq_cons = [None, None]
    wb_tok = {}
    p_cons = [None]
    q2_cons = [None]
    h_cons = [None]
    stage_cons = [None] * 4
    xsc_cons = [None, None]
    kfm_tokens = [None, init_ready]
    vtm_tokens = [None, init_ready]
    kfm_cons = [kfm_cons0_init, None]
    vtm_cons = [kfm_cons0_init, None]
    u_ready = [uacc_ready]

    def ln_stats(src_s, src_ready, bw, reps=True, cols=False, mid=None,
                 chunk_tok=None):
        guard = ln_cons[0]
        ctok = chunk_tok or {}
        banks = {}
        bsum = acquire_bank()
        tok = None
        for c in range(8):
            w = merge(ctok.get(c, src_ready), t_consts,
                      bank_free[bsum] if c == 0 else None)
            tok = sch.pe(lambda t, c=c, bi=bsum, ss=src_s: t.matmul(
                pb[bi][0:1, 0:512], onescol_s[:], ss[:, 512 * c:512 * (c + 1)],
                start=(c == 0), stop=(c == 7)), waits=w, inc=(c == 7))
        bank_open[bsum] = False
        banks[0] = (bsum, tok)
        sq_tok = {}
        mmsq_tok = {}
        bsq = acquire_bank()
        for c in range(8):
            dst = ysq_s[c % 3]
            sq_tok[c] = sch.act(
                lambda a, c=c, dst=dst, ss=src_s: a.activation(
                    out=dst[:], in_=ss[:, 512 * c:512 * (c + 1)], func=AF.Square),
                waits=merge(ctok.get(c, src_ready), guard, mmsq_tok.get(c - 3)))
            w = merge(sq_tok[c], bank_free[bsq] if c == 0 else None, t_consts)
            mmsq_tok[c] = sch.pe(
                lambda t, c=c, bi=bsq: t.matmul(
                    pb[bi][0:1, 0:512], onescol_s[:], ysq_s[c % 3][:],
                    start=(c == 0), stop=(c == 7)),
                waits=w, inc=True)
        bank_open[bsq] = False
        banks[1] = (bsq, mmsq_tok[7])
        (bsumi, toks), (bsqi, tokq) = banks[0], banks[1]
        evA = sch.dve(lambda v, bi=bsumi: v.tensor_copy(
            rowA_s[0:1, :], pb[bi][0:1, :]), waits=merge(toks, guard))
        bank_free[bsumi] = evA
        evB = sch.dve(lambda v, bi=bsqi: v.tensor_copy(
            rowB_s[0:1, :], pb[bi][0:1, :]), waits=merge(tokq, guard))
        bank_free[bsqi] = evB
        bcol = acquire_bank()
        ctok = None
        for j in range(8):
            row, rtok = (rowA_s, evA) if j < 4 else (rowB_s, evB)
            jj = j % 4
            w = merge(rtok, t_ones, bank_free[bcol] if j == 0 else None)
            ctok = sch.pe(lambda t, j=j, row=row, jj=jj, bi=bcol: t.matmul(
                pb[bi][0:128, j:j + 1], row[0:1, 128 * jj:128 * (jj + 1)],
                ones_s[0:1, 0:1], start=True, stop=True), waits=w,
                inc=(j == 7))
        bank_open[bcol] = False
        evC = sch.dve(lambda v, bi=bcol: v.tensor_copy(
            colsq_s[:], pb[bi][0:128, 0:8]), waits=merge(ctok, guard))
        bank_free[bcol] = evC
        t1c = sch.dve(lambda v: v.tensor_scalar(
            out=mcq_s[:], in0=colsq_s[:, 0:4], scalar1=float(1.0 / 1024),
            scalar2=None, op0=ALU.mult), waits=evC)
        t2c = sch.dve(lambda v: v.tensor_tensor(
            out=cscq_s[:], in0=mcq_s[:], in1=mcq_s[:], op=ALU.mult), waits=t1c)
        t3c = sch.dve(lambda v: v.scalar_tensor_tensor(
            out=vcq_s[:], in0=colsq_s[:, 4:8], scalar=float(1.0 / 1024),
            in1=cscq_s[:], op0=ALU.mult, op1=ALU.subtract),
            waits=merge(t2c, evC))
        t4c = sch.act(lambda a: a.activation(
            out=cscq_s[:], in_=vcq_s[:], func=AF.Sqrt, bias=f32(1e-5)),
            waits=t3c)
        t5c = sch.dve(lambda v: v.reciprocal(out=vcq_s[:], in_=cscq_s[:]),
                      waits=t4c)
        cb1 = sch.dve(lambda v: v.tensor_copy(colsb_s[:, 0:4], mcq_s[:]),
                      waits=merge(t1c, guard))
        cb2 = sch.dve(lambda v: v.tensor_copy(colsb_s[:, 4:8], vcq_s[:]),
                      waits=merge(t5c, guard))
        out = {"t1": evA, "sq": merge(*sq_tok.values())}
        if reps:
            brow1 = acquire_bank()
            brow2 = acquire_bank()
            rtok1 = rtok2 = None
            for j in range(8):
                bi = brow1 if j < 4 else brow2
                w = merge(cb1 if j < 4 else cb2, t_consts,
                          bank_free[bi] if j % 4 == 0 else None)
                tok = sch.pe(lambda t, j=j, bi=bi: t.matmul(
                    pb[bi][0:1, 128 * (j % 4):128 * (j % 4 + 1)],
                    colsb_s[:, j:j + 1], ident_s[:, 0:128],
                    start=True, stop=True), waits=w, inc=(j % 4 == 3))
                if j == 3:
                    rtok1 = tok
                elif j == 7:
                    rtok2 = tok
            bank_open[brow1] = False
            bank_open[brow2] = False
            evD = sch.dve(lambda v, bi=brow1: v.tensor_copy(
                rowD_s[0:1, :], pb[bi][0:1, 0:512]), waits=merge(rtok1, guard))
            evE = sch.dve(lambda v, bi=brow2: v.tensor_copy(
                rowB_s[0:1, :], pb[bi][0:1, 0:512]), waits=merge(rtok2, guard))
            bank_free[brow1] = evD
            bank_free[brow2] = evE
            bi6, tok6 = mm_group([(ones_s[0:1, 0:128], rowD_s[0:1, :])],
                                 merge(evD, guard, t_consts), nf=512)
            t6 = sch.act(lambda a, bi=bi6: a.activation(
                out=mrep_s[:], in_=pb[bi][:], func=AF.Identity),
                waits=merge(tok6, guard))
            bank_free[bi6] = t6
            bi7, tok7 = mm_group([(ones_s[0:1, 0:128], rowB_s[0:1, :])],
                                 merge(evE, guard), nf=512)
            t7 = sch.act(lambda a, bi=bi7: a.activation(
                out=rrep_s[:], in_=pb[bi][:], func=AF.Identity),
                waits=merge(tok7, guard))
            bank_free[bi7] = t7
            out["reps"] = merge(t6, t7)
        if mid is not None:
            mid()
        return out

    for r in range(NR):
        par = r % 2
        prv = 1 - par
        def bw(eng):
            return None

        with nc.sbuf_tensor(f"y1_{r}", [128, 4096], BF16) as y1_s:
            with nc.sbuf_tensor(f"yraw{r}", [128, 4096], BF16, side="right") as yraw_s:
              with nc.sbuf_tensor(f"wtq{r}", [128, 8192], BF16, side="right") as wtq_s, \
                   nc.sbuf_tensor(f"ktms{r}", [128, 4096], BF16, side="right") as ktms_s:
                gw = merge(p_cons[0], q2_cons[0], h_cons[0],
                           halo_gate if r == 0 else None)
                xqr_s = xqr2_s[:, 4096 * par:4096 * (par + 1)]
                t_xq = xq_tok[r]
                if r < 3:
                    xq_tok[r + 1] = ld_sw(
                        xqr2_s[:, 4096 * ((r + 1) % 2):4096 * ((r + 1) % 2 + 1)],
                        xq_h[:, 4096 * (r + 1):4096 * (r + 2)],
                        waits=xq_cons[(r + 1) % 2], dsem="xqs", gate=True)
                if r == 0:
                    for i in range(2):
                        ld_sw(wtq_s[:, 4096 * i:4096 * (i + 1)],
                              wtq_h[:, 4096 * i:4096 * (i + 1)],
                              waits=gw if i == 0 else None, dsem="wtq",
                              gate=(i == 0))
                    t_wtq = ("wtq", sch.cnt["wtq"])
                    wb_tok["wtq"] = ld_hw(wtqb_d, wtq_s[:], waits=t_wtq,
                                          sem="wbq", gate=True)
                else:
                    t_wtq = ld_hw_act(wtq_s[:], wtqb_d,
                                      waits=merge(p_cons[0],
                                                  wb_tok["v1"] if r == 1 else None),
                                      sem="wbq", gate=True)

                with nc.sbuf_tensor(f"abuf{r}", [128, 4096], BF16) as abuf_s, \
                     nc.sbuf_tensor(f"qfm{r}", [128, 4096], BF16) as qfm_s:
                    qfm_tok, kfm_tok, vtm_tok = {}, {}, {}
                    for ot in range(8):
                        tiles = [(wtkv_s[:, 1024 * c + 128 * ot:1024 * c + 128 * (ot + 1)],
                                  xqr_s[:, 512 * c:512 * (c + 1)]) for c in range(8)]
                        bi, tok = mm_group(tiles, merge(t_xq, t_wtk, kfm_cons[par]))
                        dst = kfm2_s[par][:, 512 * ot:512 * (ot + 1)]
                        ev = sch.act(lambda a, bi=bi, dst=dst, ot=ot, par=par:
                                     a.activation(
                                         out=dst, in_=pb[bi][:], func=AF.Identity,
                                         bias=bqcol_s[:, 8 + ot:8 + ot + 1]),
                                     waits=tok)
                        kfm_tok[ot] = ev
                        bank_free[bi] = ev
                    for tt in range(4):
                        for db in range(2):
                            tiles = [(xqr_s[:, 512 * c + 128 * tt:512 * c + 128 * (tt + 1)],
                                      wtkv_s[:, 8192 + 1024 * c + 512 * db:8192 + 1024 * c + 512 * (db + 1)])
                                     for c in range(8)]
                            tiles.append((ones_s[0:1, 0:128],
                                          brow_s[0:1, 1024 + 512 * db:1024 + 512 * (db + 1)]))
                            bi, tok = mm_group(tiles, merge(t_xq, t_wtkv, vtm_cons[par]))
                            dst = vtm2_s[par][:, 1024 * tt + 512 * db:1024 * tt + 512 * (db + 1)]
                            ev = sch.act(lambda a, bi=bi, dst=dst: a.activation(
                                out=dst, in_=pb[bi][:], func=AF.Identity),
                                waits=tok)
                            vtm_tok[(tt, db)] = ev
                            bank_free[bi] = ev
                    for ot in range(8):
                        tiles = [(wtq_s[:, 1024 * c + 128 * ot:1024 * c + 128 * (ot + 1)],
                                  xqr_s[:, 512 * c:512 * (c + 1)]) for c in range(8)]
                        bi, tok = mm_group(tiles, merge(t_xq, t_wtq, t_consts))
                        dst = qfm_s[:, 512 * ot:512 * (ot + 1)]
                        ev = sch.act(lambda a, bi=bi, dst=dst, ot=ot: a.activation(
                            out=dst, in_=pb[bi][:], func=AF.Identity,
                            bias=bqcol_s[:, ot:ot + 1]),
                            waits=merge(tok, bw("act")))
                        qfm_tok[ot] = ev
                        bank_free[bi] = ev
                    qfm_ready = merge(*qfm_tok.values())
                    kfm_ready = merge(*kfm_tok.values())
                    vtm_ready = merge(*vtm_tok.values())
                    kfm_tokens[par] = kfm_ready
                    vtm_tokens[par] = vtm_ready
                    t_s1_pe = {"pe": sch.cnt["pe"]}

                    a_tok = {}
                    for jt in range(8):
                        src = kfm2_s[prv] if jt < 4 else kfm2_s[par]
                        src_tok = kfm_tokens[prv] if jt < 4 else kfm_ready
                        jj = jt % 4
                        tiles = [(src[:, 512 * c + 128 * jj:512 * c + 128 * (jj + 1)],
                                  qfm_s[:, 512 * c:512 * (c + 1)]) for c in range(8)]
                        bi, tok = mm_group(tiles, merge(src_tok, qfm_ready))
                        if jt < 4:
                            kfm_cons[prv] = merge(kfm_cons[prv], tok)
                        dst = abuf_s[:, 512 * jt:512 * (jt + 1)]
                        c0 = 896 - 128 * jt
                        ev = sch.dve(lambda v, bi=bi, dst=dst, c0=c0:
                                     v.tensor_tensor(out=dst, in0=pb[bi][:],
                                                     in1=master_s[:, c0:c0 + 512],
                                                     op=ALU.mult), waits=tok)
                        a_tok[jt] = ev
                        bank_free[bi] = ev
                    a_ready = merge(*a_tok.values())

                    yraw_tok = {}
                    last_cross_pe = None
                    for dv in range(8):
                        tiles = []
                        for jt in range(8):
                            vsrc = vtm2_s[prv] if jt < 4 else vtm2_s[par]
                            jj = jt % 4
                            tiles.append((vsrc[:, 1024 * jj + 128 * dv:1024 * jj + 128 * (dv + 1)],
                                          abuf_s[:, 512 * jt:512 * (jt + 1)]))
                        biA, tokA = mm_group(tiles, merge(a_ready, vtm_tokens[prv],
                                                          vtm_ready))
                        vtm_cons[prv] = merge(vtm_cons[prv], tokA)
                        tiles = [(uacc_s[:, 1024 * a + 128 * dv:1024 * a + 128 * (dv + 1)],
                                  qfm_s[:, 512 * a:512 * (a + 1)]) for a in range(8)]
                        biB, tokB = mm_group(tiles, merge(u_ready[0], qfm_ready))
                        last_cross_pe = merge(last_cross_pe, tokB)
                        xb = dv % 2
                        s1 = sch.dve(lambda v, biB=biB, xb=xb: v.tensor_tensor(
                            out=xsc_s[xb][:], in0=pb[biB][:], in1=aQrep_s[:],
                            op=ALU.mult), waits=merge(tokB, xsc_cons[xb]))
                        bank_free[biB] = s1
                        s2 = sch.dve(lambda v, xq=xqr_s, dv=dv, xb=xb: v.tensor_tensor(
                            out=xsc_s[xb][:], in0=xsc_s[xb][:],
                            in1=xq[:, 512 * dv:512 * (dv + 1)], op=ALU.add),
                            waits=merge(s1, t_xq))
                        dst = yraw_s[:, 512 * dv:512 * (dv + 1)]
                        s3 = sch.dve(lambda v, biA=biA, dst=dst, xb=xb: v.tensor_tensor(
                            out=dst, in0=pb[biA][:], in1=xsc_s[xb][:], op=ALU.add),
                            waits=merge(tokA, s2))
                        yraw_tok[dv] = s3
                        xsc_cons[xb] = s3
                        bank_free[biA] = s3
                    yraw_ready = merge(*yraw_tok.values())

                    def s2_emit(prv=prv, lcp=None):
                        ktm_tok = {}
                        for tt in range(4):
                            for g in range(2):
                                pt = (2 * tt + g) % 2
                                tok = None
                                for j in range(4):
                                    dc = 4 * g + j
                                    tok = sch.pe(lambda t, pt=pt, dc=dc, tt=tt, j=j, prv=prv:
                                                 t.transpose(
                                                     ptr[pt][:, 128 * j:128 * (j + 1)],
                                                     kfm2_s[prv][:, 512 * dc + 128 * tt:512 * dc + 128 * (tt + 1)],
                                                     ident_s[:]),
                                                 waits=merge(kfm_tokens[prv],
                                                             ptr_free[pt] if j == 0 else None,
                                                             t_consts), inc=(j == 3))
                                kfm_cons[prv] = merge(kfm_cons[prv], tok)
                                dst = ktms_s[:, 1024 * tt + 512 * g:1024 * tt + 512 * (g + 1)]
                                if g == 0:
                                    ev = sch.act(lambda a, pt=pt, dst=dst, tt=tt:
                                                 a.activation(
                                                     out=dst, in_=ptr[pt][:, 0:512],
                                                     func=AF.Identity,
                                                     scale=acol_s[:, tt:tt + 1]),
                                                 waits=merge(tok, wb_tok.get("u2") if r == 1 else None))
                                else:
                                    ev = sch.dve(lambda v, pt=pt, dst=dst, tt=tt:
                                                 v.tensor_scalar(
                                                     out=dst, in0=ptr[pt][:, 0:512],
                                                     scalar1=acol_s[:, tt:tt + 1],
                                                     scalar2=None, op0=ALU.mult),
                                                 waits=merge(tok, wb_tok.get("u2") if r == 1 else None))
                                ptr_free[pt] = ev
                                ktm_tok[(tt, g)] = ev
                        ktm_ready = merge(*ktm_tok.values())
                        u2_tok = {}
                        for a in range(8):
                            for db in range(2):
                                tiles = [(ktms_s[:, 1024 * tt + 128 * a:1024 * tt + 128 * (a + 1)],
                                          vtm2_s[prv][:, 1024 * tt + 512 * db:1024 * tt + 512 * (db + 1)])
                                         for tt in range(4)]
                                bi, tok = mm_group(tiles, merge(ktm_ready, vtm_tokens[prv],
                                                                lcp))
                                vtm_cons[prv] = merge(vtm_cons[prv], tok)
                                dst = uacc_s[:, 1024 * a + 512 * db:1024 * a + 512 * (db + 1)]
                                ev = sch.dve(lambda v, bi=bi, dst=dst: v.tensor_tensor(
                                    out=dst, in0=pb[bi][:], in1=dst, op=ALU.add),
                                    waits=merge(tok, u_ready[0], lcp))
                                u2_tok[(a, db)] = ev
                                bank_free[bi] = ev
                        _chi = float(np.float32(AEFF ** W))
                        _clo = float(np.float32(AEFF ** W / np.float64(np.float32(AEFF ** W))))
                        t_us = sch.dve(lambda v: v.tensor_scalar(
                            out=uacc_s[:], in0=uacc_s[:], scalar1=_chi,
                            scalar2=_clo, op0=ALU.mult, op1=ALU.mult),
                            waits=merge(*u2_tok.values()))
                        u_ready[0] = t_us

                    pbis = {}

                    def emit_px():
                        for tb in range(2):
                            for rt in range(2):
                                bi = acquire_bank()
                                pbis[(tb, rt)] = bi
                                for i in range(8):
                                    w = merge(yraw_ready, t_consts,
                                              bank_free[bi]) if i == 0 else None
                                    sch.pe(lambda t, bi=bi, rt=rt, tb=tb, i=i, ys=yraw_s:
                                           t.matmul(
                                               pb[bi][:, 0:256],
                                               u1_s[:, 256 * i + 128 * rt:256 * i + 128 * (rt + 1)],
                                               ys[:, 512 * i + 256 * tb:512 * i + 256 * (tb + 1)],
                                               start=(i == 0), stop=False), waits=w)

                    def mid_all(prv=prv, lcp=last_cross_pe, do_s2=(r < NR - 1)):
                        if do_s2:
                            s2_emit(prv, lcp)
                        emit_px()

                    ln1 = ln_stats(yraw_s, yraw_ready, bw, reps=True, mid=mid_all,
                                   chunk_tok=yraw_tok)
                    reps_tok, sq_done = ln1["reps"], ln1["sq"]
                    w_s3done = {"pe": sch.cnt["pe"], "dve": sch.cnt["dve"]}
                    xq_cons[par] = w_s3done

              with nc.sbuf_tensor(f"v1s{r}", [128, 8192], BF16, side="right") as v1_s, \
                   nc.sbuf_tensor(f"u2s{r}", [128, 8192], BF16, side="right") as u2_s, \
                   nc.sbuf_tensor(f"h{r}", [128, 8192], BF16, side="right") as h_s, \
                   nc.sbuf_tensor(f"out2_{r}", [128, 4096], BF16, side="right") as out2_s:
                  ww2 = {"pe": sch.cnt["pe"], "dve": sch.cnt["dve"],
                         "act": sch.cnt["act"]}
                  if r == 0:
                      ld_sw(v1_s[:, 0:4096], v1_h[:, 0:4096],
                            waits=merge(t_s1_pe, wb_tok["wtq"]),
                            dsem="ffnw", gate=True)
                      ld_sw(v1_s[:, 4096:8192], v1_h[:, 4096:8192], dsem="ffnw")
                      t_v1 = ("ffnw", sch.cnt["ffnw"])
                      wb_tok["v1"] = ld_hw(v1b_d, v1_s[:], waits=t_v1,
                                           sem="wbv", gate=True)
                      ld_sw(u2_s[:, 0:4096], u2_h[:, 0:4096], waits=ww2,
                            dsem="ffnw", gate=True)
                      ld_sw(u2_s[:, 4096:8192], u2_h[:, 4096:8192], dsem="ffnw")
                      t_ffnw = ("ffnw", sch.cnt["ffnw"])
                      wb_tok["u2"] = ld_hw(u2b_d, u2_s[:], waits=t_ffnw,
                                           sem="wbu", gate=True)
                  else:
                      t_v1 = ld_hw_act(v1_s[:], v1b_d,
                                       waits=merge(t_s1_pe, wb_tok["v1"]),
                                       sem="wbv", gate=True)
                      t_ffnw = ld_hw_act(u2_s[:], u2b_d,
                                         waits=merge(ww2, wb_tok["u2"]),
                                         sem="wbu", gate=True)
                  out2_tok = {}
                  for tb in range(2):
                      p_tok = {}
                      for rt in range(2):
                          bi = pbis[(tb, rt)]
                          tok = sch.pe(lambda t, bi=bi, rt=rt, tb=tb: t.matmul(
                              pb[bi][:, 0:256],
                              nu1s_s[0:1, 128 * rt:128 * (rt + 1)],
                              rowA_s[0:1, 256 * tb:256 * (tb + 1)],
                              start=False, stop=True),
                              waits=merge(ln1["t1"], t_nu1) if (tb, rt) == (0, 0) else None,
                              inc=True)
                          bank_open[bi] = False
                          dst = p_s[:, 256 * rt:256 * (rt + 1)]
                          ev = sch.dve(lambda v, bi=bi, dst=dst, tb=tb: v.tensor_tensor(
                              out=dst, in0=pb[bi][:, 0:256],
                              in1=rrep_s[:, 256 * tb:256 * (tb + 1)], op=ALU.mult),
                              waits=merge(tok, p_cons[0], reps_tok))
                          p_tok[rt] = ev
                          bank_free[bi] = ev
                      p_ready = merge(*p_tok.values())
                      if tb == 0:
                          y1_tok = {}
                          for c in range(8):
                              dtmp = ysq_s[c % 3]
                              ta = sch.dve(lambda v, c=c, dtmp=dtmp, ys=yraw_s:
                                           v.tensor_tensor(
                                               out=dtmp[:],
                                               in0=ys[:, 512 * c:512 * (c + 1)],
                                               in1=mrep_s[:], op=ALU.subtract),
                                           waits=merge(reps_tok, sq_done,
                                                       y1_tok.get(c - 4)))
                              y1_tok[c] = sch.dve(lambda v, c=c, dtmp=dtmp, y1s=y1_s:
                                                  v.tensor_tensor(
                                                      out=y1s[:, 512 * c:512 * (c + 1)],
                                                      in0=dtmp[:], in1=rrep_s[:],
                                                      op=ALU.mult), waits=ta)
                          y1_ready = merge(*y1_tok.values())
                          ln_cons[0] = y1_ready
                      h_ev = None
                      for ft in range(32):
                          tiles = [(v1_s[:, 4096 * rc + 128 * ft:4096 * rc + 128 * (ft + 1)],
                                    p_s[:, 256 * rc:256 * (rc + 1)]) for rc in range(2)]
                          bi, tok = mm_group(tiles, merge(p_ready, h_cons[0], t_v1), nf=256)
                          p_cons[0] = merge(p_cons[0], tok)
                          dst = h_s[:, 256 * ft:256 * (ft + 1)]
                          h_ev = sch.act(lambda a, bi=bi, dst=dst, ft=ft: a.activation(
                              out=dst, in_=pb[bi][:, 0:256], func=AF.Gelu_apprx_tanh,
                              bias=b1c_s[:, ft:ft + 1]), waits=tok)
                          bank_free[bi] = h_ev
                      q2_tok = {}
                      for rt in range(2):
                          tiles = [(u2_s[:, 256 * fc + 128 * rt:256 * fc + 128 * (rt + 1)],
                                    h_s[:, 256 * fc:256 * (fc + 1)]) for fc in range(32)]
                          bi, tok = mm_group(tiles, merge(h_ev, q2_cons[0], t_ffnw), nf=256)
                          h_cons[0] = merge(h_cons[0], tok)
                          dst = q2_s[:, 256 * rt:256 * (rt + 1)]
                          ev = sch.act(lambda a, bi=bi, dst=dst: a.activation(
                              out=dst, in_=pb[bi][:, 0:256], func=AF.Identity),
                              waits=tok)
                          q2_tok[rt] = ev
                          bank_free[bi] = ev
                      q2_ready = merge(*q2_tok.values())
                      for dt in range(8):
                          tiles = [(v2_s[:, 1024 * rc + 128 * dt:1024 * rc + 128 * (dt + 1)],
                                    q2_s[:, 256 * rc:256 * (rc + 1)]) for rc in range(2)]
                          bi, tok = mm_group(tiles, q2_ready, nf=256)
                          q2_cons[0] = merge(q2_cons[0], tok)
                          dst = out2_s[:, 512 * dt + 256 * tb:512 * dt + 256 * (tb + 1)]
                          yin = y1_s[:, 512 * dt + 256 * tb:512 * dt + 256 * (tb + 1)]
                          ev = sch.dve(lambda v, bi=bi, dst=dst, yin=yin, dt=dt:
                                       v.scalar_tensor_tensor(
                                           out=dst, in0=pb[bi][:, 0:256],
                                           scalar=b2col_s[:, dt:dt + 1],
                                           in1=yin, op0=ALU.add, op1=ALU.add),
                                       waits=tok)
                          out2_tok[(tb, dt)] = ev
                          bank_free[bi] = ev
                  out2_ready = merge(*out2_tok.values())

                  s5_ev = {}
                  s5_sq_all = []
                  s5_nrm_all = []

                  def s5_tile(tt):
                      sg = tt
                      o = 4 * (tt % 2)
                      ev_all = []
                      sq_all = []
                      for g in range(2):
                          pt = g % 2
                          tok = None
                          for j in range(4):
                              dc = 4 * g + j
                              tok = sch.pe(lambda t, pt=pt, dc=dc, tt=tt, j=j, o2=out2_s:
                                           t.transpose(
                                               ptr[pt][:, 128 * j:128 * (j + 1)],
                                               o2[:, 512 * dc + 128 * tt:512 * dc + 128 * (tt + 1)],
                                               ident_s[:]),
                                           waits=merge(out2_tok[(tt // 2, dc)],
                                                       ptr_free[pt] if j == 0 else None,
                                                       t_consts),
                                           inc=(j == 3))
                          ev = sch.act(lambda a, pt=pt, g=g, sg=sg, tt=tt: a.activation(
                              out=stage_s[sg][:, 512 * g:512 * (g + 1)],
                              in_=ptr[pt][:, 0:512], func=AF.Identity,
                              accum_out=colacc_s[:, 4 * tt + g:4 * tt + g + 1]),
                              waits=merge(tok, stage_cons[sg]))
                          sq = sch.act(lambda a, pt=pt, g=g, tt=tt: a.activation(
                              out=ysq_s[g][:], in_=ptr[pt][:, 0:512], func=AF.Square,
                              accum_out=colacc_s[:, 4 * tt + 2 + g:4 * tt + 3 + g]),
                              waits=merge(tok, ln1["t1"]))
                          ptr_free[pt] = merge(ev, sq)
                          ev_all.append(ev)
                          sq_all.append(sq)
                      s5_ev[tt] = merge(*ev_all)
                      s5_sq_all.extend(sq_all)
                      ta = sch.dve(lambda v, tt=tt, o=o: v.tensor_tensor(
                          out=cw_s[:, o:o + 1], in0=colacc_s[:, 4 * tt:4 * tt + 1],
                          in1=colacc_s[:, 4 * tt + 1:4 * tt + 2], op=ALU.add),
                          waits=s5_ev[tt])
                      tb2 = sch.dve(lambda v, tt=tt, o=o: v.tensor_scalar(
                          out=mcol_s[:, tt:tt + 1], in0=cw_s[:, o:o + 1],
                          scalar1=float(-1.0 / 1024), scalar2=None, op0=ALU.mult),
                          waits=ta)
                      tc = sch.dve(lambda v, tt=tt, o=o: v.tensor_tensor(
                          out=cw_s[:, o + 1:o + 2], in0=colacc_s[:, 4 * tt + 2:4 * tt + 3],
                          in1=colacc_s[:, 4 * tt + 3:4 * tt + 4], op=ALU.add),
                          waits=merge(*sq_all))
                      td = sch.dve(lambda v, tt=tt, o=o: v.tensor_tensor(
                          out=cw_s[:, o + 2:o + 3], in0=mcol_s[:, tt:tt + 1],
                          in1=mcol_s[:, tt:tt + 1], op=ALU.mult), waits=tb2)
                      te = sch.dve(lambda v, tt=tt, o=o: v.scalar_tensor_tensor(
                          out=cw_s[:, o + 3:o + 4], in0=cw_s[:, o + 1:o + 2],
                          scalar=float(1.0 / 1024), in1=cw_s[:, o + 2:o + 3],
                          op0=ALU.mult, op1=ALU.subtract), waits=merge(tc, td))
                      tf = sch.act(lambda a, tt=tt, o=o: a.activation(
                          out=cs_s[:, tt % 2:tt % 2 + 1], in_=cw_s[:, o + 3:o + 4],
                          func=AF.Sqrt, bias=f32(1e-5)), waits=te)
                      tg = sch.dve(lambda v, tt=tt: v.reciprocal(
                          out=rcol_s[:, tt:tt + 1], in_=cs_s[:, tt % 2:tt % 2 + 1]),
                          waits=tf)
                      return merge(tb2, tg)

                  for tt in range(4):
                      sg = tt
                      cols_tok = s5_tile(tt)
                      nrm = sch.dve(lambda v, sg=sg, tt=tt: v.tensor_scalar(
                          out=stage_s[sg][:], in0=stage_s[sg][:],
                          scalar1=mcol_s[:, tt:tt + 1], scalar2=rcol_s[:, tt:tt + 1],
                          op0=ALU.add, op1=ALU.mult),
                          waits=merge(s5_ev[tt], cols_tok))
                      s5_nrm_all.append(nrm)
                      st_half = []
                      for hh in range(2):
                          st_half.append(sch.sync_dma(
                              lambda s_, sg=sg, tt=tt, r=r, hh=hh: s_.dma_start(
                                  out=y_h[512 * r + 128 * tt + 64 * hh:
                                          512 * r + 128 * tt + 64 * (hh + 1), :],
                                  in_=stage_s[sg][64 * hh:64 * (hh + 1), :]),
                              sem=f"st{sg + 4 * hh}", waits=nrm, gate=True))
                      stage_cons[sg] = merge(*st_half)
                  ln_cons[0] = merge(y1_ready, p_ready, *s5_sq_all, *s5_nrm_all)

    sch.wait_only("sync", {k: sch.cnt[k] for k in Sched.DMA_SEMS if sch.cnt[k]})

    with nc.Block() as block:
        def replay(eng_name):
            def go(engine):
                waited = {}
                for waits, fn, inc, amt in sch.ops[eng_name]:
                    for s, v in waits.items():
                        if waited.get(s, 0) < v:
                            engine.wait_ge(sem[s], v)
                            waited[s] = v
                    if fn is None:
                        continue
                    inst = fn(engine)
                    if inc is not None:
                        inst.then_inc(sem[inc], amt)
            return go
        block.tensor(replay("pe"))
        block.vector(replay("dve"))
        block.scalar(replay("act"))
        block.gpsimd(replay("gp"))
        block.sync(replay("sync"))

    ctx.close()
    return nc


_M32_CACHE = [None]


def _m32_table():
    if _M32_CACHE[0] is not None:
        return _M32_CACHE[0]
    try:
        import jax
        import jax.numpy as jnp
        with jax.default_device(jax.devices("cpu")[0]):
            m = np.asarray(0.99 ** jnp.arange(S).astype(jnp.float32),
                           dtype=np.float64)
    except Exception:
        m = (AEFF ** np.arange(S, dtype=np.float64)).astype(np.float32).astype(np.float64)
    _M32_CACHE[0] = m
    return m


def _hilo(t64):
    hi = t64.astype(np.float32)
    with np.errstate(divide="ignore", invalid="ignore"):
        lo = np.where(hi != 0, t64 / hi.astype(np.float64), 0.0).astype(np.float32)
    return np.ascontiguousarray(hi), np.ascontiguousarray(lo)


def _fm(a):
    T = a.shape[0]
    return np.ascontiguousarray(
        a.T.reshape(D // 128, 128, T).transpose(1, 0, 2).reshape(128, -1))


def _ck(w, rows):
    C = w.shape[1]
    return np.ascontiguousarray(
        w.reshape(rows, 128, C).transpose(1, 0, 2).reshape(128, rows * C))


def make_in_maps(L0_attn_x, L0_attn_W_qkv, L0_attn_b_qkv, L0_ffn_U1, L0_ffn_V1,
                 L0_ffn_b1, L0_ffn_U2, L0_ffn_V2, L0_ffn_b2):
    x = np.asarray(L0_attn_x, dtype=np.float32).reshape(S, D)
    Wq = np.asarray(L0_attn_W_qkv, dtype=np.float32)
    bq = np.asarray(L0_attn_b_qkv, dtype=np.float32)
    U1 = np.asarray(L0_ffn_U1, dtype=np.float32)
    V1 = np.asarray(L0_ffn_V1, dtype=np.float32)
    b1 = np.asarray(L0_ffn_b1, dtype=np.float32)
    U2 = np.asarray(L0_ffn_U2, dtype=np.float32)
    V2 = np.asarray(L0_ffn_V2, dtype=np.float32)
    b2 = np.asarray(L0_ffn_b2, dtype=np.float32)

    WT = np.ascontiguousarray(Wq.T)
    wtq = _ck(WT[:, 0:1024], 8)
    wtkv = np.concatenate([_ck(WT[:, 1024:2048], 8),
                           _ck(WT[:, 2048:3072], 8)], axis=1)
    brow = np.ascontiguousarray(bq[None, 1024:])
    bqcol = np.stack([bq[128 * o:128 * (o + 1)] for o in range(8)]
                     + [bq[1024 + 128 * o:1024 + 128 * (o + 1)] for o in range(8)],
                     axis=1).astype(np.float32)
    u1 = _ck(U1, 8)
    v1 = _ck(V1, 2)
    u2 = _ck(U2, 32)
    v2 = _ck(V2, 2)
    b1c = np.ascontiguousarray(b1.reshape(32, 128).T)
    b2col = np.ascontiguousarray(b2.reshape(8, 128).T)

    m32 = _m32_table()
    p = np.arange(128)[:, None]
    cc = np.arange(1408)[None, :]
    Dm = cc - 384 - p
    master = np.where(Dm >= 0, m32[np.clip(Dm, 0, None)] / 32.0, 0.0).astype(np.float32)

    t = np.arange(T_R, dtype=np.float64)
    aQ64 = np.broadcast_to(AEFF ** (W + t) / 32.0, (128, T_R))
    aQhi, aQlo = _hilo(aQ64)
    c4 = np.arange(4, dtype=np.float64)[None, :]
    acol4_64 = AEFF ** (-(p + 128 * c4))
    cH = np.arange(4 * NSMAC, dtype=np.float64)[None, :]
    acolH_base64 = AEFF ** (NSMAC * 512 - (128 * cH + p))
    ident = np.eye(128, dtype=np.float32)

    shared = dict(wtkv=wtkv, wtq=wtq, brow=brow, bqcol=bqcol, u1=u1, v1=v1,
                  u2=u2, v2=v2, b1c=b1c, b2col=b2col, master=master,
                  aQhi=aQhi, aQlo=aQlo, ident=ident)

    in_maps = []
    for core in range(NCORES):
        own = x[T_CORE * core:T_CORE * (core + 1)]
        xq = np.concatenate([_fm(own[T_R * rr:T_R * (rr + 1)]) for rr in range(NR)],
                            axis=1)
        lo = T_CORE * core - HALO
        halo = np.zeros((HALO, D), dtype=np.float32)
        src_lo = max(lo, 0)
        if src_lo < T_CORE * core:
            halo[src_lo - lo:] = x[src_lo:T_CORE * core]
        xh_tm = halo[0:2560].reshape(20, 128, D).transpose(1, 0, 2).reshape(128, 20480)
        xh = np.concatenate([xh_tm, _fm(halo[2560:3072])], axis=1)
        hidx = 128 * cH + p
        valid = (lo + hidx) >= 0
        acolH64 = np.where(valid, np.sqrt(acolH_base64), 0.0)
        ac64 = np.concatenate([acol4_64, acolH64], axis=1)
        achi, aclo = _hilo(ac64)
        vmask = np.full((128, 1), 1.0 if core > 0 else 0.0, dtype=np.float32)
        m = dict(shared)
        m.update(xq=np.ascontiguousarray(xq), xh=np.ascontiguousarray(xh),
                 achi=achi, aclo=aclo, vmask=vmask)
        in_maps.append(m)
    return in_maps


def kernel(L0_attn_x, L0_attn_W_qkv, L0_attn_b_qkv, L0_ffn_U1, L0_ffn_V1,
           L0_ffn_b1, L0_ffn_U2, L0_ffn_V2, L0_ffn_b2):
    in_maps = make_in_maps(L0_attn_x, L0_attn_W_qkv, L0_attn_b_qkv, L0_ffn_U1,
                           L0_ffn_V1, L0_ffn_b1, L0_ffn_U2, L0_ffn_V2, L0_ffn_b2)
    nc = build_nc()
    res = run_bass_kernel_spmd(nc, in_maps, core_ids=list(range(NCORES)))
    out = np.concatenate([r["y"] for r in res.results], axis=0)
    return out.reshape(1, S, D).astype(np.float32)


# version 27
# speedup vs baseline: 1.0048x; 1.0014x over previous
import math
import numpy as np
import ml_dtypes
import concourse.bass as bass
import concourse.mybir as mybir
from concourse.bass_utils import run_bass_kernel_spmd
from contextlib import ExitStack

F32 = mybir.dt.float32
BF16 = mybir.dt.bfloat16
AF = mybir.ActivationFunctionType
ALU = mybir.AluOpType
f32 = np.float32

AEFF = float(np.float32(0.99))
D = 1024
S = 16384
NCORES = 8
T_CORE = S // NCORES
NR = 4
T_R = T_CORE // NR
W = 512
HALO = 3072
NSMAC = 5
DFF = 4096
RANK = 256


def merge(*tokens):
    w = {}
    for t in tokens:
        if t is None:
            continue
        items = t.items() if isinstance(t, dict) else [t]
        for s, v in items:
            if w.get(s, -1) < v:
                w[s] = v
    return w


class Sched:
    ENGINES = ("pe", "dve", "act", "gp", "sync")
    DMA_SEMS = ("hw", "wtk", "wtv", "wtq", "miscsw", "xh0", "xh1", "xh2", "xh3",
                "xqs", "ffnw", "st2", "st3", "st4", "st5", "st6", "st7", "xh4", "wbq", "wbv", "wbu",
                "st0", "st1", "tabs")

    def __init__(self):
        self.ops = {e: [] for e in self.ENGINES}
        self.cnt = {k: 0 for k in ("pe", "dve", "act", "gp") + self.DMA_SEMS}

    def _emit(self, eng, fn, waits, inc, amt):
        self.ops[eng].append((merge(waits) if waits else {}, fn, inc, amt))
        if inc is None:
            return None
        self.cnt[inc] += amt
        return (inc, self.cnt[inc])

    def pe(self, fn, waits=None, inc=False):
        return self._emit("pe", fn, waits, "pe" if inc else None, 1)

    def dve(self, fn, waits=None):
        return self._emit("dve", fn, waits, "dve", 1)

    def act(self, fn, waits=None):
        return self._emit("act", fn, waits, "act", 1)

    def gp(self, fn, waits=None):
        return self._emit("gp", fn, waits, "gp", 1)

    def gp_dma(self, fn, sem="miscsw", waits=None, gate=False):
        w = merge(waits, {sem: self.cnt[sem]}) if gate else waits
        return self._emit("gp", fn, w, sem, 16)

    def sync_dma(self, fn, waits=None, sem="hw", gate=False):
        w = merge(waits, {sem: self.cnt[sem]}) if gate else waits
        return self._emit("sync", fn, w, sem, 16)

    def act_dma(self, fn, waits=None, sem="hw", gate=False):
        w = merge(waits, {sem: self.cnt[sem]}) if gate else waits
        return self._emit("act", fn, w, sem, 16)

    def wait_only(self, eng, waits):
        return self._emit(eng, None, waits, None, 1)

    def barrier_token(self):
        return dict(self.cnt)


def build_nc():
    nc = bass.Bass("TRN2", target_bir_lowering=False,
                   dynamic_dma_scratch_size=4096, num_swdge_queues=4)
    sch = Sched()

    def dram(name, shape, dtype=F32, kind="ExternalInput"):
        return nc.dram_tensor(name, shape, dtype, kind=kind).ap()

    xq_h = dram("xq", [128, 16384])
    xh_h = dram("xh", [128, 24576])
    wtkv_h = dram("wtkv", [128, 16384])
    wtq_h = dram("wtq", [128, 8192])
    brow_h = dram("brow", [1, 2048])
    bqcol_h = dram("bqcol", [128, 16])
    u1_h = dram("u1", [128, 2048])
    v1_h = dram("v1", [128, 8192])
    u2_h = dram("u2", [128, 8192])
    v2_h = dram("v2", [128, 2048])
    b1c_h = dram("b1c", [128, 32])
    b2col_h = dram("b2col", [128, 8])
    master_h = dram("master", [128, 1408])
    aQhi_h = dram("aQhi", [128, 512])
    aQlo_h = dram("aQlo", [128, 512])
    achi_h = dram("achi", [128, 4 + 4 * NSMAC])
    aclo_h = dram("aclo", [128, 4 + 4 * NSMAC])
    vmask_h = dram("vmask", [128, 1])
    ident_h = dram("ident", [128, 128])
    y_h = dram("y", [T_CORE, D], kind="ExternalOutput")
    wtqb_d = nc.dram_tensor("wtqb", [128, 8192], BF16, kind="Internal").ap()
    v1b_d = nc.dram_tensor("v1b", [128, 8192], BF16, kind="Internal").ap()
    u2b_d = nc.dram_tensor("u2b", [128, 8192], BF16, kind="Internal").ap()

    ctx = ExitStack()
    E = ctx.enter_context

    def sb(name, shape, dtype, side="left"):
        return E(nc.sbuf_tensor(name, shape, dtype, side=side))

    sem = {k: E(nc.semaphore(k)) for k in ("pe", "dve", "act", "gp") + Sched.DMA_SEMS}

    pb = [E(nc.psum_tensor(f"pb{i}", [128, 512], F32)) for i in range(6)]
    ptr = [E(nc.psum_tensor(f"ptr{i}", [128, 1024], BF16)) for i in range(2)]
    bank_free = [None] * 6
    bank_open = [False] * 6
    ptr_free = [None] * 2
    _rr = [0]

    def acquire_bank():
        for _ in range(6):
            i = _rr[0]
            _rr[0] = (i + 1) % 6
            if not bank_open[i]:
                bank_open[i] = True
                return i
        raise RuntimeError("no free psum bank")

    def mm_group(tiles, waits, nf=512, mrows=128, last_waits=None):
        bi = acquire_bank()
        n = len(tiles)
        tok = None
        for i, (lh, rh) in enumerate(tiles):
            w = merge(waits, bank_free[bi]) if i == 0 else None
            if i == n - 1 and last_waits:
                w = merge(w, last_waits)
            tok = sch.pe(
                lambda t, lh=lh, rh=rh, st=(i == 0), sp=(i == n - 1), bi=bi,
                nf=nf, mr=mrows:
                t.matmul(pb[bi][0:mr, 0:nf], lh, rh, start=st, stop=sp),
                waits=w, inc=(i == n - 1))
        bank_open[bi] = False
        return bi, tok

    master_s = sb("master_s", [128, 1408], F32)
    aQrep_s = sb("aQrep_s", [128, 512], F32)
    acol_s = sb("acol_s", [128, 4 + 4 * NSMAC], F32)
    vmask_s = sb("vmask_s", [128, 1], F32)
    ident_s = sb("ident_s", [128, 128], BF16)
    ones_s = sb("ones_s", [1, 512], BF16)
    onescol_s = sb("onescol_s", [128, 1], BF16)
    brow_s = sb("brow_s", [1, 2048], BF16)
    bqcol_s = sb("bqcol_s", [128, 16], F32)
    b1c_s = sb("b1c_s", [128, 32], F32)
    b2col_s = sb("b2col_s", [128, 8], F32)
    rowA_s = sb("rowA_s", [1, 512], BF16)
    rowB_s = sb("rowB_s", [1, 512], BF16)
    rowD_s = sb("rowD_s", [1, 512], BF16)
    colsq_s = sb("colsq_s", [128, 8], F32)
    mcq_s = sb("mcq_s", [128, 4], F32)
    cscq_s = sb("cscq_s", [128, 4], F32)
    vcq_s = sb("vcq_s", [128, 4], F32)
    colsb_s = sb("colsb_s", [128, 8], BF16)
    colacc_s = sb("colacc_s", [128, 16], F32)
    cw_s = sb("cw_s", [128, 8], F32)
    cs_s = sb("cs_s", [128, 2], F32)
    mrep_s = sb("mrep_s", [128, 512], BF16)
    rrep_s = sb("rrep_s", [128, 512], BF16)
    ysq_s = [sb(f"ysq{i}", [128, 512], BF16) for i in range(3)]
    xsc_s = [sb(f"xsc{i}", [128, 512], BF16) for i in range(2)]
    mcol_s = sb("mcol_s", [128, 4], F32)
    rcol_s = sb("rcol_s", [128, 4], F32)
    nu1s_s = sb("nu1s_s", [1, 256], BF16)
    p_s = sb("p_s", [128, 512], BF16)
    q2_s = sb("q2_s", [128, 512], BF16)
    stage_s = [sb(f"stage{i}", [128, 1024], F32) for i in range(4)]
    u1_s = sb("u1_s", [128, 2048], BF16)
    v2_s = sb("v2_s", [128, 2048], BF16)

    wtkv_s = sb("wtkv_s", [128, 16384], BF16, side="right")
    xqr2_s = sb("xqr2_s", [128, 8192], BF16, side="right")
    uacc_s = sb("uacc_s", [128, 8192], BF16, side="right")
    kfm2_s = [sb(f"kfm2_{i}", [128, 4096], BF16, side="right") for i in range(2)]
    vtm2_s = [sb(f"vtm2_{i}", [128, 4096], BF16, side="right") for i in range(2)]

    t_ones = sch.gp(lambda g: g.memset(ones_s[:], 1.0))
    t_onescol = sch.gp(lambda g: g.memset(onescol_s[:], 1.0))

    def ld_sw(dst, src, waits=None, dsem="miscsw", gate=False):
        return sch.gp_dma(lambda g, d=dst, s=src: g.dma_start(out=d, in_=s),
                          sem=dsem, waits=waits, gate=gate)

    def ld_hw(dst, src, waits=None, sem="hw", gate=False):
        return sch.sync_dma(lambda s_, d=dst, s=src: s_.dma_start(out=d, in_=s),
                            waits=waits, sem=sem, gate=gate)

    def ld_hw_act(dst, src, waits=None, sem="hw", gate=False):
        return sch.act_dma(lambda a_, d=dst, s=src: a_.dma_start(out=d, in_=s),
                           waits=waits, sem=sem, gate=gate)

    NA = 4 + 4 * NSMAC
    with nc.sbuf_tensor("tabhi", [128, 536], F32) as tabhi_s, \
         nc.sbuf_tensor("tablo", [128, 536], F32) as tablo_s:
        th2 = ld_hw(tabhi_s[:, 0:NA], achi_h[:], sem="tabs")
        tl2 = ld_hw(tablo_s[:, 0:NA], aclo_h[:], sem="tabs")
        t_acol = sch.dve(lambda v: v.tensor_tensor(
            out=acol_s[:], in0=tabhi_s[:, 0:NA], in1=tablo_s[:, 0:NA],
            op=ALU.mult), waits=merge(th2, tl2))
        th = ld_hw(tabhi_s[:, NA:NA + 512], aQhi_h[:], sem="tabs", gate=True)
        tl = ld_hw(tablo_s[:, NA:NA + 512], aQlo_h[:], sem="tabs")
        t_aQ = sch.dve(lambda v: v.tensor_tensor(
            out=aQrep_s[:], in0=tabhi_s[:, NA:NA + 512],
            in1=tablo_s[:, NA:NA + 512], op=ALU.mult), waits=merge(th, tl))
    t_tab = merge(t_acol, t_aQ)
    t_hw_misc = merge(ld_hw(bqcol_s[:], bqcol_h[:]),
                      ld_hw(b1c_s[:], b1c_h[:]),
                      ld_hw(b2col_s[:], b2col_h[:]),
                      ld_hw(vmask_s[:], vmask_h[:]),
                      ld_hw(master_s[:, 0:704], master_h[:, 0:704]),
                      ld_hw(master_s[:, 704:1408], master_h[:, 704:1408]))

    acolb_s = sb("acolb_s", [128, 4 + 4 * NSMAC], BF16)
    s0_s = sb("s0_s", [1, 1], F32)
    t_acolb = sch.dve(lambda v: v.tensor_copy(acolb_s[:], acol_s[:]),
                      waits=t_acol)
    with nc.sbuf_tensor("xhtm", [128, 20480], BF16, side="right") as xhtm_s, \
         nc.sbuf_tensor("sxx", [128, 8192], BF16, side="right") as sxx_s, \
         nc.sbuf_tensor("hx5", [128, 4096], BF16, side="right") as hx5_s, \
         nc.sbuf_tensor("hrow", [1, 1024], BF16, side="right") as hrow_s, \
         nc.sbuf_tensor("hcol", [128, 8], BF16, side="right") as hcol_s, \
         nc.sbuf_tensor("wkr", [1, 1024], BF16, side="right") as wkr_s:
        xh_cuts = [0, 2048, 7168, 12288, 17408, 20480]
        t_xh = [ld_sw(xhtm_s[:, xh_cuts[i]:xh_cuts[i + 1]],
                      xh_h[:, xh_cuts[i]:xh_cuts[i + 1]], dsem=f"xh{i}")
                for i in range(5)]
        xh_piece = [0, 0] + [1] * 5 + [2] * 5 + [3] * 5 + [4] * 3
        for i in range(2):
            ld_sw(wtkv_s[:, 4096 * i:4096 * (i + 1)],
                  wtkv_h[:, 4096 * i:4096 * (i + 1)], dsem="wtk")
        t_wtk = ("wtk", sch.cnt["wtk"])
        ld_sw(brow_s[:], brow_h[:])
        _t_ident = ld_sw(ident_s[:], ident_h[:], gate=True)
        t_sw_misc1 = ("miscsw", sch.cnt["miscsw"])
        t_consts = merge(t_sw_misc1, t_hw_misc, t_ones, t_onescol, t_tab)
        for i in range(2, 4):
            ld_sw(wtkv_s[:, 4096 * i:4096 * (i + 1)],
                  wtkv_h[:, 4096 * i:4096 * (i + 1)], dsem="wtv")
        t_wtkv = ("wtv", sch.cnt["wtv"])
        t_hx5 = ld_sw(hx5_s[:], xh_h[:, 20480:24576], dsem="xh1", gate=True)
        xq_tok = {}
        for rr in range(2):
            xq_tok[rr] = ld_sw(xqr2_s[:, 4096 * rr:4096 * (rr + 1)],
                               xq_h[:, 4096 * rr:4096 * (rr + 1)],
                               dsem="xqs", gate=True)
        sc_tok = []
        for c in range(20):
            sc_tok.append(sch.act(lambda a, c=c: a.activation(
                out=xhtm_s[:, 1024 * c:1024 * (c + 1)],
                in_=xhtm_s[:, 1024 * c:1024 * (c + 1)], func=AF.Identity,
                scale=acol_s[:, 4 + c:4 + c + 1]),
                waits=merge(t_xh[xh_piece[c]], t_acol)))
        t_xs = sc_tok[-1]
        sxx_tok = {}
        all_gk = [(a, db) for a in range(8) for db in range(2)
                  if not (db == 0 and a >= 4)]
        for p in range(2):
            gks = all_gk[6 * p:6 * (p + 1)]
            bks = {g: acquire_bank() for g in gks}
            mmt = {}
            for c in range(20):
                for a, db in gks:
                    bi = bks[(a, db)]
                    w = merge(sc_tok[c],
                              bank_free[bi] if c == 0 else None)
                    mmt[(a, db)] = sch.pe(
                        lambda t, a=a, db=db, c=c, bi=bi: t.matmul(
                            pb[bi][:],
                            xhtm_s[:, 1024 * c + 128 * a:1024 * c + 128 * (a + 1)],
                            xhtm_s[:, 1024 * c + 512 * db:1024 * c + 512 * (db + 1)],
                            start=(c == 0), stop=(c == 19)),
                        waits=w, inc=(c == 19))
            for a, db in gks:
                bi = bks[(a, db)]
                bank_open[bi] = False
                dst = sxx_s[:, 1024 * a + 512 * db:1024 * a + 512 * (db + 1)]
                ev = sch.act(lambda ac, bi=bi, dst=dst: ac.activation(
                    out=dst, in_=pb[bi][:], func=AF.Identity),
                    waits=mmt[(a, db)])
                sxx_tok[(a, db)] = ev
                bank_free[bi] = ev
        for a in range(4, 8):
            pt = a % 2
            ptok = None
            for j in range(4):
                src = sxx_s[:, 1024 * j + 512 + 128 * (a - 4):
                            1024 * j + 512 + 128 * (a - 3)]
                ptok = sch.pe(lambda t, src=src, j=j, pt=pt: t.transpose(
                    ptr[pt][:, 128 * j:128 * (j + 1)], src, ident_s[:]),
                    waits=merge(sxx_tok[(j, 1)], t_consts, _t_ident,
                                ptr_free[pt] if j == 0 else None),
                    inc=(j == 3))
            ev = sch.act(lambda ac, a=a, pt=pt: ac.activation(
                out=sxx_s[:, 1024 * a:1024 * a + 512], in_=ptr[pt][:, 0:512],
                func=AF.Identity), waits=ptok)
            ptr_free[pt] = ev
            sxx_tok[(a, 0)] = ev
        sxx_ready = merge(*sxx_tok.values())
        hr_tok = []
        for db in range(2):
            tiles = [(acolb_s[:, 4 + c:4 + c + 1],
                      xhtm_s[:, 1024 * c + 512 * db:1024 * c + 512 * (db + 1)])
                     for c in range(20)]
            bi, tok = mm_group(tiles, merge(t_xs, t_acolb), mrows=1)
            ev = sch.dve(lambda v, bi=bi, db=db: v.tensor_copy(
                hrow_s[0:1, 512 * db:512 * (db + 1)], pb[bi][0:1, :]), waits=tok)
            hr_tok.append(ev)
            bank_free[bi] = ev
        tiles = [(acolb_s[:, 4 + c:4 + c + 1], acolb_s[:, 4 + c:4 + c + 1])
                 for c in range(20)]
        bi, tok = mm_group(tiles, t_acolb, nf=1, mrows=1)
        t_s0 = sch.dve(lambda v, bi=bi: v.tensor_copy(
            s0_s[0:1, 0:1], pb[bi][0:1, 0:1]), waits=tok)
        bank_free[bi] = t_s0
        t_hrow = merge(*hr_tok)
        halo_gate = merge({"pe": sch.cnt["pe"]},
                          *[(f"xh{i}", sch.cnt[f"xh{i}"]) for i in range(5)])
        hc_tok = []
        for c in range(8):
            bi, tok = mm_group([(hrow_s[0:1, 128 * c:128 * (c + 1)],
                                 ones_s[0:1, 0:1])], merge(t_hrow, t_ones), nf=1)
            ev = sch.dve(lambda v, bi=bi, c=c: v.tensor_copy(
                hcol_s[:, c:c + 1], pb[bi][0:128, 0:1]), waits=tok)
            hc_tok.append(ev)
            bank_free[bi] = ev
        t_hcol = merge(*hc_tok)
        wk_tok = []
        for db in range(2):
            tiles = [(hcol_s[:, c:c + 1],
                      wtkv_s[:, 1024 * c + 512 * db:1024 * c + 512 * (db + 1)])
                     for c in range(8)]
            bi, tok = mm_group(tiles, merge(t_hcol, t_wtk), mrows=1)
            ev = sch.dve(lambda v, bi=bi, db=db: v.tensor_copy(
                wkr_s[0:1, 512 * db:512 * (db + 1)], pb[bi][0:1, :]), waits=tok)
            wk_tok.append(ev)
            bank_free[bi] = ev
        t_wkr = sch.dve(lambda v: v.scalar_tensor_tensor(
            out=wkr_s[0:1, :], in0=brow_s[0:1, 0:1024], scalar=s0_s[0:1, 0:1],
            in1=wkr_s[0:1, :], op0=ALU.mult, op1=ALU.add),
            waits=merge(t_s0, t_sw_misc1, *wk_tok))

        init_tok = {}
        last5 = None
        for ot in range(8):
            tiles = [(wtkv_s[:, 1024 * c + 128 * ot:1024 * c + 128 * (ot + 1)],
                      hx5_s[:, 512 * c:512 * (c + 1)]) for c in range(8)]
            bi, tok = mm_group(tiles, merge(t_hx5, t_wtk))
            last5 = tok
            dst = kfm2_s[1][:, 512 * ot:512 * (ot + 1)]
            ev = sch.act(lambda a, bi=bi, dst=dst, ot=ot: a.activation(
                out=dst, in_=pb[bi][:], func=AF.Identity,
                bias=bqcol_s[:, 8 + ot:8 + ot + 1]), waits=merge(tok, t_hw_misc))
            init_tok[("k", ot)] = ev
            bank_free[bi] = ev
        for tt in range(4):
            for db in range(2):
                tiles = [(hx5_s[:, 512 * c + 128 * tt:512 * c + 128 * (tt + 1)],
                          wtkv_s[:, 8192 + 1024 * c + 512 * db:8192 + 1024 * c + 512 * (db + 1)])
                         for c in range(8)]
                tiles.append((ones_s[0:1, 0:128],
                              brow_s[0:1, 1024 + 512 * db:1024 + 512 * (db + 1)]))
                bi, tok = mm_group(tiles, merge(t_hx5, t_wtkv, t_ones, t_sw_misc1))
                last5 = tok
                dst = vtm2_s[1][:, 1024 * tt + 512 * db:1024 * tt + 512 * (db + 1)]
                ev = sch.act(lambda a, bi=bi, dst=dst: a.activation(
                    out=dst, in_=pb[bi][:], func=AF.Identity,
                    scale=vmask_s[:, 0:1]), waits=merge(tok, t_hw_misc))
                init_tok[("v", tt, db)] = ev
                bank_free[bi] = ev
        init_ready = merge(*init_tok.values())

        m1_half = [kfm2_s[0], vtm2_s[0]]
        m1_tok = {}
        for ap2 in range(8):
            for db in range(2):
                tiles = [(sxx_s[:, 1024 * c + 128 * ap2:1024 * c + 128 * (ap2 + 1)],
                          wtkv_s[:, 1024 * c + 512 * db:1024 * c + 512 * (db + 1)])
                         for c in range(8)]
                tiles.append((hrow_s[0:1, 128 * ap2:128 * (ap2 + 1)],
                              brow_s[0:1, 512 * db:512 * (db + 1)]))
                bi, tok = mm_group(tiles, merge(sxx_ready, t_hrow, t_wtk,
                                                t_sw_misc1))
                dst = m1_half[db][:, 512 * ap2:512 * (ap2 + 1)]
                ev = sch.act(lambda a, bi=bi, dst=dst: a.activation(
                    out=dst, in_=pb[bi][:], func=AF.Identity), waits=tok)
                m1_tok[(ap2, db)] = ev
                bank_free[bi] = ev
        m1_ready = merge(*m1_tok.values())
        uacc_tok = {}
        m1_cons = None
        for a in range(8):
            dbk, ak = a // 4, a % 4
            for db in range(2):
                tiles = [(m1_half[dbk][:, 512 * c + 128 * ak:512 * c + 128 * (ak + 1)],
                          wtkv_s[:, 8192 + 1024 * c + 512 * db:8192 + 1024 * c + 512 * (db + 1)])
                         for c in range(8)]
                tiles.append((wkr_s[0:1, 128 * a:128 * (a + 1)],
                              brow_s[0:1, 1024 + 512 * db:1024 + 512 * (db + 1)]))
                bi, tok = mm_group(tiles, merge(m1_ready, t_wkr, t_wtkv))
                m1_cons = merge(m1_cons, tok)
                dst = uacc_s[:, 1024 * a + 512 * db:1024 * a + 512 * (db + 1)]
                ev = sch.act(lambda ac, bi=bi, dst=dst: ac.activation(
                    out=dst, in_=pb[bi][:], func=AF.Identity), waits=tok)
                uacc_tok[(a, db)] = ev
                bank_free[bi] = ev
        uacc_ready = merge(*uacc_tok.values())
        kfm_cons0_init = m1_cons
    t_consts = merge(t_consts, _t_ident, ("miscsw", sch.cnt["miscsw"]))
    t_u1 = ld_sw(u1_s[:], u1_h[:], gate=True)
    t_v2 = ld_sw(v2_s[:], v2_h[:], gate=True)
    t_consts = merge(t_consts, t_u1, t_v2)
    bi_us, tok_us = mm_group([(onescol_s[:], u1_s[:, 256 * c:256 * (c + 1)])
                              for c in range(8)],
                             merge(t_u1, t_ones, t_onescol), nf=256, mrows=1)
    t_nu1 = sch.act(lambda a, bi=bi_us: a.activation(
        out=nu1s_s[0:1, :], in_=pb[bi][0:1, 0:256], func=AF.Identity,
        scale=f32(-1.0 / 1024.0)), waits=tok_us)
    bank_free[bi_us] = t_nu1

    ln_cons = [None]
    xq_cons = [None, None]
    wb_tok = {}
    p_cons = [None]
    q2_cons = [None]
    h_cons = [None]
    stage_cons = [None] * 4
    xsc_cons = [None, None]
    kfm_tokens = [None, init_ready]
    vtm_tokens = [None, init_ready]
    kfm_cons = [kfm_cons0_init, None]
    vtm_cons = [kfm_cons0_init, None]
    u_ready = [uacc_ready]

    def ln_stats(src_s, src_ready, bw, reps=True, cols=False, mid=None,
                 chunk_tok=None):
        guard = ln_cons[0]
        ctok = chunk_tok or {}
        banks = {}
        bsum = acquire_bank()
        tok = None
        for c in range(8):
            w = merge(ctok.get(c, src_ready), t_consts,
                      bank_free[bsum] if c == 0 else None)
            tok = sch.pe(lambda t, c=c, bi=bsum, ss=src_s: t.matmul(
                pb[bi][0:1, 0:512], onescol_s[:], ss[:, 512 * c:512 * (c + 1)],
                start=(c == 0), stop=(c == 7)), waits=w, inc=(c == 7))
        bank_open[bsum] = False
        banks[0] = (bsum, tok)
        sq_tok = {}
        mmsq_tok = {}
        bsq = acquire_bank()
        for c in range(8):
            dst = ysq_s[c % 3]
            sq_tok[c] = sch.act(
                lambda a, c=c, dst=dst, ss=src_s: a.activation(
                    out=dst[:], in_=ss[:, 512 * c:512 * (c + 1)], func=AF.Square),
                waits=merge(ctok.get(c, src_ready), guard, mmsq_tok.get(c - 3)))
            w = merge(sq_tok[c], bank_free[bsq] if c == 0 else None, t_consts)
            mmsq_tok[c] = sch.pe(
                lambda t, c=c, bi=bsq: t.matmul(
                    pb[bi][0:1, 0:512], onescol_s[:], ysq_s[c % 3][:],
                    start=(c == 0), stop=(c == 7)),
                waits=w, inc=True)
        bank_open[bsq] = False
        banks[1] = (bsq, mmsq_tok[7])
        (bsumi, toks), (bsqi, tokq) = banks[0], banks[1]
        evA = sch.dve(lambda v, bi=bsumi: v.tensor_copy(
            rowA_s[0:1, :], pb[bi][0:1, :]), waits=merge(toks, guard))
        bank_free[bsumi] = evA
        evB = sch.dve(lambda v, bi=bsqi: v.tensor_copy(
            rowB_s[0:1, :], pb[bi][0:1, :]), waits=merge(tokq, guard))
        bank_free[bsqi] = evB
        bcol = acquire_bank()
        ctok = None
        for j in range(8):
            row, rtok = (rowA_s, evA) if j < 4 else (rowB_s, evB)
            jj = j % 4
            w = merge(rtok, t_ones, bank_free[bcol] if j == 0 else None)
            ctok = sch.pe(lambda t, j=j, row=row, jj=jj, bi=bcol: t.matmul(
                pb[bi][0:128, j:j + 1], row[0:1, 128 * jj:128 * (jj + 1)],
                ones_s[0:1, 0:1], start=True, stop=True), waits=w,
                inc=(j == 7))
        bank_open[bcol] = False
        evC = sch.dve(lambda v, bi=bcol: v.tensor_copy(
            colsq_s[:], pb[bi][0:128, 0:8]), waits=merge(ctok, guard))
        bank_free[bcol] = evC
        t1c = sch.dve(lambda v: v.tensor_scalar(
            out=mcq_s[:], in0=colsq_s[:, 0:4], scalar1=float(1.0 / 1024),
            scalar2=None, op0=ALU.mult), waits=evC)
        t2c = sch.dve(lambda v: v.tensor_tensor(
            out=cscq_s[:], in0=mcq_s[:], in1=mcq_s[:], op=ALU.mult), waits=t1c)
        t3c = sch.dve(lambda v: v.scalar_tensor_tensor(
            out=vcq_s[:], in0=colsq_s[:, 4:8], scalar=float(1.0 / 1024),
            in1=cscq_s[:], op0=ALU.mult, op1=ALU.subtract),
            waits=merge(t2c, evC))
        t4c = sch.act(lambda a: a.activation(
            out=cscq_s[:], in_=vcq_s[:], func=AF.Sqrt, bias=f32(1e-5)),
            waits=t3c)
        t5c = sch.dve(lambda v: v.reciprocal(out=vcq_s[:], in_=cscq_s[:]),
                      waits=t4c)
        cb1 = sch.dve(lambda v: v.tensor_copy(colsb_s[:, 0:4], mcq_s[:]),
                      waits=merge(t1c, guard))
        cb2 = sch.dve(lambda v: v.tensor_copy(colsb_s[:, 4:8], vcq_s[:]),
                      waits=merge(t5c, guard))
        out = {"t1": evA, "sq": merge(*sq_tok.values())}
        if reps:
            brow1 = acquire_bank()
            brow2 = acquire_bank()
            rtok1 = rtok2 = None
            for j in range(8):
                bi = brow1 if j < 4 else brow2
                w = merge(cb1 if j < 4 else cb2, t_consts,
                          bank_free[bi] if j % 4 == 0 else None)
                tok = sch.pe(lambda t, j=j, bi=bi: t.matmul(
                    pb[bi][0:1, 128 * (j % 4):128 * (j % 4 + 1)],
                    colsb_s[:, j:j + 1], ident_s[:, 0:128],
                    start=True, stop=True), waits=w, inc=(j % 4 == 3))
                if j == 3:
                    rtok1 = tok
                elif j == 7:
                    rtok2 = tok
            bank_open[brow1] = False
            bank_open[brow2] = False
            evD = sch.dve(lambda v, bi=brow1: v.tensor_copy(
                rowD_s[0:1, :], pb[bi][0:1, 0:512]), waits=merge(rtok1, guard))
            evE = sch.dve(lambda v, bi=brow2: v.tensor_copy(
                rowB_s[0:1, :], pb[bi][0:1, 0:512]), waits=merge(rtok2, guard))
            bank_free[brow1] = evD
            bank_free[brow2] = evE
            bi6, tok6 = mm_group([(ones_s[0:1, 0:128], rowD_s[0:1, :])],
                                 merge(evD, guard, t_consts), nf=512)
            t6 = sch.act(lambda a, bi=bi6: a.activation(
                out=mrep_s[:], in_=pb[bi][:], func=AF.Identity),
                waits=merge(tok6, guard))
            bank_free[bi6] = t6
            bi7, tok7 = mm_group([(ones_s[0:1, 0:128], rowB_s[0:1, :])],
                                 merge(evE, guard), nf=512)
            t7 = sch.act(lambda a, bi=bi7: a.activation(
                out=rrep_s[:], in_=pb[bi][:], func=AF.Identity),
                waits=merge(tok7, guard))
            bank_free[bi7] = t7
            out["reps"] = merge(t6, t7)
        if mid is not None:
            mid()
        return out

    for r in range(NR):
        par = r % 2
        prv = 1 - par
        def bw(eng):
            return None

        with nc.sbuf_tensor(f"y1_{r}", [128, 4096], BF16) as y1_s:
            with nc.sbuf_tensor(f"yraw{r}", [128, 4096], BF16, side="right") as yraw_s:
              with nc.sbuf_tensor(f"wtq{r}", [128, 8192], BF16, side="right") as wtq_s, \
                   nc.sbuf_tensor(f"ktms{r}", [128, 4096], BF16, side="right") as ktms_s:
                gw = merge(p_cons[0], q2_cons[0], h_cons[0],
                           halo_gate if r == 0 else None)
                xqr_s = xqr2_s[:, 4096 * par:4096 * (par + 1)]
                t_xq = xq_tok[r]
                if r < 3:
                    xq_tok[r + 1] = ld_sw(
                        xqr2_s[:, 4096 * ((r + 1) % 2):4096 * ((r + 1) % 2 + 1)],
                        xq_h[:, 4096 * (r + 1):4096 * (r + 2)],
                        waits=xq_cons[(r + 1) % 2], dsem="xqs", gate=True)
                if r == 0:
                    for i in range(2):
                        ld_sw(wtq_s[:, 4096 * i:4096 * (i + 1)],
                              wtq_h[:, 4096 * i:4096 * (i + 1)],
                              waits=gw if i == 0 else None, dsem="wtq",
                              gate=(i == 0))
                    t_wtq = ("wtq", sch.cnt["wtq"])
                    wb_tok["wtq"] = ld_hw(wtqb_d, wtq_s[:], waits=t_wtq,
                                          sem="wbq", gate=True)
                else:
                    t_wtq = ld_hw_act(wtq_s[:], wtqb_d,
                                      waits=merge(p_cons[0],
                                                  wb_tok["v1"] if r == 1 else None),
                                      sem="wbq", gate=True)

                with nc.sbuf_tensor(f"abuf{r}", [128, 4096], BF16) as abuf_s, \
                     nc.sbuf_tensor(f"qfm{r}", [128, 4096], BF16) as qfm_s:
                    qfm_tok, kfm_tok, vtm_tok = {}, {}, {}
                    for ot in range(8):
                        tiles = [(wtkv_s[:, 1024 * c + 128 * ot:1024 * c + 128 * (ot + 1)],
                                  xqr_s[:, 512 * c:512 * (c + 1)]) for c in range(8)]
                        bi, tok = mm_group(tiles, merge(t_xq, t_wtk, kfm_cons[par]))
                        dst = kfm2_s[par][:, 512 * ot:512 * (ot + 1)]
                        ev = sch.act(lambda a, bi=bi, dst=dst, ot=ot, par=par:
                                     a.activation(
                                         out=dst, in_=pb[bi][:], func=AF.Identity,
                                         bias=bqcol_s[:, 8 + ot:8 + ot + 1]),
                                     waits=tok)
                        kfm_tok[ot] = ev
                        bank_free[bi] = ev
                    for tt in range(4):
                        for db in range(2):
                            tiles = [(xqr_s[:, 512 * c + 128 * tt:512 * c + 128 * (tt + 1)],
                                      wtkv_s[:, 8192 + 1024 * c + 512 * db:8192 + 1024 * c + 512 * (db + 1)])
                                     for c in range(8)]
                            tiles.append((ones_s[0:1, 0:128],
                                          brow_s[0:1, 1024 + 512 * db:1024 + 512 * (db + 1)]))
                            bi, tok = mm_group(tiles, merge(t_xq, t_wtkv, vtm_cons[par]))
                            dst = vtm2_s[par][:, 1024 * tt + 512 * db:1024 * tt + 512 * (db + 1)]
                            ev = sch.act(lambda a, bi=bi, dst=dst: a.activation(
                                out=dst, in_=pb[bi][:], func=AF.Identity),
                                waits=tok)
                            vtm_tok[(tt, db)] = ev
                            bank_free[bi] = ev
                    for ot in range(8):
                        tiles = [(wtq_s[:, 1024 * c + 128 * ot:1024 * c + 128 * (ot + 1)],
                                  xqr_s[:, 512 * c:512 * (c + 1)]) for c in range(8)]
                        bi, tok = mm_group(tiles, merge(t_xq, t_wtq, t_consts))
                        dst = qfm_s[:, 512 * ot:512 * (ot + 1)]
                        ev = sch.act(lambda a, bi=bi, dst=dst, ot=ot: a.activation(
                            out=dst, in_=pb[bi][:], func=AF.Identity,
                            bias=bqcol_s[:, ot:ot + 1]),
                            waits=merge(tok, bw("act")))
                        qfm_tok[ot] = ev
                        bank_free[bi] = ev
                    qfm_ready = merge(*qfm_tok.values())
                    kfm_ready = merge(*kfm_tok.values())
                    vtm_ready = merge(*vtm_tok.values())
                    kfm_tokens[par] = kfm_ready
                    vtm_tokens[par] = vtm_ready
                    t_s1_pe = {"pe": sch.cnt["pe"]}

                    a_tok = {}
                    for jt in range(8):
                        src = kfm2_s[prv] if jt < 4 else kfm2_s[par]
                        src_tok = kfm_tokens[prv] if jt < 4 else kfm_ready
                        jj = jt % 4
                        tiles = [(src[:, 512 * c + 128 * jj:512 * c + 128 * (jj + 1)],
                                  qfm_s[:, 512 * c:512 * (c + 1)]) for c in range(8)]
                        bi, tok = mm_group(tiles, merge(src_tok, qfm_ready))
                        if jt < 4:
                            kfm_cons[prv] = merge(kfm_cons[prv], tok)
                        dst = abuf_s[:, 512 * jt:512 * (jt + 1)]
                        c0 = 896 - 128 * jt
                        ev = sch.dve(lambda v, bi=bi, dst=dst, c0=c0:
                                     v.tensor_tensor(out=dst, in0=pb[bi][:],
                                                     in1=master_s[:, c0:c0 + 512],
                                                     op=ALU.mult), waits=tok)
                        a_tok[jt] = ev
                        bank_free[bi] = ev
                    a_ready = merge(*a_tok.values())

                    yraw_tok = {}
                    last_cross_pe = None
                    for dv in range(8):
                        tiles = []
                        for jt in range(8):
                            vsrc = vtm2_s[prv] if jt < 4 else vtm2_s[par]
                            jj = jt % 4
                            tiles.append((vsrc[:, 1024 * jj + 128 * dv:1024 * jj + 128 * (dv + 1)],
                                          abuf_s[:, 512 * jt:512 * (jt + 1)]))
                        biA, tokA = mm_group(tiles, merge(a_ready, vtm_tokens[prv],
                                                          vtm_ready))
                        vtm_cons[prv] = merge(vtm_cons[prv], tokA)
                        tiles = [(uacc_s[:, 1024 * a + 128 * dv:1024 * a + 128 * (dv + 1)],
                                  qfm_s[:, 512 * a:512 * (a + 1)]) for a in range(8)]
                        biB, tokB = mm_group(tiles, merge(u_ready[0], qfm_ready))
                        last_cross_pe = merge(last_cross_pe, tokB)
                        xb = dv % 2
                        s1 = sch.dve(lambda v, biB=biB, xb=xb: v.tensor_tensor(
                            out=xsc_s[xb][:], in0=pb[biB][:], in1=aQrep_s[:],
                            op=ALU.mult), waits=merge(tokB, xsc_cons[xb]))
                        bank_free[biB] = s1
                        s2 = sch.dve(lambda v, xq=xqr_s, dv=dv, xb=xb: v.tensor_tensor(
                            out=xsc_s[xb][:], in0=xsc_s[xb][:],
                            in1=xq[:, 512 * dv:512 * (dv + 1)], op=ALU.add),
                            waits=merge(s1, t_xq))
                        dst = yraw_s[:, 512 * dv:512 * (dv + 1)]
                        s3 = sch.dve(lambda v, biA=biA, dst=dst, xb=xb: v.tensor_tensor(
                            out=dst, in0=pb[biA][:], in1=xsc_s[xb][:], op=ALU.add),
                            waits=merge(tokA, s2))
                        yraw_tok[dv] = s3
                        xsc_cons[xb] = s3
                        bank_free[biA] = s3
                    yraw_ready = merge(*yraw_tok.values())

                    def s2_emit(prv=prv, lcp=None):
                        ktm_tok = {}
                        for tt in range(4):
                            for g in range(2):
                                pt = (2 * tt + g) % 2
                                tok = None
                                for j in range(4):
                                    dc = 4 * g + j
                                    tok = sch.pe(lambda t, pt=pt, dc=dc, tt=tt, j=j, prv=prv:
                                                 t.transpose(
                                                     ptr[pt][:, 128 * j:128 * (j + 1)],
                                                     kfm2_s[prv][:, 512 * dc + 128 * tt:512 * dc + 128 * (tt + 1)],
                                                     ident_s[:]),
                                                 waits=merge(kfm_tokens[prv],
                                                             ptr_free[pt] if j == 0 else None,
                                                             t_consts), inc=(j == 3))
                                kfm_cons[prv] = merge(kfm_cons[prv], tok)
                                dst = ktms_s[:, 1024 * tt + 512 * g:1024 * tt + 512 * (g + 1)]
                                if g == 0:
                                    ev = sch.act(lambda a, pt=pt, dst=dst, tt=tt:
                                                 a.activation(
                                                     out=dst, in_=ptr[pt][:, 0:512],
                                                     func=AF.Identity,
                                                     scale=acol_s[:, tt:tt + 1]),
                                                 waits=merge(tok, wb_tok.get("u2") if r == 1 else None))
                                else:
                                    ev = sch.dve(lambda v, pt=pt, dst=dst, tt=tt:
                                                 v.tensor_scalar(
                                                     out=dst, in0=ptr[pt][:, 0:512],
                                                     scalar1=acol_s[:, tt:tt + 1],
                                                     scalar2=None, op0=ALU.mult),
                                                 waits=merge(tok, wb_tok.get("u2") if r == 1 else None))
                                ptr_free[pt] = ev
                                ktm_tok[(tt, g)] = ev
                        ktm_ready = merge(*ktm_tok.values())
                        u2_tok = {}
                        for a in range(8):
                            for db in range(2):
                                tiles = [(ktms_s[:, 1024 * tt + 128 * a:1024 * tt + 128 * (a + 1)],
                                          vtm2_s[prv][:, 1024 * tt + 512 * db:1024 * tt + 512 * (db + 1)])
                                         for tt in range(4)]
                                bi, tok = mm_group(tiles, merge(ktm_ready, vtm_tokens[prv],
                                                                lcp))
                                vtm_cons[prv] = merge(vtm_cons[prv], tok)
                                dst = uacc_s[:, 1024 * a + 512 * db:1024 * a + 512 * (db + 1)]
                                ev = sch.dve(lambda v, bi=bi, dst=dst: v.tensor_tensor(
                                    out=dst, in0=pb[bi][:], in1=dst, op=ALU.add),
                                    waits=merge(tok, u_ready[0], lcp))
                                u2_tok[(a, db)] = ev
                                bank_free[bi] = ev
                        _chi = float(np.float32(AEFF ** W))
                        _clo = float(np.float32(AEFF ** W / np.float64(np.float32(AEFF ** W))))
                        t_us = sch.dve(lambda v: v.tensor_scalar(
                            out=uacc_s[:], in0=uacc_s[:], scalar1=_chi,
                            scalar2=_clo, op0=ALU.mult, op1=ALU.mult),
                            waits=merge(*u2_tok.values()))
                        u_ready[0] = t_us

                    pbis = {}

                    def emit_px():
                        for tb in range(2):
                            for rt in range(2):
                                bi = acquire_bank()
                                pbis[(tb, rt)] = bi
                                for i in range(8):
                                    w = merge(yraw_ready, t_consts,
                                              bank_free[bi]) if i == 0 else None
                                    sch.pe(lambda t, bi=bi, rt=rt, tb=tb, i=i, ys=yraw_s:
                                           t.matmul(
                                               pb[bi][:, 0:256],
                                               u1_s[:, 256 * i + 128 * rt:256 * i + 128 * (rt + 1)],
                                               ys[:, 512 * i + 256 * tb:512 * i + 256 * (tb + 1)],
                                               start=(i == 0), stop=False), waits=w)

                    def mid_all(prv=prv, lcp=last_cross_pe, do_s2=(r < NR - 1)):
                        if do_s2:
                            s2_emit(prv, lcp)
                        emit_px()

                    ln1 = ln_stats(yraw_s, yraw_ready, bw, reps=True, mid=mid_all,
                                   chunk_tok=yraw_tok)
                    reps_tok, sq_done = ln1["reps"], ln1["sq"]
                    w_s3done = {"pe": sch.cnt["pe"], "dve": sch.cnt["dve"]}
                    xq_cons[par] = w_s3done

              with nc.sbuf_tensor(f"v1s{r}", [128, 8192], BF16, side="right") as v1_s, \
                   nc.sbuf_tensor(f"u2s{r}", [128, 8192], BF16, side="right") as u2_s, \
                   nc.sbuf_tensor(f"h{r}", [128, 8192], BF16, side="right") as h_s, \
                   nc.sbuf_tensor(f"out2_{r}", [128, 4096], BF16, side="right") as out2_s:
                  ww2 = {"pe": sch.cnt["pe"], "dve": sch.cnt["dve"],
                         "act": sch.cnt["act"]}
                  if r == 0:
                      ld_sw(v1_s[:, 0:4096], v1_h[:, 0:4096],
                            waits=merge(t_s1_pe, wb_tok["wtq"]),
                            dsem="ffnw", gate=True)
                      ld_sw(v1_s[:, 4096:8192], v1_h[:, 4096:8192], dsem="ffnw")
                      t_v1 = ("ffnw", sch.cnt["ffnw"])
                      wb_tok["v1"] = ld_hw(v1b_d, v1_s[:], waits=t_v1,
                                           sem="wbv", gate=True)
                      ld_sw(u2_s[:, 0:4096], u2_h[:, 0:4096], waits=ww2,
                            dsem="ffnw", gate=True)
                      ld_sw(u2_s[:, 4096:8192], u2_h[:, 4096:8192], dsem="ffnw")
                      t_ffnw = ("ffnw", sch.cnt["ffnw"])
                      wb_tok["u2"] = ld_hw(u2b_d, u2_s[:], waits=t_ffnw,
                                           sem="wbu", gate=True)
                  else:
                      t_v1 = ld_hw_act(v1_s[:], v1b_d,
                                       waits=merge(t_s1_pe, wb_tok["v1"]),
                                       sem="wbv", gate=True)
                      t_ffnw = ld_hw_act(u2_s[:], u2b_d,
                                         waits=merge(ww2, wb_tok["u2"]),
                                         sem="wbu", gate=True)
                  out2_tok = {}
                  for tb in range(2):
                      p_tok = {}
                      for rt in range(2):
                          bi = pbis[(tb, rt)]
                          tok = sch.pe(lambda t, bi=bi, rt=rt, tb=tb: t.matmul(
                              pb[bi][:, 0:256],
                              nu1s_s[0:1, 128 * rt:128 * (rt + 1)],
                              rowA_s[0:1, 256 * tb:256 * (tb + 1)],
                              start=False, stop=True),
                              waits=merge(ln1["t1"], t_nu1) if (tb, rt) == (0, 0) else None,
                              inc=True)
                          bank_open[bi] = False
                          dst = p_s[:, 256 * rt:256 * (rt + 1)]
                          ev = sch.dve(lambda v, bi=bi, dst=dst, tb=tb: v.tensor_tensor(
                              out=dst, in0=pb[bi][:, 0:256],
                              in1=rrep_s[:, 256 * tb:256 * (tb + 1)], op=ALU.mult),
                              waits=merge(tok, p_cons[0], reps_tok))
                          p_tok[rt] = ev
                          bank_free[bi] = ev
                      p_ready = merge(*p_tok.values())
                      if tb == 0:
                          y1_tok = {}
                          for c in range(8):
                              dtmp = ysq_s[c % 3]
                              ta = sch.dve(lambda v, c=c, dtmp=dtmp, ys=yraw_s:
                                           v.tensor_tensor(
                                               out=dtmp[:],
                                               in0=ys[:, 512 * c:512 * (c + 1)],
                                               in1=mrep_s[:], op=ALU.subtract),
                                           waits=merge(reps_tok, sq_done,
                                                       y1_tok.get(c - 4)))
                              y1_tok[c] = sch.dve(lambda v, c=c, dtmp=dtmp, y1s=y1_s:
                                                  v.tensor_tensor(
                                                      out=y1s[:, 512 * c:512 * (c + 1)],
                                                      in0=dtmp[:], in1=rrep_s[:],
                                                      op=ALU.mult), waits=ta)
                          y1_ready = merge(*y1_tok.values())
                          ln_cons[0] = y1_ready
                      h_ev = None
                      for ft in range(32):
                          tiles = [(v1_s[:, 4096 * rc + 128 * ft:4096 * rc + 128 * (ft + 1)],
                                    p_s[:, 256 * rc:256 * (rc + 1)]) for rc in range(2)]
                          bi, tok = mm_group(tiles, merge(p_ready, h_cons[0], t_v1), nf=256)
                          p_cons[0] = merge(p_cons[0], tok)
                          dst = h_s[:, 256 * ft:256 * (ft + 1)]
                          h_ev = sch.act(lambda a, bi=bi, dst=dst, ft=ft: a.activation(
                              out=dst, in_=pb[bi][:, 0:256], func=AF.Gelu_apprx_tanh,
                              bias=b1c_s[:, ft:ft + 1]), waits=tok)
                          bank_free[bi] = h_ev
                      q2_tok = {}
                      for rt in range(2):
                          tiles = [(u2_s[:, 256 * fc + 128 * rt:256 * fc + 128 * (rt + 1)],
                                    h_s[:, 256 * fc:256 * (fc + 1)]) for fc in range(32)]
                          bi, tok = mm_group(tiles, merge(h_ev, q2_cons[0], t_ffnw), nf=256)
                          h_cons[0] = merge(h_cons[0], tok)
                          dst = q2_s[:, 256 * rt:256 * (rt + 1)]
                          ev = sch.act(lambda a, bi=bi, dst=dst: a.activation(
                              out=dst, in_=pb[bi][:, 0:256], func=AF.Identity),
                              waits=tok)
                          q2_tok[rt] = ev
                          bank_free[bi] = ev
                      q2_ready = merge(*q2_tok.values())
                      for dt in range(8):
                          tiles = [(v2_s[:, 1024 * rc + 128 * dt:1024 * rc + 128 * (dt + 1)],
                                    q2_s[:, 256 * rc:256 * (rc + 1)]) for rc in range(2)]
                          bi, tok = mm_group(tiles, q2_ready, nf=256)
                          q2_cons[0] = merge(q2_cons[0], tok)
                          dst = out2_s[:, 512 * dt + 256 * tb:512 * dt + 256 * (tb + 1)]
                          yin = y1_s[:, 512 * dt + 256 * tb:512 * dt + 256 * (tb + 1)]
                          ev = sch.dve(lambda v, bi=bi, dst=dst, yin=yin, dt=dt:
                                       v.scalar_tensor_tensor(
                                           out=dst, in0=pb[bi][:, 0:256],
                                           scalar=b2col_s[:, dt:dt + 1],
                                           in1=yin, op0=ALU.add, op1=ALU.add),
                                       waits=tok)
                          out2_tok[(tb, dt)] = ev
                          bank_free[bi] = ev
                  out2_ready = merge(*out2_tok.values())

                  s5_ev = {}
                  s5_sq_all = []
                  s5_nrm_all = []

                  def s5_tile(tt):
                      sg = tt
                      o = 4 * (tt % 2)
                      ev_all = []
                      sq_all = []
                      for g in range(2):
                          pt = g % 2
                          tok = None
                          for j in range(4):
                              dc = 4 * g + j
                              tok = sch.pe(lambda t, pt=pt, dc=dc, tt=tt, j=j, o2=out2_s:
                                           t.transpose(
                                               ptr[pt][:, 128 * j:128 * (j + 1)],
                                               o2[:, 512 * dc + 128 * tt:512 * dc + 128 * (tt + 1)],
                                               ident_s[:]),
                                           waits=merge(out2_tok[(tt // 2, dc)],
                                                       ptr_free[pt] if j == 0 else None,
                                                       t_consts),
                                           inc=(j == 3))
                          ev = sch.act(lambda a, pt=pt, g=g, sg=sg, tt=tt: a.activation(
                              out=stage_s[sg][:, 512 * g:512 * (g + 1)],
                              in_=ptr[pt][:, 0:512], func=AF.Identity,
                              accum_out=colacc_s[:, 4 * tt + g:4 * tt + g + 1]),
                              waits=merge(tok, stage_cons[sg]))
                          sq = sch.act(lambda a, pt=pt, g=g, tt=tt: a.activation(
                              out=ysq_s[g][:], in_=ptr[pt][:, 0:512], func=AF.Square,
                              accum_out=colacc_s[:, 4 * tt + 2 + g:4 * tt + 3 + g]),
                              waits=merge(tok, ln1["t1"]))
                          ptr_free[pt] = merge(ev, sq)
                          ev_all.append(ev)
                          sq_all.append(sq)
                      s5_ev[tt] = merge(*ev_all)
                      s5_sq_all.extend(sq_all)
                      ta = sch.dve(lambda v, tt=tt, o=o: v.tensor_tensor(
                          out=cw_s[:, o:o + 1], in0=colacc_s[:, 4 * tt:4 * tt + 1],
                          in1=colacc_s[:, 4 * tt + 1:4 * tt + 2], op=ALU.add),
                          waits=s5_ev[tt])
                      tb2 = sch.dve(lambda v, tt=tt, o=o: v.tensor_scalar(
                          out=mcol_s[:, tt:tt + 1], in0=cw_s[:, o:o + 1],
                          scalar1=float(-1.0 / 1024), scalar2=None, op0=ALU.mult),
                          waits=ta)
                      tc = sch.dve(lambda v, tt=tt, o=o: v.tensor_tensor(
                          out=cw_s[:, o + 1:o + 2], in0=colacc_s[:, 4 * tt + 2:4 * tt + 3],
                          in1=colacc_s[:, 4 * tt + 3:4 * tt + 4], op=ALU.add),
                          waits=merge(*sq_all))
                      td = sch.dve(lambda v, tt=tt, o=o: v.tensor_tensor(
                          out=cw_s[:, o + 2:o + 3], in0=mcol_s[:, tt:tt + 1],
                          in1=mcol_s[:, tt:tt + 1], op=ALU.mult), waits=tb2)
                      te = sch.dve(lambda v, tt=tt, o=o: v.scalar_tensor_tensor(
                          out=cw_s[:, o + 3:o + 4], in0=cw_s[:, o + 1:o + 2],
                          scalar=float(1.0 / 1024), in1=cw_s[:, o + 2:o + 3],
                          op0=ALU.mult, op1=ALU.subtract), waits=merge(tc, td))
                      tf = sch.act(lambda a, tt=tt, o=o: a.activation(
                          out=cs_s[:, tt % 2:tt % 2 + 1], in_=cw_s[:, o + 3:o + 4],
                          func=AF.Sqrt, bias=f32(1e-5)), waits=te)
                      tg = sch.dve(lambda v, tt=tt: v.reciprocal(
                          out=rcol_s[:, tt:tt + 1], in_=cs_s[:, tt % 2:tt % 2 + 1]),
                          waits=tf)
                      return merge(tb2, tg)

                  for tt in range(4):
                      sg = tt
                      cols_tok = s5_tile(tt)
                      nrm = sch.dve(lambda v, sg=sg, tt=tt: v.tensor_scalar(
                          out=stage_s[sg][:], in0=stage_s[sg][:],
                          scalar1=mcol_s[:, tt:tt + 1], scalar2=rcol_s[:, tt:tt + 1],
                          op0=ALU.add, op1=ALU.mult),
                          waits=merge(s5_ev[tt], cols_tok))
                      s5_nrm_all.append(nrm)
                      st_half = []
                      for hh in range(2):
                          st_half.append(sch.sync_dma(
                              lambda s_, sg=sg, tt=tt, r=r, hh=hh: s_.dma_start(
                                  out=y_h[512 * r + 128 * tt + 64 * hh:
                                          512 * r + 128 * tt + 64 * (hh + 1), :],
                                  in_=stage_s[sg][64 * hh:64 * (hh + 1), :]),
                              sem=f"st{sg + 4 * hh}", waits=nrm, gate=True))
                      stage_cons[sg] = merge(*st_half)
                  ln_cons[0] = merge(y1_ready, p_ready, *s5_sq_all, *s5_nrm_all)

    sch.wait_only("sync", {k: sch.cnt[k] for k in Sched.DMA_SEMS if sch.cnt[k]})

    with nc.Block() as block:
        def replay(eng_name):
            def go(engine):
                waited = {}
                for waits, fn, inc, amt in sch.ops[eng_name]:
                    for s, v in waits.items():
                        if waited.get(s, 0) < v:
                            engine.wait_ge(sem[s], v)
                            waited[s] = v
                    if fn is None:
                        continue
                    inst = fn(engine)
                    if inc is not None:
                        inst.then_inc(sem[inc], amt)
            return go
        block.tensor(replay("pe"))
        block.vector(replay("dve"))
        block.scalar(replay("act"))
        block.gpsimd(replay("gp"))
        block.sync(replay("sync"))

    ctx.close()
    return nc


_M32_CACHE = [None]


def _m32_table():
    if _M32_CACHE[0] is not None:
        return _M32_CACHE[0]
    try:
        import jax
        import jax.numpy as jnp
        with jax.default_device(jax.devices("cpu")[0]):
            m = np.asarray(0.99 ** jnp.arange(S).astype(jnp.float32),
                           dtype=np.float64)
    except Exception:
        m = (AEFF ** np.arange(S, dtype=np.float64)).astype(np.float32).astype(np.float64)
    _M32_CACHE[0] = m
    return m


def _hilo(t64):
    hi = t64.astype(np.float32)
    with np.errstate(divide="ignore", invalid="ignore"):
        lo = np.where(hi != 0, t64 / hi.astype(np.float64), 0.0).astype(np.float32)
    return np.ascontiguousarray(hi), np.ascontiguousarray(lo)


def _fm(a):
    T = a.shape[0]
    return np.ascontiguousarray(
        a.T.reshape(D // 128, 128, T).transpose(1, 0, 2).reshape(128, -1))


def _ck(w, rows):
    C = w.shape[1]
    return np.ascontiguousarray(
        w.reshape(rows, 128, C).transpose(1, 0, 2).reshape(128, rows * C))


def make_in_maps(L0_attn_x, L0_attn_W_qkv, L0_attn_b_qkv, L0_ffn_U1, L0_ffn_V1,
                 L0_ffn_b1, L0_ffn_U2, L0_ffn_V2, L0_ffn_b2):
    x = np.asarray(L0_attn_x, dtype=np.float32).reshape(S, D)
    Wq = np.asarray(L0_attn_W_qkv, dtype=np.float32)
    bq = np.asarray(L0_attn_b_qkv, dtype=np.float32)
    U1 = np.asarray(L0_ffn_U1, dtype=np.float32)
    V1 = np.asarray(L0_ffn_V1, dtype=np.float32)
    b1 = np.asarray(L0_ffn_b1, dtype=np.float32)
    U2 = np.asarray(L0_ffn_U2, dtype=np.float32)
    V2 = np.asarray(L0_ffn_V2, dtype=np.float32)
    b2 = np.asarray(L0_ffn_b2, dtype=np.float32)

    WT = np.ascontiguousarray(Wq.T)
    wtq = _ck(WT[:, 0:1024], 8)
    wtkv = np.concatenate([_ck(WT[:, 1024:2048], 8),
                           _ck(WT[:, 2048:3072], 8)], axis=1)
    brow = np.ascontiguousarray(bq[None, 1024:])
    bqcol = np.stack([bq[128 * o:128 * (o + 1)] for o in range(8)]
                     + [bq[1024 + 128 * o:1024 + 128 * (o + 1)] for o in range(8)],
                     axis=1).astype(np.float32)
    u1 = _ck(U1, 8)
    v1 = _ck(V1, 2)
    u2 = _ck(U2, 32)
    v2 = _ck(V2, 2)
    b1c = np.ascontiguousarray(b1.reshape(32, 128).T)
    b2col = np.ascontiguousarray(b2.reshape(8, 128).T)

    m32 = _m32_table()
    p = np.arange(128)[:, None]
    cc = np.arange(1408)[None, :]
    Dm = cc - 384 - p
    master = np.where(Dm >= 0, m32[np.clip(Dm, 0, None)] / 32.0, 0.0).astype(np.float32)

    t = np.arange(T_R, dtype=np.float64)
    aQ64 = np.broadcast_to(AEFF ** (W + t) / 32.0, (128, T_R))
    aQhi, aQlo = _hilo(aQ64)
    c4 = np.arange(4, dtype=np.float64)[None, :]
    acol4_64 = AEFF ** (-(p + 128 * c4))
    cH = np.arange(4 * NSMAC, dtype=np.float64)[None, :]
    acolH_base64 = AEFF ** (NSMAC * 512 - (128 * cH + p))
    ident = np.eye(128, dtype=np.float32)

    shared = dict(wtkv=wtkv, wtq=wtq, brow=brow, bqcol=bqcol, u1=u1, v1=v1,
                  u2=u2, v2=v2, b1c=b1c, b2col=b2col, master=master,
                  aQhi=aQhi, aQlo=aQlo, ident=ident)

    in_maps = []
    for core in range(NCORES):
        own = x[T_CORE * core:T_CORE * (core + 1)]
        xq = np.concatenate([_fm(own[T_R * rr:T_R * (rr + 1)]) for rr in range(NR)],
                            axis=1)
        lo = T_CORE * core - HALO
        halo = np.zeros((HALO, D), dtype=np.float32)
        src_lo = max(lo, 0)
        if src_lo < T_CORE * core:
            halo[src_lo - lo:] = x[src_lo:T_CORE * core]
        xh_tm = halo[0:2560].reshape(20, 128, D).transpose(1, 0, 2).reshape(128, 20480)
        xh = np.concatenate([xh_tm, _fm(halo[2560:3072])], axis=1)
        hidx = 128 * cH + p
        valid = (lo + hidx) >= 0
        acolH64 = np.where(valid, np.sqrt(acolH_base64), 0.0)
        ac64 = np.concatenate([acol4_64, acolH64], axis=1)
        achi, aclo = _hilo(ac64)
        vmask = np.full((128, 1), 1.0 if core > 0 else 0.0, dtype=np.float32)
        m = dict(shared)
        m.update(xq=np.ascontiguousarray(xq), xh=np.ascontiguousarray(xh),
                 achi=achi, aclo=aclo, vmask=vmask)
        in_maps.append(m)
    return in_maps


def kernel(L0_attn_x, L0_attn_W_qkv, L0_attn_b_qkv, L0_ffn_U1, L0_ffn_V1,
           L0_ffn_b1, L0_ffn_U2, L0_ffn_V2, L0_ffn_b2):
    in_maps = make_in_maps(L0_attn_x, L0_attn_W_qkv, L0_attn_b_qkv, L0_ffn_U1,
                           L0_ffn_V1, L0_ffn_b1, L0_ffn_U2, L0_ffn_V2, L0_ffn_b2)
    nc = build_nc()
    res = run_bass_kernel_spmd(nc, in_maps, core_ids=list(range(NCORES)))
    out = np.concatenate([r["y"] for r in res.results], axis=0)
    return out.reshape(1, S, D).astype(np.float32)


# version 28
# speedup vs baseline: 1.0117x; 1.0069x over previous
import math
import numpy as np
import ml_dtypes
import concourse.bass as bass
import concourse.mybir as mybir
from concourse.bass_utils import run_bass_kernel_spmd
from contextlib import ExitStack

F32 = mybir.dt.float32
BF16 = mybir.dt.bfloat16
AF = mybir.ActivationFunctionType
ALU = mybir.AluOpType
f32 = np.float32

AEFF = float(np.float32(0.99))
D = 1024
S = 16384
NCORES = 8
T_CORE = S // NCORES
NR = 4
T_R = T_CORE // NR
W = 512
HALO = 3072
NSMAC = 5
DFF = 4096
RANK = 256


def merge(*tokens):
    w = {}
    for t in tokens:
        if t is None:
            continue
        items = t.items() if isinstance(t, dict) else [t]
        for s, v in items:
            if w.get(s, -1) < v:
                w[s] = v
    return w


class Sched:
    ENGINES = ("pe", "dve", "act", "gp", "sync")
    DMA_SEMS = ("hw", "wtk", "wtv", "wtq", "miscsw", "xh0", "xh1", "xh2", "xh3",
                "xqs", "ffnw", "st2", "st3", "st4", "st5", "st6", "st7", "xh4", "wbq", "wbv", "wbu",
                "st0", "st1", "tabs")

    def __init__(self):
        self.ops = {e: [] for e in self.ENGINES}
        self.cnt = {k: 0 for k in ("pe", "dve", "act", "gp") + self.DMA_SEMS}

    def _emit(self, eng, fn, waits, inc, amt):
        self.ops[eng].append((merge(waits) if waits else {}, fn, inc, amt))
        if inc is None:
            return None
        self.cnt[inc] += amt
        return (inc, self.cnt[inc])

    def pe(self, fn, waits=None, inc=False):
        return self._emit("pe", fn, waits, "pe" if inc else None, 1)

    def dve(self, fn, waits=None):
        return self._emit("dve", fn, waits, "dve", 1)

    def act(self, fn, waits=None):
        return self._emit("act", fn, waits, "act", 1)

    def gp(self, fn, waits=None):
        return self._emit("gp", fn, waits, "gp", 1)

    def gp_dma(self, fn, sem="miscsw", waits=None, gate=False):
        w = merge(waits, {sem: self.cnt[sem]}) if gate else waits
        return self._emit("gp", fn, w, sem, 16)

    def sync_dma(self, fn, waits=None, sem="hw", gate=False):
        w = merge(waits, {sem: self.cnt[sem]}) if gate else waits
        return self._emit("sync", fn, w, sem, 16)

    def act_dma(self, fn, waits=None, sem="hw", gate=False):
        w = merge(waits, {sem: self.cnt[sem]}) if gate else waits
        return self._emit("act", fn, w, sem, 16)

    def wait_only(self, eng, waits):
        return self._emit(eng, None, waits, None, 1)

    def barrier_token(self):
        return dict(self.cnt)


def build_nc():
    nc = bass.Bass("TRN2", target_bir_lowering=False,
                   dynamic_dma_scratch_size=4096, num_swdge_queues=3)
    sch = Sched()

    def dram(name, shape, dtype=F32, kind="ExternalInput"):
        return nc.dram_tensor(name, shape, dtype, kind=kind).ap()

    xq_h = dram("xq", [128, 16384])
    xh_h = dram("xh", [128, 24576])
    wtkv_h = dram("wtkv", [128, 16384])
    wtq_h = dram("wtq", [128, 8192])
    brow_h = dram("brow", [1, 2048])
    bqcol_h = dram("bqcol", [128, 16])
    u1_h = dram("u1", [128, 2048])
    v1_h = dram("v1", [128, 8192])
    u2_h = dram("u2", [128, 8192])
    v2_h = dram("v2", [128, 2048])
    b1c_h = dram("b1c", [128, 32])
    b2col_h = dram("b2col", [128, 8])
    master_h = dram("master", [128, 1408])
    aQhi_h = dram("aQhi", [128, 512])
    aQlo_h = dram("aQlo", [128, 512])
    achi_h = dram("achi", [128, 4 + 4 * NSMAC])
    aclo_h = dram("aclo", [128, 4 + 4 * NSMAC])
    vmask_h = dram("vmask", [128, 1])
    ident_h = dram("ident", [128, 128])
    y_h = dram("y", [T_CORE, D], kind="ExternalOutput")
    wtqb_d = nc.dram_tensor("wtqb", [128, 8192], BF16, kind="Internal").ap()
    v1b_d = nc.dram_tensor("v1b", [128, 8192], BF16, kind="Internal").ap()
    u2b_d = nc.dram_tensor("u2b", [128, 8192], BF16, kind="Internal").ap()

    ctx = ExitStack()
    E = ctx.enter_context

    def sb(name, shape, dtype, side="left"):
        return E(nc.sbuf_tensor(name, shape, dtype, side=side))

    sem = {k: E(nc.semaphore(k)) for k in ("pe", "dve", "act", "gp") + Sched.DMA_SEMS}

    pb = [E(nc.psum_tensor(f"pb{i}", [128, 512], F32)) for i in range(6)]
    ptr = [E(nc.psum_tensor(f"ptr{i}", [128, 1024], BF16)) for i in range(2)]
    bank_free = [None] * 6
    bank_open = [False] * 6
    ptr_free = [None] * 2
    _rr = [0]

    def acquire_bank():
        for _ in range(6):
            i = _rr[0]
            _rr[0] = (i + 1) % 6
            if not bank_open[i]:
                bank_open[i] = True
                return i
        raise RuntimeError("no free psum bank")

    def mm_group(tiles, waits, nf=512, mrows=128, last_waits=None):
        bi = acquire_bank()
        n = len(tiles)
        tok = None
        for i, (lh, rh) in enumerate(tiles):
            w = merge(waits, bank_free[bi]) if i == 0 else None
            if i == n - 1 and last_waits:
                w = merge(w, last_waits)
            tok = sch.pe(
                lambda t, lh=lh, rh=rh, st=(i == 0), sp=(i == n - 1), bi=bi,
                nf=nf, mr=mrows:
                t.matmul(pb[bi][0:mr, 0:nf], lh, rh, start=st, stop=sp),
                waits=w, inc=(i == n - 1))
        bank_open[bi] = False
        return bi, tok

    master_s = sb("master_s", [128, 1408], F32)
    aQrep_s = sb("aQrep_s", [128, 512], F32)
    acol_s = sb("acol_s", [128, 4 + 4 * NSMAC], F32)
    vmask_s = sb("vmask_s", [128, 1], F32)
    ident_s = sb("ident_s", [128, 128], BF16)
    ones_s = sb("ones_s", [1, 512], BF16)
    onescol_s = sb("onescol_s", [128, 1], BF16)
    brow_s = sb("brow_s", [1, 2048], BF16)
    bqcol_s = sb("bqcol_s", [128, 16], F32)
    b1c_s = sb("b1c_s", [128, 32], F32)
    b2col_s = sb("b2col_s", [128, 8], F32)
    rowA_s = sb("rowA_s", [1, 512], BF16)
    rowB_s = sb("rowB_s", [1, 512], BF16)
    rowD_s = sb("rowD_s", [1, 512], BF16)
    colsq_s = sb("colsq_s", [128, 8], F32)
    mcq_s = sb("mcq_s", [128, 4], F32)
    cscq_s = sb("cscq_s", [128, 4], F32)
    vcq_s = sb("vcq_s", [128, 4], F32)
    colsb_s = sb("colsb_s", [128, 8], BF16)
    colacc_s = sb("colacc_s", [128, 16], F32)
    cw_s = sb("cw_s", [128, 8], F32)
    cs_s = sb("cs_s", [128, 2], F32)
    mrep_s = sb("mrep_s", [128, 512], BF16)
    rrep_s = sb("rrep_s", [128, 512], BF16)
    ysq_s = [sb(f"ysq{i}", [128, 512], BF16) for i in range(3)]
    xsc_s = [sb(f"xsc{i}", [128, 512], BF16) for i in range(2)]
    mcol_s = sb("mcol_s", [128, 4], F32)
    rcol_s = sb("rcol_s", [128, 4], F32)
    nu1s_s = sb("nu1s_s", [1, 256], BF16)
    p_s = sb("p_s", [128, 512], BF16)
    q2_s = sb("q2_s", [128, 512], BF16)
    stage_s = [sb(f"stage{i}", [128, 1024], F32) for i in range(4)]
    u1_s = sb("u1_s", [128, 2048], BF16)
    v2_s = sb("v2_s", [128, 2048], BF16)

    wtkv_s = sb("wtkv_s", [128, 16384], BF16, side="right")
    xqr2_s = sb("xqr2_s", [128, 8192], BF16, side="right")
    uacc_s = sb("uacc_s", [128, 8192], BF16, side="right")
    kfm2_s = [sb(f"kfm2_{i}", [128, 4096], BF16, side="right") for i in range(2)]
    vtm2_s = [sb(f"vtm2_{i}", [128, 4096], BF16, side="right") for i in range(2)]

    t_ones = sch.gp(lambda g: g.memset(ones_s[:], 1.0))
    t_onescol = sch.gp(lambda g: g.memset(onescol_s[:], 1.0))

    def ld_sw(dst, src, waits=None, dsem="miscsw", gate=False):
        return sch.gp_dma(lambda g, d=dst, s=src: g.dma_start(out=d, in_=s),
                          sem=dsem, waits=waits, gate=gate)

    def ld_hw(dst, src, waits=None, sem="hw", gate=False):
        return sch.sync_dma(lambda s_, d=dst, s=src: s_.dma_start(out=d, in_=s),
                            waits=waits, sem=sem, gate=gate)

    def ld_hw_act(dst, src, waits=None, sem="hw", gate=False):
        return sch.act_dma(lambda a_, d=dst, s=src: a_.dma_start(out=d, in_=s),
                           waits=waits, sem=sem, gate=gate)

    NA = 4 + 4 * NSMAC
    with nc.sbuf_tensor("tabhi", [128, 536], F32) as tabhi_s, \
         nc.sbuf_tensor("tablo", [128, 536], F32) as tablo_s:
        th2 = ld_hw(tabhi_s[:, 0:NA], achi_h[:], sem="tabs")
        tl2 = ld_hw(tablo_s[:, 0:NA], aclo_h[:], sem="tabs")
        t_acol = sch.dve(lambda v: v.tensor_tensor(
            out=acol_s[:], in0=tabhi_s[:, 0:NA], in1=tablo_s[:, 0:NA],
            op=ALU.mult), waits=merge(th2, tl2))
        th = ld_hw(tabhi_s[:, NA:NA + 512], aQhi_h[:], sem="tabs", gate=True)
        tl = ld_hw(tablo_s[:, NA:NA + 512], aQlo_h[:], sem="tabs")
        t_aQ = sch.dve(lambda v: v.tensor_tensor(
            out=aQrep_s[:], in0=tabhi_s[:, NA:NA + 512],
            in1=tablo_s[:, NA:NA + 512], op=ALU.mult), waits=merge(th, tl))
    t_tab = merge(t_acol, t_aQ)
    t_hw_misc = merge(ld_hw(bqcol_s[:], bqcol_h[:]),
                      ld_hw(b1c_s[:], b1c_h[:]),
                      ld_hw(b2col_s[:], b2col_h[:]),
                      ld_hw(vmask_s[:], vmask_h[:]),
                      ld_hw(master_s[:, 0:704], master_h[:, 0:704]),
                      ld_hw(master_s[:, 704:1408], master_h[:, 704:1408]))

    acolb_s = sb("acolb_s", [128, 4 + 4 * NSMAC], BF16)
    s0_s = sb("s0_s", [1, 1], F32)
    t_acolb = sch.dve(lambda v: v.tensor_copy(acolb_s[:], acol_s[:]),
                      waits=t_acol)
    with nc.sbuf_tensor("xhtm", [128, 20480], BF16, side="right") as xhtm_s, \
         nc.sbuf_tensor("sxx", [128, 8192], BF16, side="right") as sxx_s, \
         nc.sbuf_tensor("hx5", [128, 4096], BF16, side="right") as hx5_s, \
         nc.sbuf_tensor("hrow", [1, 1024], BF16, side="right") as hrow_s, \
         nc.sbuf_tensor("hcol", [128, 8], BF16, side="right") as hcol_s, \
         nc.sbuf_tensor("wkr", [1, 1024], BF16, side="right") as wkr_s:
        xh_cuts = [0, 2048, 7168, 12288, 17408, 20480]
        t_xh = [ld_sw(xhtm_s[:, xh_cuts[i]:xh_cuts[i + 1]],
                      xh_h[:, xh_cuts[i]:xh_cuts[i + 1]], dsem=f"xh{i}")
                for i in range(5)]
        xh_piece = [0, 0] + [1] * 5 + [2] * 5 + [3] * 5 + [4] * 3
        for i in range(2):
            ld_sw(wtkv_s[:, 4096 * i:4096 * (i + 1)],
                  wtkv_h[:, 4096 * i:4096 * (i + 1)], dsem="wtk")
        t_wtk = ("wtk", sch.cnt["wtk"])
        ld_sw(brow_s[:], brow_h[:])
        _t_ident = ld_sw(ident_s[:], ident_h[:], gate=True)
        t_sw_misc1 = ("miscsw", sch.cnt["miscsw"])
        t_consts = merge(t_sw_misc1, t_hw_misc, t_ones, t_onescol, t_tab)
        for i in range(2, 4):
            ld_sw(wtkv_s[:, 4096 * i:4096 * (i + 1)],
                  wtkv_h[:, 4096 * i:4096 * (i + 1)], dsem="wtv")
        t_wtkv = ("wtv", sch.cnt["wtv"])
        t_hx5 = ld_sw(hx5_s[:], xh_h[:, 20480:24576], dsem="xh1", gate=True)
        xq_tok = {}
        for rr in range(2):
            xq_tok[rr] = ld_sw(xqr2_s[:, 4096 * rr:4096 * (rr + 1)],
                               xq_h[:, 4096 * rr:4096 * (rr + 1)],
                               dsem="xqs", gate=True)
        sc_tok = []
        for c in range(20):
            sc_tok.append(sch.act(lambda a, c=c: a.activation(
                out=xhtm_s[:, 1024 * c:1024 * (c + 1)],
                in_=xhtm_s[:, 1024 * c:1024 * (c + 1)], func=AF.Identity,
                scale=acol_s[:, 4 + c:4 + c + 1]),
                waits=merge(t_xh[xh_piece[c]], t_acol)))
        t_xs = sc_tok[-1]
        sxx_tok = {}
        all_gk = [(a, db) for a in range(8) for db in range(2)
                  if not (db == 0 and a >= 4)]
        for p in range(2):
            gks = all_gk[6 * p:6 * (p + 1)]
            bks = {g: acquire_bank() for g in gks}
            mmt = {}
            for c in range(20):
                for a, db in gks:
                    bi = bks[(a, db)]
                    w = merge(sc_tok[c],
                              bank_free[bi] if c == 0 else None)
                    mmt[(a, db)] = sch.pe(
                        lambda t, a=a, db=db, c=c, bi=bi: t.matmul(
                            pb[bi][:],
                            xhtm_s[:, 1024 * c + 128 * a:1024 * c + 128 * (a + 1)],
                            xhtm_s[:, 1024 * c + 512 * db:1024 * c + 512 * (db + 1)],
                            start=(c == 0), stop=(c == 19)),
                        waits=w, inc=(c == 19))
            for a, db in gks:
                bi = bks[(a, db)]
                bank_open[bi] = False
                dst = sxx_s[:, 1024 * a + 512 * db:1024 * a + 512 * (db + 1)]
                ev = sch.act(lambda ac, bi=bi, dst=dst: ac.activation(
                    out=dst, in_=pb[bi][:], func=AF.Identity),
                    waits=mmt[(a, db)])
                sxx_tok[(a, db)] = ev
                bank_free[bi] = ev
        for a in range(4, 8):
            pt = a % 2
            ptok = None
            for j in range(4):
                src = sxx_s[:, 1024 * j + 512 + 128 * (a - 4):
                            1024 * j + 512 + 128 * (a - 3)]
                ptok = sch.pe(lambda t, src=src, j=j, pt=pt: t.transpose(
                    ptr[pt][:, 128 * j:128 * (j + 1)], src, ident_s[:]),
                    waits=merge(sxx_tok[(j, 1)], t_consts, _t_ident,
                                ptr_free[pt] if j == 0 else None),
                    inc=(j == 3))
            ev = sch.act(lambda ac, a=a, pt=pt: ac.activation(
                out=sxx_s[:, 1024 * a:1024 * a + 512], in_=ptr[pt][:, 0:512],
                func=AF.Identity), waits=ptok)
            ptr_free[pt] = ev
            sxx_tok[(a, 0)] = ev
        sxx_ready = merge(*sxx_tok.values())
        hr_tok = []
        for db in range(2):
            tiles = [(acolb_s[:, 4 + c:4 + c + 1],
                      xhtm_s[:, 1024 * c + 512 * db:1024 * c + 512 * (db + 1)])
                     for c in range(20)]
            bi, tok = mm_group(tiles, merge(t_xs, t_acolb), mrows=1)
            ev = sch.dve(lambda v, bi=bi, db=db: v.tensor_copy(
                hrow_s[0:1, 512 * db:512 * (db + 1)], pb[bi][0:1, :]), waits=tok)
            hr_tok.append(ev)
            bank_free[bi] = ev
        tiles = [(acolb_s[:, 4 + c:4 + c + 1], acolb_s[:, 4 + c:4 + c + 1])
                 for c in range(20)]
        bi, tok = mm_group(tiles, t_acolb, nf=1, mrows=1)
        t_s0 = sch.dve(lambda v, bi=bi: v.tensor_copy(
            s0_s[0:1, 0:1], pb[bi][0:1, 0:1]), waits=tok)
        bank_free[bi] = t_s0
        t_hrow = merge(*hr_tok)
        halo_gate = merge({"pe": sch.cnt["pe"]},
                          *[(f"xh{i}", sch.cnt[f"xh{i}"]) for i in range(5)])
        hc_tok = []
        for c in range(8):
            bi, tok = mm_group([(hrow_s[0:1, 128 * c:128 * (c + 1)],
                                 ones_s[0:1, 0:1])], merge(t_hrow, t_ones), nf=1)
            ev = sch.dve(lambda v, bi=bi, c=c: v.tensor_copy(
                hcol_s[:, c:c + 1], pb[bi][0:128, 0:1]), waits=tok)
            hc_tok.append(ev)
            bank_free[bi] = ev
        t_hcol = merge(*hc_tok)
        wk_tok = []
        for db in range(2):
            tiles = [(hcol_s[:, c:c + 1],
                      wtkv_s[:, 1024 * c + 512 * db:1024 * c + 512 * (db + 1)])
                     for c in range(8)]
            bi, tok = mm_group(tiles, merge(t_hcol, t_wtk), mrows=1)
            ev = sch.dve(lambda v, bi=bi, db=db: v.tensor_copy(
                wkr_s[0:1, 512 * db:512 * (db + 1)], pb[bi][0:1, :]), waits=tok)
            wk_tok.append(ev)
            bank_free[bi] = ev
        t_wkr = sch.dve(lambda v: v.scalar_tensor_tensor(
            out=wkr_s[0:1, :], in0=brow_s[0:1, 0:1024], scalar=s0_s[0:1, 0:1],
            in1=wkr_s[0:1, :], op0=ALU.mult, op1=ALU.add),
            waits=merge(t_s0, t_sw_misc1, *wk_tok))

        init_tok = {}
        last5 = None
        for ot in range(8):
            tiles = [(wtkv_s[:, 1024 * c + 128 * ot:1024 * c + 128 * (ot + 1)],
                      hx5_s[:, 512 * c:512 * (c + 1)]) for c in range(8)]
            bi, tok = mm_group(tiles, merge(t_hx5, t_wtk))
            last5 = tok
            dst = kfm2_s[1][:, 512 * ot:512 * (ot + 1)]
            ev = sch.act(lambda a, bi=bi, dst=dst, ot=ot: a.activation(
                out=dst, in_=pb[bi][:], func=AF.Identity,
                bias=bqcol_s[:, 8 + ot:8 + ot + 1]), waits=merge(tok, t_hw_misc))
            init_tok[("k", ot)] = ev
            bank_free[bi] = ev
        for tt in range(4):
            for db in range(2):
                tiles = [(hx5_s[:, 512 * c + 128 * tt:512 * c + 128 * (tt + 1)],
                          wtkv_s[:, 8192 + 1024 * c + 512 * db:8192 + 1024 * c + 512 * (db + 1)])
                         for c in range(8)]
                tiles.append((ones_s[0:1, 0:128],
                              brow_s[0:1, 1024 + 512 * db:1024 + 512 * (db + 1)]))
                bi, tok = mm_group(tiles, merge(t_hx5, t_wtkv, t_ones, t_sw_misc1))
                last5 = tok
                dst = vtm2_s[1][:, 1024 * tt + 512 * db:1024 * tt + 512 * (db + 1)]
                ev = sch.act(lambda a, bi=bi, dst=dst: a.activation(
                    out=dst, in_=pb[bi][:], func=AF.Identity,
                    scale=vmask_s[:, 0:1]), waits=merge(tok, t_hw_misc))
                init_tok[("v", tt, db)] = ev
                bank_free[bi] = ev
        init_ready = merge(*init_tok.values())

        m1_half = [kfm2_s[0], vtm2_s[0]]
        m1_tok = {}
        for ap2 in range(8):
            for db in range(2):
                tiles = [(sxx_s[:, 1024 * c + 128 * ap2:1024 * c + 128 * (ap2 + 1)],
                          wtkv_s[:, 1024 * c + 512 * db:1024 * c + 512 * (db + 1)])
                         for c in range(8)]
                tiles.append((hrow_s[0:1, 128 * ap2:128 * (ap2 + 1)],
                              brow_s[0:1, 512 * db:512 * (db + 1)]))
                bi, tok = mm_group(tiles, merge(sxx_ready, t_hrow, t_wtk,
                                                t_sw_misc1))
                dst = m1_half[db][:, 512 * ap2:512 * (ap2 + 1)]
                ev = sch.act(lambda a, bi=bi, dst=dst: a.activation(
                    out=dst, in_=pb[bi][:], func=AF.Identity), waits=tok)
                m1_tok[(ap2, db)] = ev
                bank_free[bi] = ev
        m1_ready = merge(*m1_tok.values())
        uacc_tok = {}
        m1_cons = None
        for a in range(8):
            dbk, ak = a // 4, a % 4
            for db in range(2):
                tiles = [(m1_half[dbk][:, 512 * c + 128 * ak:512 * c + 128 * (ak + 1)],
                          wtkv_s[:, 8192 + 1024 * c + 512 * db:8192 + 1024 * c + 512 * (db + 1)])
                         for c in range(8)]
                tiles.append((wkr_s[0:1, 128 * a:128 * (a + 1)],
                              brow_s[0:1, 1024 + 512 * db:1024 + 512 * (db + 1)]))
                bi, tok = mm_group(tiles, merge(m1_ready, t_wkr, t_wtkv))
                m1_cons = merge(m1_cons, tok)
                dst = uacc_s[:, 1024 * a + 512 * db:1024 * a + 512 * (db + 1)]
                ev = sch.act(lambda ac, bi=bi, dst=dst: ac.activation(
                    out=dst, in_=pb[bi][:], func=AF.Identity), waits=tok)
                uacc_tok[(a, db)] = ev
                bank_free[bi] = ev
        uacc_ready = merge(*uacc_tok.values())
        kfm_cons0_init = m1_cons
    t_consts = merge(t_consts, _t_ident, ("miscsw", sch.cnt["miscsw"]))
    t_u1 = ld_sw(u1_s[:], u1_h[:], gate=True)
    t_v2 = ld_sw(v2_s[:], v2_h[:], gate=True)
    t_consts = merge(t_consts, t_u1, t_v2)
    bi_us, tok_us = mm_group([(onescol_s[:], u1_s[:, 256 * c:256 * (c + 1)])
                              for c in range(8)],
                             merge(t_u1, t_ones, t_onescol), nf=256, mrows=1)
    t_nu1 = sch.act(lambda a, bi=bi_us: a.activation(
        out=nu1s_s[0:1, :], in_=pb[bi][0:1, 0:256], func=AF.Identity,
        scale=f32(-1.0 / 1024.0)), waits=tok_us)
    bank_free[bi_us] = t_nu1

    ln_cons = [None]
    xq_cons = [None, None]
    wb_tok = {}
    p_cons = [None]
    q2_cons = [None]
    h_cons = [None]
    stage_cons = [None] * 4
    xsc_cons = [None, None]
    kfm_tokens = [None, init_ready]
    vtm_tokens = [None, init_ready]
    kfm_cons = [kfm_cons0_init, None]
    vtm_cons = [kfm_cons0_init, None]
    u_ready = [uacc_ready]

    def ln_stats(src_s, src_ready, bw, reps=True, cols=False, mid=None,
                 chunk_tok=None):
        guard = ln_cons[0]
        ctok = chunk_tok or {}
        banks = {}
        bsum = acquire_bank()
        tok = None
        for c in range(8):
            w = merge(ctok.get(c, src_ready), t_consts,
                      bank_free[bsum] if c == 0 else None)
            tok = sch.pe(lambda t, c=c, bi=bsum, ss=src_s: t.matmul(
                pb[bi][0:1, 0:512], onescol_s[:], ss[:, 512 * c:512 * (c + 1)],
                start=(c == 0), stop=(c == 7)), waits=w, inc=(c == 7))
        bank_open[bsum] = False
        banks[0] = (bsum, tok)
        sq_tok = {}
        mmsq_tok = {}
        bsq = acquire_bank()
        for c in range(8):
            dst = ysq_s[c % 3]
            sq_tok[c] = sch.act(
                lambda a, c=c, dst=dst, ss=src_s: a.activation(
                    out=dst[:], in_=ss[:, 512 * c:512 * (c + 1)], func=AF.Square),
                waits=merge(ctok.get(c, src_ready), guard, mmsq_tok.get(c - 3)))
            w = merge(sq_tok[c], bank_free[bsq] if c == 0 else None, t_consts)
            mmsq_tok[c] = sch.pe(
                lambda t, c=c, bi=bsq: t.matmul(
                    pb[bi][0:1, 0:512], onescol_s[:], ysq_s[c % 3][:],
                    start=(c == 0), stop=(c == 7)),
                waits=w, inc=True)
        bank_open[bsq] = False
        banks[1] = (bsq, mmsq_tok[7])
        (bsumi, toks), (bsqi, tokq) = banks[0], banks[1]
        evA = sch.dve(lambda v, bi=bsumi: v.tensor_copy(
            rowA_s[0:1, :], pb[bi][0:1, :]), waits=merge(toks, guard))
        bank_free[bsumi] = evA
        evB = sch.dve(lambda v, bi=bsqi: v.tensor_copy(
            rowB_s[0:1, :], pb[bi][0:1, :]), waits=merge(tokq, guard))
        bank_free[bsqi] = evB
        bcol = acquire_bank()
        ctok = None
        for j in range(8):
            row, rtok = (rowA_s, evA) if j < 4 else (rowB_s, evB)
            jj = j % 4
            w = merge(rtok, t_ones, bank_free[bcol] if j == 0 else None)
            ctok = sch.pe(lambda t, j=j, row=row, jj=jj, bi=bcol: t.matmul(
                pb[bi][0:128, j:j + 1], row[0:1, 128 * jj:128 * (jj + 1)],
                ones_s[0:1, 0:1], start=True, stop=True), waits=w,
                inc=(j == 7))
        bank_open[bcol] = False
        evC = sch.dve(lambda v, bi=bcol: v.tensor_copy(
            colsq_s[:], pb[bi][0:128, 0:8]), waits=merge(ctok, guard))
        bank_free[bcol] = evC
        t1c = sch.dve(lambda v: v.tensor_scalar(
            out=mcq_s[:], in0=colsq_s[:, 0:4], scalar1=float(1.0 / 1024),
            scalar2=None, op0=ALU.mult), waits=evC)
        t2c = sch.dve(lambda v: v.tensor_tensor(
            out=cscq_s[:], in0=mcq_s[:], in1=mcq_s[:], op=ALU.mult), waits=t1c)
        t3c = sch.dve(lambda v: v.scalar_tensor_tensor(
            out=vcq_s[:], in0=colsq_s[:, 4:8], scalar=float(1.0 / 1024),
            in1=cscq_s[:], op0=ALU.mult, op1=ALU.subtract),
            waits=merge(t2c, evC))
        t4c = sch.act(lambda a: a.activation(
            out=cscq_s[:], in_=vcq_s[:], func=AF.Sqrt, bias=f32(1e-5)),
            waits=t3c)
        t5c = sch.dve(lambda v: v.reciprocal(out=vcq_s[:], in_=cscq_s[:]),
                      waits=t4c)
        cb1 = sch.dve(lambda v: v.tensor_copy(colsb_s[:, 0:4], mcq_s[:]),
                      waits=merge(t1c, guard))
        cb2 = sch.dve(lambda v: v.tensor_copy(colsb_s[:, 4:8], vcq_s[:]),
                      waits=merge(t5c, guard))
        out = {"t1": evA, "sq": merge(*sq_tok.values())}
        if reps:
            brow1 = acquire_bank()
            brow2 = acquire_bank()
            rtok1 = rtok2 = None
            for j in range(8):
                bi = brow1 if j < 4 else brow2
                w = merge(cb1 if j < 4 else cb2, t_consts,
                          bank_free[bi] if j % 4 == 0 else None)
                tok = sch.pe(lambda t, j=j, bi=bi: t.matmul(
                    pb[bi][0:1, 128 * (j % 4):128 * (j % 4 + 1)],
                    colsb_s[:, j:j + 1], ident_s[:, 0:128],
                    start=True, stop=True), waits=w, inc=(j % 4 == 3))
                if j == 3:
                    rtok1 = tok
                elif j == 7:
                    rtok2 = tok
            bank_open[brow1] = False
            bank_open[brow2] = False
            evD = sch.dve(lambda v, bi=brow1: v.tensor_copy(
                rowD_s[0:1, :], pb[bi][0:1, 0:512]), waits=merge(rtok1, guard))
            evE = sch.dve(lambda v, bi=brow2: v.tensor_copy(
                rowB_s[0:1, :], pb[bi][0:1, 0:512]), waits=merge(rtok2, guard))
            bank_free[brow1] = evD
            bank_free[brow2] = evE
            bi6, tok6 = mm_group([(ones_s[0:1, 0:128], rowD_s[0:1, :])],
                                 merge(evD, guard, t_consts), nf=512)
            t6 = sch.act(lambda a, bi=bi6: a.activation(
                out=mrep_s[:], in_=pb[bi][:], func=AF.Identity),
                waits=merge(tok6, guard))
            bank_free[bi6] = t6
            bi7, tok7 = mm_group([(ones_s[0:1, 0:128], rowB_s[0:1, :])],
                                 merge(evE, guard), nf=512)
            t7 = sch.act(lambda a, bi=bi7: a.activation(
                out=rrep_s[:], in_=pb[bi][:], func=AF.Identity),
                waits=merge(tok7, guard))
            bank_free[bi7] = t7
            out["reps"] = merge(t6, t7)
        if mid is not None:
            mid()
        return out

    for r in range(NR):
        par = r % 2
        prv = 1 - par
        def bw(eng):
            return None

        with nc.sbuf_tensor(f"y1_{r}", [128, 4096], BF16) as y1_s:
            with nc.sbuf_tensor(f"yraw{r}", [128, 4096], BF16, side="right") as yraw_s:
              with nc.sbuf_tensor(f"wtq{r}", [128, 8192], BF16, side="right") as wtq_s, \
                   nc.sbuf_tensor(f"ktms{r}", [128, 4096], BF16, side="right") as ktms_s:
                gw = merge(p_cons[0], q2_cons[0], h_cons[0],
                           halo_gate if r == 0 else None)
                xqr_s = xqr2_s[:, 4096 * par:4096 * (par + 1)]
                t_xq = xq_tok[r]
                if r < 3:
                    xq_tok[r + 1] = ld_sw(
                        xqr2_s[:, 4096 * ((r + 1) % 2):4096 * ((r + 1) % 2 + 1)],
                        xq_h[:, 4096 * (r + 1):4096 * (r + 2)],
                        waits=xq_cons[(r + 1) % 2], dsem="xqs", gate=True)
                if r == 0:
                    for i in range(2):
                        ld_sw(wtq_s[:, 4096 * i:4096 * (i + 1)],
                              wtq_h[:, 4096 * i:4096 * (i + 1)],
                              waits=gw if i == 0 else None, dsem="wtq",
                              gate=(i == 0))
                    t_wtq = ("wtq", sch.cnt["wtq"])
                    wb_tok["wtq"] = ld_hw(wtqb_d, wtq_s[:], waits=t_wtq,
                                          sem="wbq", gate=True)
                else:
                    t_wtq = ld_hw_act(wtq_s[:], wtqb_d,
                                      waits=merge(p_cons[0],
                                                  wb_tok["v1"] if r == 1 else None),
                                      sem="wbq", gate=True)

                with nc.sbuf_tensor(f"abuf{r}", [128, 4096], BF16) as abuf_s, \
                     nc.sbuf_tensor(f"qfm{r}", [128, 4096], BF16) as qfm_s:
                    qfm_tok, kfm_tok, vtm_tok = {}, {}, {}
                    for ot in range(8):
                        tiles = [(wtkv_s[:, 1024 * c + 128 * ot:1024 * c + 128 * (ot + 1)],
                                  xqr_s[:, 512 * c:512 * (c + 1)]) for c in range(8)]
                        bi, tok = mm_group(tiles, merge(t_xq, t_wtk, kfm_cons[par]))
                        dst = kfm2_s[par][:, 512 * ot:512 * (ot + 1)]
                        ev = sch.act(lambda a, bi=bi, dst=dst, ot=ot, par=par:
                                     a.activation(
                                         out=dst, in_=pb[bi][:], func=AF.Identity,
                                         bias=bqcol_s[:, 8 + ot:8 + ot + 1]),
                                     waits=tok)
                        kfm_tok[ot] = ev
                        bank_free[bi] = ev
                    for tt in range(4):
                        for db in range(2):
                            tiles = [(xqr_s[:, 512 * c + 128 * tt:512 * c + 128 * (tt + 1)],
                                      wtkv_s[:, 8192 + 1024 * c + 512 * db:8192 + 1024 * c + 512 * (db + 1)])
                                     for c in range(8)]
                            tiles.append((ones_s[0:1, 0:128],
                                          brow_s[0:1, 1024 + 512 * db:1024 + 512 * (db + 1)]))
                            bi, tok = mm_group(tiles, merge(t_xq, t_wtkv, vtm_cons[par]))
                            dst = vtm2_s[par][:, 1024 * tt + 512 * db:1024 * tt + 512 * (db + 1)]
                            ev = sch.act(lambda a, bi=bi, dst=dst: a.activation(
                                out=dst, in_=pb[bi][:], func=AF.Identity),
                                waits=tok)
                            vtm_tok[(tt, db)] = ev
                            bank_free[bi] = ev
                    for ot in range(8):
                        tiles = [(wtq_s[:, 1024 * c + 128 * ot:1024 * c + 128 * (ot + 1)],
                                  xqr_s[:, 512 * c:512 * (c + 1)]) for c in range(8)]
                        bi, tok = mm_group(tiles, merge(t_xq, t_wtq, t_consts))
                        dst = qfm_s[:, 512 * ot:512 * (ot + 1)]
                        ev = sch.act(lambda a, bi=bi, dst=dst, ot=ot: a.activation(
                            out=dst, in_=pb[bi][:], func=AF.Identity,
                            bias=bqcol_s[:, ot:ot + 1]),
                            waits=merge(tok, bw("act")))
                        qfm_tok[ot] = ev
                        bank_free[bi] = ev
                    qfm_ready = merge(*qfm_tok.values())
                    kfm_ready = merge(*kfm_tok.values())
                    vtm_ready = merge(*vtm_tok.values())
                    kfm_tokens[par] = kfm_ready
                    vtm_tokens[par] = vtm_ready
                    t_s1_pe = {"pe": sch.cnt["pe"]}

                    a_tok = {}
                    for jt in range(8):
                        src = kfm2_s[prv] if jt < 4 else kfm2_s[par]
                        src_tok = kfm_tokens[prv] if jt < 4 else kfm_ready
                        jj = jt % 4
                        tiles = [(src[:, 512 * c + 128 * jj:512 * c + 128 * (jj + 1)],
                                  qfm_s[:, 512 * c:512 * (c + 1)]) for c in range(8)]
                        bi, tok = mm_group(tiles, merge(src_tok, qfm_ready))
                        if jt < 4:
                            kfm_cons[prv] = merge(kfm_cons[prv], tok)
                        dst = abuf_s[:, 512 * jt:512 * (jt + 1)]
                        c0 = 896 - 128 * jt
                        ev = sch.dve(lambda v, bi=bi, dst=dst, c0=c0:
                                     v.tensor_tensor(out=dst, in0=pb[bi][:],
                                                     in1=master_s[:, c0:c0 + 512],
                                                     op=ALU.mult), waits=tok)
                        a_tok[jt] = ev
                        bank_free[bi] = ev
                    a_ready = merge(*a_tok.values())

                    yraw_tok = {}
                    last_cross_pe = None
                    for dv in range(8):
                        tiles = []
                        for jt in range(8):
                            vsrc = vtm2_s[prv] if jt < 4 else vtm2_s[par]
                            jj = jt % 4
                            tiles.append((vsrc[:, 1024 * jj + 128 * dv:1024 * jj + 128 * (dv + 1)],
                                          abuf_s[:, 512 * jt:512 * (jt + 1)]))
                        biA, tokA = mm_group(tiles, merge(a_ready, vtm_tokens[prv],
                                                          vtm_ready))
                        vtm_cons[prv] = merge(vtm_cons[prv], tokA)
                        tiles = [(uacc_s[:, 1024 * a + 128 * dv:1024 * a + 128 * (dv + 1)],
                                  qfm_s[:, 512 * a:512 * (a + 1)]) for a in range(8)]
                        biB, tokB = mm_group(tiles, merge(u_ready[0], qfm_ready))
                        last_cross_pe = merge(last_cross_pe, tokB)
                        xb = dv % 2
                        s1 = sch.dve(lambda v, biB=biB, xb=xb: v.tensor_tensor(
                            out=xsc_s[xb][:], in0=pb[biB][:], in1=aQrep_s[:],
                            op=ALU.mult), waits=merge(tokB, xsc_cons[xb]))
                        bank_free[biB] = s1
                        s2 = sch.dve(lambda v, xq=xqr_s, dv=dv, xb=xb: v.tensor_tensor(
                            out=xsc_s[xb][:], in0=xsc_s[xb][:],
                            in1=xq[:, 512 * dv:512 * (dv + 1)], op=ALU.add),
                            waits=merge(s1, t_xq))
                        dst = yraw_s[:, 512 * dv:512 * (dv + 1)]
                        s3 = sch.dve(lambda v, biA=biA, dst=dst, xb=xb: v.tensor_tensor(
                            out=dst, in0=pb[biA][:], in1=xsc_s[xb][:], op=ALU.add),
                            waits=merge(tokA, s2))
                        yraw_tok[dv] = s3
                        xsc_cons[xb] = s3
                        bank_free[biA] = s3
                    yraw_ready = merge(*yraw_tok.values())

                    def s2_emit(prv=prv, lcp=None):
                        ktm_tok = {}
                        for tt in range(4):
                            for g in range(2):
                                pt = (2 * tt + g) % 2
                                tok = None
                                for j in range(4):
                                    dc = 4 * g + j
                                    tok = sch.pe(lambda t, pt=pt, dc=dc, tt=tt, j=j, prv=prv:
                                                 t.transpose(
                                                     ptr[pt][:, 128 * j:128 * (j + 1)],
                                                     kfm2_s[prv][:, 512 * dc + 128 * tt:512 * dc + 128 * (tt + 1)],
                                                     ident_s[:]),
                                                 waits=merge(kfm_tokens[prv],
                                                             ptr_free[pt] if j == 0 else None,
                                                             t_consts), inc=(j == 3))
                                kfm_cons[prv] = merge(kfm_cons[prv], tok)
                                dst = ktms_s[:, 1024 * tt + 512 * g:1024 * tt + 512 * (g + 1)]
                                if g == 0:
                                    ev = sch.act(lambda a, pt=pt, dst=dst, tt=tt:
                                                 a.activation(
                                                     out=dst, in_=ptr[pt][:, 0:512],
                                                     func=AF.Identity,
                                                     scale=acol_s[:, tt:tt + 1]),
                                                 waits=merge(tok, wb_tok.get("u2") if r == 1 else None))
                                else:
                                    ev = sch.dve(lambda v, pt=pt, dst=dst, tt=tt:
                                                 v.tensor_scalar(
                                                     out=dst, in0=ptr[pt][:, 0:512],
                                                     scalar1=acol_s[:, tt:tt + 1],
                                                     scalar2=None, op0=ALU.mult),
                                                 waits=merge(tok, wb_tok.get("u2") if r == 1 else None))
                                ptr_free[pt] = ev
                                ktm_tok[(tt, g)] = ev
                        ktm_ready = merge(*ktm_tok.values())
                        u2_tok = {}
                        for a in range(8):
                            for db in range(2):
                                tiles = [(ktms_s[:, 1024 * tt + 128 * a:1024 * tt + 128 * (a + 1)],
                                          vtm2_s[prv][:, 1024 * tt + 512 * db:1024 * tt + 512 * (db + 1)])
                                         for tt in range(4)]
                                bi, tok = mm_group(tiles, merge(ktm_ready, vtm_tokens[prv],
                                                                lcp))
                                vtm_cons[prv] = merge(vtm_cons[prv], tok)
                                dst = uacc_s[:, 1024 * a + 512 * db:1024 * a + 512 * (db + 1)]
                                ev = sch.dve(lambda v, bi=bi, dst=dst: v.tensor_tensor(
                                    out=dst, in0=pb[bi][:], in1=dst, op=ALU.add),
                                    waits=merge(tok, u_ready[0], lcp))
                                u2_tok[(a, db)] = ev
                                bank_free[bi] = ev
                        _chi = float(np.float32(AEFF ** W))
                        _clo = float(np.float32(AEFF ** W / np.float64(np.float32(AEFF ** W))))
                        t_us = sch.dve(lambda v: v.tensor_scalar(
                            out=uacc_s[:], in0=uacc_s[:], scalar1=_chi,
                            scalar2=_clo, op0=ALU.mult, op1=ALU.mult),
                            waits=merge(*u2_tok.values()))
                        u_ready[0] = t_us

                    pbis = {}

                    def emit_px():
                        for tb in range(2):
                            for rt in range(2):
                                bi = acquire_bank()
                                pbis[(tb, rt)] = bi
                                for i in range(8):
                                    w = merge(yraw_ready, t_consts,
                                              bank_free[bi]) if i == 0 else None
                                    sch.pe(lambda t, bi=bi, rt=rt, tb=tb, i=i, ys=yraw_s:
                                           t.matmul(
                                               pb[bi][:, 0:256],
                                               u1_s[:, 256 * i + 128 * rt:256 * i + 128 * (rt + 1)],
                                               ys[:, 512 * i + 256 * tb:512 * i + 256 * (tb + 1)],
                                               start=(i == 0), stop=False), waits=w)

                    def mid_all(prv=prv, lcp=last_cross_pe, do_s2=(r < NR - 1)):
                        if do_s2:
                            s2_emit(prv, lcp)
                        emit_px()

                    ln1 = ln_stats(yraw_s, yraw_ready, bw, reps=True, mid=mid_all,
                                   chunk_tok=yraw_tok)
                    reps_tok, sq_done = ln1["reps"], ln1["sq"]
                    w_s3done = {"pe": sch.cnt["pe"], "dve": sch.cnt["dve"]}
                    xq_cons[par] = w_s3done

              with nc.sbuf_tensor(f"v1s{r}", [128, 8192], BF16, side="right") as v1_s, \
                   nc.sbuf_tensor(f"u2s{r}", [128, 8192], BF16, side="right") as u2_s, \
                   nc.sbuf_tensor(f"h{r}", [128, 8192], BF16, side="right") as h_s, \
                   nc.sbuf_tensor(f"out2_{r}", [128, 4096], BF16, side="right") as out2_s:
                  ww2 = {"pe": sch.cnt["pe"], "dve": sch.cnt["dve"],
                         "act": sch.cnt["act"]}
                  if r == 0:
                      ld_sw(v1_s[:, 0:4096], v1_h[:, 0:4096],
                            waits=merge(t_s1_pe, wb_tok["wtq"]),
                            dsem="ffnw", gate=True)
                      ld_sw(v1_s[:, 4096:8192], v1_h[:, 4096:8192], dsem="ffnw")
                      t_v1 = ("ffnw", sch.cnt["ffnw"])
                      wb_tok["v1"] = ld_hw(v1b_d, v1_s[:], waits=t_v1,
                                           sem="wbv", gate=True)
                      ld_sw(u2_s[:, 0:4096], u2_h[:, 0:4096], waits=ww2,
                            dsem="ffnw", gate=True)
                      ld_sw(u2_s[:, 4096:8192], u2_h[:, 4096:8192], dsem="ffnw")
                      t_ffnw = ("ffnw", sch.cnt["ffnw"])
                      wb_tok["u2"] = ld_hw(u2b_d, u2_s[:], waits=t_ffnw,
                                           sem="wbu", gate=True)
                  else:
                      t_v1 = ld_hw_act(v1_s[:], v1b_d,
                                       waits=merge(t_s1_pe, wb_tok["v1"]),
                                       sem="wbv", gate=True)
                      t_ffnw = ld_hw_act(u2_s[:], u2b_d,
                                         waits=merge(ww2, wb_tok["u2"]),
                                         sem="wbu", gate=True)
                  out2_tok = {}
                  for tb in range(2):
                      p_tok = {}
                      for rt in range(2):
                          bi = pbis[(tb, rt)]
                          tok = sch.pe(lambda t, bi=bi, rt=rt, tb=tb: t.matmul(
                              pb[bi][:, 0:256],
                              nu1s_s[0:1, 128 * rt:128 * (rt + 1)],
                              rowA_s[0:1, 256 * tb:256 * (tb + 1)],
                              start=False, stop=True),
                              waits=merge(ln1["t1"], t_nu1) if (tb, rt) == (0, 0) else None,
                              inc=True)
                          bank_open[bi] = False
                          dst = p_s[:, 256 * rt:256 * (rt + 1)]
                          ev = sch.dve(lambda v, bi=bi, dst=dst, tb=tb: v.tensor_tensor(
                              out=dst, in0=pb[bi][:, 0:256],
                              in1=rrep_s[:, 256 * tb:256 * (tb + 1)], op=ALU.mult),
                              waits=merge(tok, p_cons[0], reps_tok))
                          p_tok[rt] = ev
                          bank_free[bi] = ev
                      p_ready = merge(*p_tok.values())
                      if tb == 0:
                          y1_tok = {}
                          for c in range(8):
                              dtmp = ysq_s[c % 3]
                              ta = sch.dve(lambda v, c=c, dtmp=dtmp, ys=yraw_s:
                                           v.tensor_tensor(
                                               out=dtmp[:],
                                               in0=ys[:, 512 * c:512 * (c + 1)],
                                               in1=mrep_s[:], op=ALU.subtract),
                                           waits=merge(reps_tok, sq_done,
                                                       y1_tok.get(c - 4)))
                              y1_tok[c] = sch.dve(lambda v, c=c, dtmp=dtmp, y1s=y1_s:
                                                  v.tensor_tensor(
                                                      out=y1s[:, 512 * c:512 * (c + 1)],
                                                      in0=dtmp[:], in1=rrep_s[:],
                                                      op=ALU.mult), waits=ta)
                          y1_ready = merge(*y1_tok.values())
                          ln_cons[0] = y1_ready
                      h_ev = None
                      for ft in range(32):
                          tiles = [(v1_s[:, 4096 * rc + 128 * ft:4096 * rc + 128 * (ft + 1)],
                                    p_s[:, 256 * rc:256 * (rc + 1)]) for rc in range(2)]
                          bi, tok = mm_group(tiles, merge(p_ready, h_cons[0], t_v1), nf=256)
                          p_cons[0] = merge(p_cons[0], tok)
                          dst = h_s[:, 256 * ft:256 * (ft + 1)]
                          h_ev = sch.act(lambda a, bi=bi, dst=dst, ft=ft: a.activation(
                              out=dst, in_=pb[bi][:, 0:256], func=AF.Gelu_apprx_tanh,
                              bias=b1c_s[:, ft:ft + 1]), waits=tok)
                          bank_free[bi] = h_ev
                      q2_tok = {}
                      for rt in range(2):
                          tiles = [(u2_s[:, 256 * fc + 128 * rt:256 * fc + 128 * (rt + 1)],
                                    h_s[:, 256 * fc:256 * (fc + 1)]) for fc in range(32)]
                          bi, tok = mm_group(tiles, merge(h_ev, q2_cons[0], t_ffnw), nf=256)
                          h_cons[0] = merge(h_cons[0], tok)
                          dst = q2_s[:, 256 * rt:256 * (rt + 1)]
                          ev = sch.act(lambda a, bi=bi, dst=dst: a.activation(
                              out=dst, in_=pb[bi][:, 0:256], func=AF.Identity),
                              waits=tok)
                          q2_tok[rt] = ev
                          bank_free[bi] = ev
                      q2_ready = merge(*q2_tok.values())
                      for dt in range(8):
                          tiles = [(v2_s[:, 1024 * rc + 128 * dt:1024 * rc + 128 * (dt + 1)],
                                    q2_s[:, 256 * rc:256 * (rc + 1)]) for rc in range(2)]
                          bi, tok = mm_group(tiles, q2_ready, nf=256)
                          q2_cons[0] = merge(q2_cons[0], tok)
                          dst = out2_s[:, 512 * dt + 256 * tb:512 * dt + 256 * (tb + 1)]
                          yin = y1_s[:, 512 * dt + 256 * tb:512 * dt + 256 * (tb + 1)]
                          ev = sch.dve(lambda v, bi=bi, dst=dst, yin=yin, dt=dt:
                                       v.scalar_tensor_tensor(
                                           out=dst, in0=pb[bi][:, 0:256],
                                           scalar=b2col_s[:, dt:dt + 1],
                                           in1=yin, op0=ALU.add, op1=ALU.add),
                                       waits=tok)
                          out2_tok[(tb, dt)] = ev
                          bank_free[bi] = ev
                  out2_ready = merge(*out2_tok.values())

                  s5_ev = {}
                  s5_sq_all = []
                  s5_nrm_all = []

                  def s5_tile(tt):
                      sg = tt
                      o = 4 * (tt % 2)
                      ev_all = []
                      sq_all = []
                      for g in range(2):
                          pt = g % 2
                          tok = None
                          for j in range(4):
                              dc = 4 * g + j
                              tok = sch.pe(lambda t, pt=pt, dc=dc, tt=tt, j=j, o2=out2_s:
                                           t.transpose(
                                               ptr[pt][:, 128 * j:128 * (j + 1)],
                                               o2[:, 512 * dc + 128 * tt:512 * dc + 128 * (tt + 1)],
                                               ident_s[:]),
                                           waits=merge(out2_tok[(tt // 2, dc)],
                                                       ptr_free[pt] if j == 0 else None,
                                                       t_consts),
                                           inc=(j == 3))
                          ev = sch.act(lambda a, pt=pt, g=g, sg=sg, tt=tt: a.activation(
                              out=stage_s[sg][:, 512 * g:512 * (g + 1)],
                              in_=ptr[pt][:, 0:512], func=AF.Identity,
                              accum_out=colacc_s[:, 4 * tt + g:4 * tt + g + 1]),
                              waits=merge(tok, stage_cons[sg]))
                          sq = sch.act(lambda a, pt=pt, g=g, tt=tt: a.activation(
                              out=ysq_s[g][:], in_=ptr[pt][:, 0:512], func=AF.Square,
                              accum_out=colacc_s[:, 4 * tt + 2 + g:4 * tt + 3 + g]),
                              waits=merge(tok, ln1["t1"]))
                          ptr_free[pt] = merge(ev, sq)
                          ev_all.append(ev)
                          sq_all.append(sq)
                      s5_ev[tt] = merge(*ev_all)
                      s5_sq_all.extend(sq_all)
                      ta = sch.dve(lambda v, tt=tt, o=o: v.tensor_tensor(
                          out=cw_s[:, o:o + 1], in0=colacc_s[:, 4 * tt:4 * tt + 1],
                          in1=colacc_s[:, 4 * tt + 1:4 * tt + 2], op=ALU.add),
                          waits=s5_ev[tt])
                      tb2 = sch.dve(lambda v, tt=tt, o=o: v.tensor_scalar(
                          out=mcol_s[:, tt:tt + 1], in0=cw_s[:, o:o + 1],
                          scalar1=float(-1.0 / 1024), scalar2=None, op0=ALU.mult),
                          waits=ta)
                      tc = sch.dve(lambda v, tt=tt, o=o: v.tensor_tensor(
                          out=cw_s[:, o + 1:o + 2], in0=colacc_s[:, 4 * tt + 2:4 * tt + 3],
                          in1=colacc_s[:, 4 * tt + 3:4 * tt + 4], op=ALU.add),
                          waits=merge(*sq_all))
                      td = sch.dve(lambda v, tt=tt, o=o: v.tensor_tensor(
                          out=cw_s[:, o + 2:o + 3], in0=mcol_s[:, tt:tt + 1],
                          in1=mcol_s[:, tt:tt + 1], op=ALU.mult), waits=tb2)
                      te = sch.dve(lambda v, tt=tt, o=o: v.scalar_tensor_tensor(
                          out=cw_s[:, o + 3:o + 4], in0=cw_s[:, o + 1:o + 2],
                          scalar=float(1.0 / 1024), in1=cw_s[:, o + 2:o + 3],
                          op0=ALU.mult, op1=ALU.subtract), waits=merge(tc, td))
                      tf = sch.act(lambda a, tt=tt, o=o: a.activation(
                          out=cs_s[:, tt % 2:tt % 2 + 1], in_=cw_s[:, o + 3:o + 4],
                          func=AF.Sqrt, bias=f32(1e-5)), waits=te)
                      tg = sch.dve(lambda v, tt=tt: v.reciprocal(
                          out=rcol_s[:, tt:tt + 1], in_=cs_s[:, tt % 2:tt % 2 + 1]),
                          waits=tf)
                      return merge(tb2, tg)

                  for tt in range(4):
                      sg = tt
                      cols_tok = s5_tile(tt)
                      nrm = sch.dve(lambda v, sg=sg, tt=tt: v.tensor_scalar(
                          out=stage_s[sg][:], in0=stage_s[sg][:],
                          scalar1=mcol_s[:, tt:tt + 1], scalar2=rcol_s[:, tt:tt + 1],
                          op0=ALU.add, op1=ALU.mult),
                          waits=merge(s5_ev[tt], cols_tok))
                      s5_nrm_all.append(nrm)
                      st_half = []
                      for hh in range(2):
                          st_half.append(sch.sync_dma(
                              lambda s_, sg=sg, tt=tt, r=r, hh=hh: s_.dma_start(
                                  out=y_h[512 * r + 128 * tt + 64 * hh:
                                          512 * r + 128 * tt + 64 * (hh + 1), :],
                                  in_=stage_s[sg][64 * hh:64 * (hh + 1), :]),
                              sem=f"st{sg + 4 * hh}", waits=nrm, gate=True))
                      stage_cons[sg] = merge(*st_half)
                  ln_cons[0] = merge(y1_ready, p_ready, *s5_sq_all, *s5_nrm_all)

    sch.wait_only("sync", {k: sch.cnt[k] for k in Sched.DMA_SEMS if sch.cnt[k]})

    with nc.Block() as block:
        def replay(eng_name):
            def go(engine):
                waited = {}
                for waits, fn, inc, amt in sch.ops[eng_name]:
                    for s, v in waits.items():
                        if waited.get(s, 0) < v:
                            engine.wait_ge(sem[s], v)
                            waited[s] = v
                    if fn is None:
                        continue
                    inst = fn(engine)
                    if inc is not None:
                        inst.then_inc(sem[inc], amt)
            return go
        block.tensor(replay("pe"))
        block.vector(replay("dve"))
        block.scalar(replay("act"))
        block.gpsimd(replay("gp"))
        block.sync(replay("sync"))

    ctx.close()
    return nc


_M32_CACHE = [None]


def _m32_table():
    if _M32_CACHE[0] is not None:
        return _M32_CACHE[0]
    try:
        import jax
        import jax.numpy as jnp
        with jax.default_device(jax.devices("cpu")[0]):
            m = np.asarray(0.99 ** jnp.arange(S).astype(jnp.float32),
                           dtype=np.float64)
    except Exception:
        m = (AEFF ** np.arange(S, dtype=np.float64)).astype(np.float32).astype(np.float64)
    _M32_CACHE[0] = m
    return m


def _hilo(t64):
    hi = t64.astype(np.float32)
    with np.errstate(divide="ignore", invalid="ignore"):
        lo = np.where(hi != 0, t64 / hi.astype(np.float64), 0.0).astype(np.float32)
    return np.ascontiguousarray(hi), np.ascontiguousarray(lo)


def _fm(a):
    T = a.shape[0]
    return np.ascontiguousarray(
        a.T.reshape(D // 128, 128, T).transpose(1, 0, 2).reshape(128, -1))


def _ck(w, rows):
    C = w.shape[1]
    return np.ascontiguousarray(
        w.reshape(rows, 128, C).transpose(1, 0, 2).reshape(128, rows * C))


def make_in_maps(L0_attn_x, L0_attn_W_qkv, L0_attn_b_qkv, L0_ffn_U1, L0_ffn_V1,
                 L0_ffn_b1, L0_ffn_U2, L0_ffn_V2, L0_ffn_b2):
    x = np.asarray(L0_attn_x, dtype=np.float32).reshape(S, D)
    Wq = np.asarray(L0_attn_W_qkv, dtype=np.float32)
    bq = np.asarray(L0_attn_b_qkv, dtype=np.float32)
    U1 = np.asarray(L0_ffn_U1, dtype=np.float32)
    V1 = np.asarray(L0_ffn_V1, dtype=np.float32)
    b1 = np.asarray(L0_ffn_b1, dtype=np.float32)
    U2 = np.asarray(L0_ffn_U2, dtype=np.float32)
    V2 = np.asarray(L0_ffn_V2, dtype=np.float32)
    b2 = np.asarray(L0_ffn_b2, dtype=np.float32)

    WT = np.ascontiguousarray(Wq.T)
    wtq = _ck(WT[:, 0:1024], 8)
    wtkv = np.concatenate([_ck(WT[:, 1024:2048], 8),
                           _ck(WT[:, 2048:3072], 8)], axis=1)
    brow = np.ascontiguousarray(bq[None, 1024:])
    bqcol = np.stack([bq[128 * o:128 * (o + 1)] for o in range(8)]
                     + [bq[1024 + 128 * o:1024 + 128 * (o + 1)] for o in range(8)],
                     axis=1).astype(np.float32)
    u1 = _ck(U1, 8)
    v1 = _ck(V1, 2)
    u2 = _ck(U2, 32)
    v2 = _ck(V2, 2)
    b1c = np.ascontiguousarray(b1.reshape(32, 128).T)
    b2col = np.ascontiguousarray(b2.reshape(8, 128).T)

    m32 = _m32_table()
    p = np.arange(128)[:, None]
    cc = np.arange(1408)[None, :]
    Dm = cc - 384 - p
    master = np.where(Dm >= 0, m32[np.clip(Dm, 0, None)] / 32.0, 0.0).astype(np.float32)

    t = np.arange(T_R, dtype=np.float64)
    aQ64 = np.broadcast_to(AEFF ** (W + t) / 32.0, (128, T_R))
    aQhi, aQlo = _hilo(aQ64)
    c4 = np.arange(4, dtype=np.float64)[None, :]
    acol4_64 = AEFF ** (-(p + 128 * c4))
    cH = np.arange(4 * NSMAC, dtype=np.float64)[None, :]
    acolH_base64 = AEFF ** (NSMAC * 512 - (128 * cH + p))
    ident = np.eye(128, dtype=np.float32)

    shared = dict(wtkv=wtkv, wtq=wtq, brow=brow, bqcol=bqcol, u1=u1, v1=v1,
                  u2=u2, v2=v2, b1c=b1c, b2col=b2col, master=master,
                  aQhi=aQhi, aQlo=aQlo, ident=ident)

    in_maps = []
    for core in range(NCORES):
        own = x[T_CORE * core:T_CORE * (core + 1)]
        xq = np.concatenate([_fm(own[T_R * rr:T_R * (rr + 1)]) for rr in range(NR)],
                            axis=1)
        lo = T_CORE * core - HALO
        halo = np.zeros((HALO, D), dtype=np.float32)
        src_lo = max(lo, 0)
        if src_lo < T_CORE * core:
            halo[src_lo - lo:] = x[src_lo:T_CORE * core]
        xh_tm = halo[0:2560].reshape(20, 128, D).transpose(1, 0, 2).reshape(128, 20480)
        xh = np.concatenate([xh_tm, _fm(halo[2560:3072])], axis=1)
        hidx = 128 * cH + p
        valid = (lo + hidx) >= 0
        acolH64 = np.where(valid, np.sqrt(acolH_base64), 0.0)
        ac64 = np.concatenate([acol4_64, acolH64], axis=1)
        achi, aclo = _hilo(ac64)
        vmask = np.full((128, 1), 1.0 if core > 0 else 0.0, dtype=np.float32)
        m = dict(shared)
        m.update(xq=np.ascontiguousarray(xq), xh=np.ascontiguousarray(xh),
                 achi=achi, aclo=aclo, vmask=vmask)
        in_maps.append(m)
    return in_maps


def kernel(L0_attn_x, L0_attn_W_qkv, L0_attn_b_qkv, L0_ffn_U1, L0_ffn_V1,
           L0_ffn_b1, L0_ffn_U2, L0_ffn_V2, L0_ffn_b2):
    in_maps = make_in_maps(L0_attn_x, L0_attn_W_qkv, L0_attn_b_qkv, L0_ffn_U1,
                           L0_ffn_V1, L0_ffn_b1, L0_ffn_U2, L0_ffn_V2, L0_ffn_b2)
    nc = build_nc()
    res = run_bass_kernel_spmd(nc, in_maps, core_ids=list(range(NCORES)))
    out = np.concatenate([r["y"] for r in res.results], axis=0)
    return out.reshape(1, S, D).astype(np.float32)
